# version 12
# baseline (speedup 1.0000x reference)
import contextlib
import numpy as np
import ml_dtypes

import concourse.bass as bass
import concourse.mybir as mybir
from concourse.bass_utils import run_bass_kernel_spmd

BFNP = ml_dtypes.bfloat16
F32 = mybir.dt.float32
BF16 = mybir.dt.bfloat16
AF = mybir.ActivationFunctionType
ALU = mybir.AluOpType
AX = mybir.AxisListType

D = 1024
NB = 8192
NQ = 2048
DFF = 4096

PERS_BYTES = 22 * 1024
SBUF_REGION = 185 * 1024


class Ctr:
    def __init__(self, handle):
        self.h = handle
        self.n = 0

    def inc(self, amt):
        self.n += amt
        return self.n


class Prog:
    ENGS = ("tensor", "vector", "scalar", "sync", "gpsimd")

    def __init__(self):
        self.stream = {e: [] for e in self.ENGS}

    def emit(self, eng, fn, waits=(), inc=None, amt=1):
        if inc is not None:
            inc.inc(amt)
        self.stream[eng].append(
            (fn, [w for w in waits if w is not None and w[1]], inc, amt))

    def realize(self, block):
        streams = self.stream

        def run(eng_name, eng):
            waited = {}
            for fn, waits, inc, amt in streams[eng_name]:
                for (ctr, val) in waits:
                    if waited.get(id(ctr), -1) >= val:
                        continue
                    eng.wait_ge(ctr.h, val)
                    waited[id(ctr)] = val
                inst = fn(eng)
                if inc is not None:
                    assert inst is not None, f"{eng_name} op with inc returned None"
                    inst.then_inc(inc.h, amt)

        @block.tensor
        def _(t):
            run("tensor", t)

        @block.vector
        def _(v):
            run("vector", v)

        @block.scalar
        def _(s):
            run("scalar", s)

        @block.sync
        def _(s):
            run("sync", s)

        @block.gpsimd
        def _(g):
            run("gpsimd", g)


class Arena:
    def __init__(self, nc, backing, base_off, limit):
        self.nc = nc
        self.backing = backing
        self.base = nc.lookup_mloc(backing).addr + base_off
        self.off = 0
        self.limit = limit
        self.uid = 0

    def alloc(self, name, shape, dtype):
        nbytes = int(np.prod(shape[1:])) * mybir.dt.size(dtype)
        nbytes = (nbytes + 63) // 64 * 64
        assert self.off + nbytes <= self.limit, \
            f"SBUF region overflow at {name}: {self.off + nbytes} > {self.limit}"
        h = self.nc.alloc_sbuf_tensor_at(
            f"{name}_u{self.uid}", shape, dtype, offset=self.base + self.off)
        self.uid += 1
        self.off += nbytes
        return h

    def reset(self):
        self.off = 0


def build_nc(dbg=False, stages='ABCD', dparts=None):
    nc = bass.Bass("TRN2", target_bir_lowering=False)

    def din(name, shape):
        return nc.dram_tensor(name, shape, F32, kind="ExternalInput").ap()

    x_b = din("x_b", [NB, D])
    xq = din("xq", [NQ, D])
    wqkv_h = din("wqkv_h", [128, 8, 3072])
    beta_c = din("beta_c", [128, 8])
    gamma_c = din("gamma_c", [128, 8])
    wproj_h = din("wproj_h", [128, 8, 1024])
    bproj_c = din("bproj_c", [128, 8])
    wiT_h = din("wiT_h", [32, 128, 8, 128])
    bi_c = din("bi_c", [128, 32])
    woT_h = din("woT_h", [8, 128, 32, 128])
    laT_h = din("laT_h", [128, 8, 8, 16])
    lbT_h = din("lbT_h", [8, 16, DFF])
    gwT_h = din("gwT_h", [128, 8, 8])
    gb_row = din("gb_row", [1, 8])
    bo_row = din("bo_row", [1, D])

    out = nc.dram_tensor("out", [NQ, D], F32, kind="ExternalOutput").ap()

    def scr(name, shape, dt):
        return nc.dram_tensor(name, shape, dt, kind="Internal").ap()

    sc_k = scr("sc_k", [D, NB], BF16)
    sc_v = scr("sc_v", [NB, 16 * 65], BF16)
    sc_q = scr("sc_q", [D, NQ], BF16)
    sc_at = scr("sc_at", [D, NQ], BF16)
    sc_x2 = scr("sc_x2", [NQ, D], F32)
    sc_c = scr("sc_c", [1, NQ], BF16)
    sc_wo = scr("sc_wo", [8, 128, DFF], BF16)

    dbg_outs = []
    if dbg:
        for nm, t in [("dk", sc_k), ("dv", sc_v), ("dq", sc_q), ("dat", sc_at),
                      ("dx2", sc_x2)]:
            o = nc.dram_tensor("dbg_" + nm, [int(s) for s in t.tensor.shape],
                               t.tensor.dtype, kind="ExternalOutput").ap()
            dbg_outs.append((o, t))

    ident_f32 = nc.inline_tensor(np.eye(128, dtype=np.float32), "c_idf")
    ident_bf = nc.inline_tensor(np.eye(128, dtype=BFNP), "c_idb")
    ones_c = nc.inline_tensor(np.ones((1, 128), dtype=np.float32), "c_ones")
    eps_c = nc.inline_tensor(np.full((128, 1), 1e-5, dtype=np.float32), "c_eps")
    selm = np.zeros((8, 8 * 128), dtype=np.float32)
    for e in range(8):
        selm[e, e * 128:(e + 1) * 128] = 1.0
    sel_c = nc.inline_tensor(selm, "c_sel")

    P = Prog()
    E = P.emit

    ctx = contextlib.ExitStack()
    with ctx:
        En = ctx.enter_context
        backing = En(nc.sbuf_tensor("arena", [128, PERS_BYTES + SBUF_REGION],
                                    mybir.dt.uint8))
        pers = Arena(nc, backing, 0, PERS_BYTES)
        reg = Arena(nc, backing, PERS_BYTES, SBUF_REGION)


        def full_barrier(waitlist):
            for eng in ("tensor", "vector", "scalar", "sync", "gpsimd"):
                E(eng, lambda e: e.nop(), waits=list(waitlist))
        _puid = [0]

        def psum_at(name, shape, dtype, bank):
            h = nc.virtual_psum_tensor(f"{name}_u{_puid[0]}", shape, dtype)
            _puid[0] += 1
            nc._tensor(h.name, h.shape, h.dtype, type="PSUM")
            mloc = nc.lookup_mloc(h)
            mloc.allocated = True
            mloc.addr = bank * 2048
            return h

        LD = Ctr(En(nc.semaphore("LD")))
        GD = Ctr(En(nc.semaphore("GD")))
        T = Ctr(En(nc.semaphore("T")))
        V = Ctr(En(nc.semaphore("V")))
        S = Ctr(En(nc.semaphore("S")))
        KV = [Ctr(En(nc.semaphore("KV0"))), Ctr(En(nc.semaphore("KV1")))]
        XD3 = [Ctr(En(nc.semaphore(f"XD{i}"))) for i in range(3)]
        XQ = Ctr(En(nc.semaphore("XQ")))
        OS = Ctr(En(nc.semaphore("OS")))
        WO2 = [Ctr(En(nc.semaphore(f"WO{i}"))) for i in range(2)]
        XT2 = [Ctr(En(nc.semaphore(f"XT{i}"))) for i in range(2)]
        WI2 = [Ctr(En(nc.semaphore(f"WI{i}"))) for i in range(2)]
        XE2 = [Ctr(En(nc.semaphore(f"XE{i}"))) for i in range(2)]
        LB2 = [Ctr(En(nc.semaphore(f"LB{i}"))) for i in range(2)]
        WS2 = [Ctr(En(nc.semaphore(f"WS{i}"))) for i in range(2)]

        id32 = pers.alloc("id32", [128, 128], F32)
        idbf = pers.alloc("idbf", [128, 128], BF16)
        ones128 = pers.alloc("ones128", [2, 128], F32)
        ones65 = pers.alloc("ones65", [65, 64], F32)
        eps_sb = pers.alloc("eps_sb", [128, 1], F32)
        sel_sb = pers.alloc("sel_sb", [8, 8 * 128], F32)
        gam_sb = pers.alloc("gam_sb", [128, 8], F32)
        bet_sb = pers.alloc("bet_sb", [128, 64], F32)
        bpr_sb = pers.alloc("bpr_sb", [128, 8], F32)
        bi_sb = pers.alloc("bi_sb", [128, 32], F32)
        gw_sb = pers.alloc("gw_sb", [128, 64], F32)
        gbr_sb = pers.alloc("gbr_sb", [128, 8], F32)
        bo_rep = pers.alloc("bo_rep", [128, D], F32)
        qb_sb = pers.alloc("qb_sb", [128, 64], F32)
        qbq8 = pers.alloc("qbq8", [128, 8], F32)
        laT = pers.alloc("laT", [128, 1024], BF16)
        mu1q = pers.alloc("mu1q", [128, 16], F32)
        rs1q = pers.alloc("rs1q", [128, 16], F32)
        dmy = pers.alloc("dmy", [128, 1024], BF16)
        dmy2 = pers.alloc("dmy2", [128, 1024], BF16)
        st_sb = pers.alloc("st_sb", [128, 32], F32)
        gst = pers.alloc("gst", [128, 16], F32)
        gbrow = pers.alloc("gbrow", [2, 128], F32)
        borow = pers.alloc("borow", [2, D], F32)

        def ld(dst, src):
            E("sync", lambda s, d=dst, so=src: s.dma_start(out=d, in_=so),
              inc=LD, amt=16)

        ld(id32[:], ident_f32.ap())
        ld(idbf[:], ident_bf.ap())
        ld(ones128[0:1, :], ones_c.ap())
        ld(eps_sb[:], eps_c.ap())
        ld(sel_sb[:], sel_c.ap())
        ld(gam_sb[:], gamma_c)
        ld(bet_sb[:, 0:8], beta_c)
        ld(bpr_sb[:], bproj_c)
        ld(bi_sb[:], bi_c)
        ld(gw_sb[:], gwT_h.rearrange("p a b -> p (a b)"))
        ld(gbrow[0:1, 0:8], gb_row)
        ld(borow[0:1, :], bo_row)
        ld_pers = LD.n

        E("gpsimd", lambda g: g.dma_start(
            out=laT[:], in_=laT_h.rearrange("p a e r -> p (a e r)")),
            inc=GD, amt=16)
        laT_gd = GD.n

        E("vector", lambda v: v.memset(ones65[64:65, :], 1.0),
          waits=[(LD, ld_pers)], inc=V)
        ones65_v = V.n

        p_i0 = psum_at("p_i0", [128, 512], F32, 7)
        E("tensor", lambda t: t.matmul(p_i0[:, 0:8], ones128[0:1, :], gbrow[0:1, 0:8],
                                       start=True, stop=True),
          waits=[(LD, ld_pers)], inc=T)
        tv = T.n
        E("vector", lambda v: v.tensor_copy(gbr_sb[:], p_i0[:, 0:8]),
          waits=[(T, tv)], inc=V)
        vv = V.n
        for i in range(2):
            E("tensor", lambda t, i=i: t.matmul(
                p_i0[:], ones128[0:1, :], borow[0:1, i * 512:(i + 1) * 512],
                start=True, stop=True),
                waits=[(V, vv)], inc=T)
            tv = T.n
            E("vector", lambda v, i=i: v.tensor_copy(
                bo_rep[:, i * 512:(i + 1) * 512], p_i0[:]),
                waits=[(T, tv)], inc=V)
            vv = V.n

        reg.reset()
        wqkv_bf = reg.alloc("wqkv_bf", [128, 8 * 3072], BF16)
        stg_w = [reg.alloc(f"stg_w{i}", [128, 3072], F32) for i in range(2)]
        wostg = [reg.alloc(f"wostg{i}", [128, 4096], BF16) for i in range(2)]
        x_t = [reg.alloc(f"x_t{i}", [128, D], F32) for i in range(3)]
        zn_t = [reg.alloc(f"zn_t{i}", [128, D], BF16) for i in range(2)]
        znT = [reg.alloc(f"znT{i}", [128, 8 * 512], BF16) for i in range(2)]
        kstg = [reg.alloc(f"kstg{i}", [128, 512], BF16) for i in range(2)]
        vstg = [reg.alloc(f"vstg{i}", [128, 520], BF16) for i in range(2)]
        qstg = [reg.alloc(f"qstg{i}", [128, 512], BF16) for i in range(2)]
        vb8 = reg.alloc("vb8", [8, 128], F32)
        vbias_rep = reg.alloc("vbias_rep", [128, D], F32)

        p_qb = psum_at("p_qb", [128, 512], F32, 6)
        p_tr = [psum_at(f"p_tr{i}", [128, 128], F32, i) for i in range(2)]
        p_trb = [psum_at(f"p_trb{i}", [128, 128], BF16, i) for i in range(2)]
        p_k = [psum_at(f"p_k{i}", [128, 512], F32, 2 + i) for i in range(2)]
        p_v = [psum_at(f"p_v{i}", [128, 512], F32, 4 + i) for i in range(2)]

        for dc in range(8):
            wb = dc % 2
            W = WS2[wb]
            E("gpsimd", lambda g, dc=dc, wb=wb: g.dma_start(
                out=wostg[wb][:].rearrange("p (a c) -> p a c", a=32, c=128),
                in_=woT_h[dc]),
                waits=[(W, W.n)], inc=W, amt=16)
            E("gpsimd", lambda g, dc=dc, wb=wb: g.dma_start(
                out=sc_wo[dc],
                in_=wostg[wb][:].rearrange("p (a c) -> p a c", a=32, c=128)),
                waits=[(W, W.n)], inc=W, amt=16)

        stg_free_v = [0, 0]
        qb_t = 0
        qb_acc_v = 0
        for dc in range(8):
            b = dc % 2
            E("gpsimd", lambda g, b=b, dc=dc: g.dma_start(
                out=stg_w[b][:], in_=wqkv_h[:, dc, :]),
                waits=[(T, qb_t), (V, stg_free_v[b])],
                inc=GD, amt=16)
            wld = GD.n
            for cc in range(24):
                E("tensor", lambda t, b=b, cc=cc, dc=dc: t.matmul(
                    p_qb[:, cc:cc + 1], stg_w[b][:, cc * 128:(cc + 1) * 128],
                    bet_sb[:, dc:dc + 1], start=True, stop=True),
                    waits=([(GD, wld), (V, qb_acc_v)] if cc == 0 else []),
                    inc=T if cc == 23 else None)
            qb_t = T.n
            if dc == 0:
                E("vector", lambda v: v.tensor_copy(qb_sb[:, 0:24], p_qb[:, 0:24]),
                  waits=[(T, qb_t)], inc=V)
            else:
                E("vector", lambda v: v.tensor_tensor(
                    qb_sb[:, 0:24], qb_sb[:, 0:24], p_qb[:, 0:24], ALU.add),
                    waits=[(T, qb_t), (V, V.n)], inc=V)
            qb_acc_v = V.n
            E("vector", lambda v, b=b, dc=dc: v.tensor_scalar(
                wqkv_bf[:, dc * 3072:(dc + 1) * 3072], stg_w[b][:],
                gam_sb[:, dc:dc + 1], None, ALU.mult),
                waits=[(GD, wld), (T, qb_t)], inc=V)
            stg_free_v[b] = V.n

        qb_v = qb_acc_v
        E("vector", lambda v: v.tensor_scalar(
            qbq8[:], qb_sb[:, 0:8], 0.125, None, ALU.mult),
            waits=[(V, qb_v), (V, V.n)], inc=V)
        qbq8_v = V.n
        E("tensor", lambda t: t.transpose(p_tr[0][0:8, 0:128], qb_sb[:, 16:24], id32[:]),
          waits=[(V, qb_v)], inc=T)
        vb_t = T.n
        E("vector", lambda v: v.tensor_copy(vb8[:], p_tr[0][0:8, 0:128]),
          waits=[(T, vb_t)], inc=V)
        vb_v = V.n
        for i in range(8):
            E("tensor", lambda t, i=i: t.matmul(
                p_k[0][:, 0:128], sel_sb[:, i * 128:(i + 1) * 128], vb8[:],
                start=True, stop=True),
                waits=[(V, vb_v), (V, V.n)], inc=T)
            t_r = T.n
            E("vector", lambda v, i=i: v.tensor_copy(
                vbias_rep[:, i * 128:(i + 1) * 128], p_k[0][:, 0:128]),
                waits=[(T, t_r)], inc=V)
        vbias_v = V.n

        for i in range(2):
            E("vector", lambda v, i=i: v.memset(
                vstg[i][:].rearrange("p (c w) -> p c w", c=8, w=65)[:, :, 64:65],
                1.0), inc=V)
        vstg_ones_v = V.n

        def st(par, col):
            return st_sb[:, par * 16 + col: par * 16 + col + 1]

        x_free_S = [0, 0, 0]
        x_free_V = [0, 0, 0]
        zn_free_T = [0, 0]
        znT_mm_t = [0, 0]
        st_free_S = [0, 0]
        kst_ld = [0, 0]
        vst_ld = [0, 0]
        qst_ld = [0, 0]
        pk_free_S = [0, 0]
        pv_free_V = [vbias_v, 0]
        ptr_free_V = [0, 0]

        for tc in range(20):
            is_q = tc >= 16
            par_c = tc % 2
            znb = znT[par_c]
            for sub in range(4):
                g = tc * 4 + sub
                xb = x_t[g % 3]
                par = g % 2
                if is_q:
                    src = xq[(tc - 16) * 512 + sub * 128:
                             (tc - 16) * 512 + sub * 128 + 128, :]
                else:
                    src = x_b[tc * 512 + sub * 128: tc * 512 + sub * 128 + 128, :]
                E("sync", lambda s, xb=xb, src=src: s.dma_start(out=xb[:], in_=src),
                  waits=[(S, x_free_S[g % 3]), (V, x_free_V[g % 3])], inc=XD3[g % 3], amt=16)
                x_ld = XD3[g % 3].n
                E("scalar", lambda sc, xb=xb, par=par: sc.activation(
                    dmy[:], xb[:], AF.Copy, accum_out=st(par, 0)),
                    waits=[(XD3[g % 3], x_ld), (V, st_free_S[par])], inc=S)
                s_sum = S.n
                E("scalar", lambda sc, xb=xb, par=par: sc.activation(
                    dmy[:], xb[:], AF.Square, accum_out=st(par, 4)),
                    waits=[(S, S.n)], inc=S)
                s_sq = S.n
                x_free_S[g % 3] = s_sq
                E("vector", lambda v, par=par: v.tensor_scalar(
                    st(par, 1), st(par, 0), 1.0 / 1024.0, None, ALU.mult),
                    waits=[(S, s_sum), (V, V.n)], inc=V)
                E("vector", lambda v, par=par: v.tensor_tensor(
                    st(par, 6), st(par, 1), st(par, 1), ALU.mult),
                    waits=[(V, V.n)], inc=V)
                E("vector", lambda v, par=par: v.tensor_scalar(
                    st(par, 5), st(par, 4), 1.0 / 1024.0, st(par, 6),
                    ALU.mult, ALU.subtract),
                    waits=[(V, V.n), (S, s_sq)], inc=V)
                v_v1 = V.n
                E("scalar", lambda sc, par=par: sc.activation(
                    st(par, 9), st(par, 5), AF.Ln, bias=eps_sb[:]),
                    waits=[(V, v_v1)], inc=S)
                E("scalar", lambda sc, par=par: sc.activation(
                    st(par, 9), st(par, 9), AF.Exp, scale=-0.5),
                    waits=[(S, S.n)], inc=S)
                s_r1 = S.n
                E("vector", lambda v, par=par: v.tensor_tensor(
                    st(par, 8), st(par, 9), st(par, 9), ALU.mult),
                    waits=[(S, s_r1), (V, V.n)], inc=V)
                E("vector", lambda v, par=par: v.tensor_tensor(
                    st(par, 8), st(par, 8), st(par, 5), ALU.mult),
                    waits=[(V, V.n)], inc=V)
                v_var2 = V.n
                E("scalar", lambda sc, par=par: sc.activation(
                    st(par, 10), st(par, 8), AF.Ln, bias=eps_sb[:]),
                    waits=[(V, v_var2)], inc=S)
                E("scalar", lambda sc, par=par: sc.activation(
                    st(par, 10), st(par, 10), AF.Exp, scale=-0.5),
                    waits=[(S, S.n)], inc=S)
                s_r2 = S.n
                E("vector", lambda v, par=par: v.tensor_tensor(
                    st(par, 11), st(par, 9), st(par, 10), ALU.mult),
                    waits=[(S, s_r2), (V, V.n)], inc=V)
                v_a = V.n
                if is_q:
                    qi = (tc - 16) * 4 + sub
                    E("vector", lambda v, par=par, qi=qi: v.tensor_copy(
                        mu1q[:, qi:qi + 1], st(par, 1)), waits=[(V, V.n)], inc=V)
                    E("vector", lambda v, par=par, qi=qi: v.tensor_copy(
                        rs1q[:, qi:qi + 1], st(par, 9)), inc=V)
                E("vector", lambda v, xb=xb, par=par: v.tensor_scalar(
                    zn_t[par][:], xb[:], st(par, 1), st(par, 11),
                    ALU.subtract, ALU.mult),
                    waits=[(V, v_a), (T, zn_free_T[par]), (V, V.n)], inc=V)
                v_zn = V.n
                x_free_V[g % 3] = v_zn
                st_free_S[par] = s_r2
                for dc in range(8):
                    pt = dc % 2
                    E("tensor", lambda t, par=par, dc=dc, pt=pt: t.transpose(
                        p_trb[pt][:], zn_t[par][:, dc * 128:(dc + 1) * 128], idbf[:]),
                        waits=[(V, v_zn), (T, znT_mm_t[par_c]), (V, ptr_free_V[pt])],
                        inc=T)
                    t_tr = T.n
                    E("vector", lambda v, par_c=par_c, dc=dc, sub=sub, pt=pt: v.tensor_copy(
                        znT[par_c][:, dc * 512 + sub * 128: dc * 512 + sub * 128 + 128],
                        p_trb[pt][:]),
                        waits=[(T, t_tr)], inc=V)
                    ptr_free_V[pt] = V.n
                zn_free_T[par] = T.n
            v_znT = V.n

            tok0 = (tc - 16) * 512 if is_q else tc * 512
            if not is_q:
                for cc in range(8):
                    pb = cc % 2
                    for dc in range(8):
                        E("tensor", lambda t, pb=pb, cc=cc, dc=dc, znb=znb: t.matmul(
                            p_k[pb][:],
                            wqkv_bf[:, dc * 3072 + 1024 + cc * 128:
                                    dc * 3072 + 1024 + cc * 128 + 128],
                            znb[:, dc * 512:(dc + 1) * 512],
                            start=(dc == 0), stop=(dc == 7)),
                            waits=([(V, v_znT), (S, pk_free_S[pb])] if dc == 0 else []),
                            inc=T if dc == 7 else None)
                    t_k = T.n
                    E("scalar", lambda sc, pb=pb, cc=cc: sc.activation(
                        kstg[pb][:], p_k[pb][:], AF.Identity,
                        bias=qb_sb[:, 8 + cc: 9 + cc]),
                        waits=[(T, t_k), (LD, LD.n), (V, qb_v)], inc=S)
                    pk_free_S[pb] = S.n
                    s_k = S.n
                    E("sync", lambda s, pb=pb, cc=cc, tok0=tok0: s.dma_start(
                        out=sc_k[cc * 128:(cc + 1) * 128, tok0:tok0 + 512],
                        in_=kstg[pb][:]),
                        waits=[(S, s_k)], inc=LD, amt=16)
                    kst_ld[pb] = LD.n
                for sub in range(4):
                    for vcc in range(2):
                        pb = (sub * 2 + vcc) % 2
                        for dc in range(8):
                            E("tensor", lambda t, pb=pb, sub=sub, vcc=vcc, dc=dc, znb=znb: t.matmul(
                                p_v[pb][:],
                                znb[:, dc * 512 + sub * 128: dc * 512 + sub * 128 + 128],
                                wqkv_bf[:, dc * 3072 + 2048 + vcc * 512:
                                        dc * 3072 + 2048 + vcc * 512 + 512],
                                start=(dc == 0), stop=(dc == 7)),
                                waits=([(V, v_znT), (V, pv_free_V[pb])] if dc == 0 else []),
                                inc=T if dc == 7 else None)
                        t_v = T.n
                        E("vector", lambda v, pb=pb, vcc=vcc: v.tensor_tensor(
                            vstg[pb][:].rearrange("p (c w) -> p c w", c=8, w=65)[:, :, 0:64],
                            p_v[pb][:].rearrange("p (c w) -> p c w", c=8, w=64),
                            vbias_rep[:, vcc * 512:(vcc + 1) * 512].rearrange(
                                "p (c w) -> p c w", c=8, w=64),
                            ALU.add),
                            waits=[(T, t_v), (V, vstg_ones_v), (V, vbias_v),
                                   (LD, LD.n)], inc=V)
                        pv_free_V[pb] = V.n
                        v_v = V.n
                        E("sync", lambda s, pb=pb, sub=sub, vcc=vcc, tok0=tok0: s.dma_start(
                            out=sc_v[tok0 + sub * 128: tok0 + sub * 128 + 128,
                                     vcc * 520:(vcc + 1) * 520],
                            in_=vstg[pb][:]),
                            waits=[(V, v_v)], inc=LD, amt=16)
                        vst_ld[pb] = LD.n
            else:
                for cc in range(8):
                    pb = cc % 2
                    for dc in range(8):
                        E("tensor", lambda t, pb=pb, cc=cc, dc=dc, znb=znb: t.matmul(
                            p_k[pb][:],
                            wqkv_bf[:, dc * 3072 + cc * 128: dc * 3072 + cc * 128 + 128],
                            znb[:, dc * 512:(dc + 1) * 512],
                            start=(dc == 0), stop=(dc == 7)),
                            waits=([(V, v_znT), (S, pk_free_S[pb])] if dc == 0 else []),
                            inc=T if dc == 7 else None)
                    t_q = T.n
                    E("scalar", lambda sc, pb=pb, cc=cc: sc.activation(
                        qstg[pb][:], p_k[pb][:], AF.Identity, scale=0.125,
                        bias=qbq8[:, cc:cc + 1]),
                        waits=[(T, t_q), (LD, LD.n), (V, qbq8_v)], inc=S)
                    pk_free_S[pb] = S.n
                    s_q = S.n
                    E("sync", lambda s, pb=pb, cc=cc, tok0=tok0: s.dma_start(
                        out=sc_q[cc * 128:(cc + 1) * 128, tok0:tok0 + 512],
                        in_=qstg[pb][:]),
                        waits=[(S, s_q)], inc=LD, amt=16)
                    qst_ld[pb] = LD.n
            znT_mm_t[par_c] = T.n

        barrierA = [(T, T.n), (V, V.n), (S, S.n), (LD, LD.n), (GD, GD.n)] + \
            [(x, x.n) for x in XD3] + [(w, w.n) for w in WS2]
        full_barrier(barrierA)

        if 'B' not in stages:
            for o, t in dbg_outs:
                E("sync", lambda s, o=o, t=t: s.dma_start(out=o, in_=t),
                  waits=[(LD, LD.n), (GD, GD.n)], inc=LD, amt=16)
            E("sync", lambda s: s.nop(),
              waits=[(LD, LD.n), (GD, GD.n)] + barrierA)
            block = En(nc.Block())
            P.realize(block)
            return nc
        reg.reset()
        wproj_bf = reg.alloc("wproj_bf", [128, 8 * 1024], BF16)
        kT2 = [reg.alloc(f"kT2_{i}", [128, 2 * NB], BF16) for i in range(2)]
        vA = [reg.alloc(f"vA_{i}", [128, 64 * 130 + 64], BF16) for i in range(2)]
        qA = [reg.alloc(f"qA_{i}", [128, 2 * NQ], BF16) for i in range(2)]
        PT = [reg.alloc(f"PT_{i}", [128, 1024], BF16) for i in range(3)]
        astg = [reg.alloc(f"astg{i}", [64, 512], BF16) for i in range(2)]
        cvec = reg.alloc("cvec", [128, 64], F32)
        crow = reg.alloc("crow", [16, 128], BF16)
        rd65 = [reg.alloc(f"rd65_{i}", [65, 512], F32) for i in range(2)]
        rep_sb = [reg.alloc(f"rep_sb{i}", [64, 512], F32) for i in range(2)]

        p_s = [psum_at(f"p_s{i}", [128, 1024], F32, 2 * i) for i in range(3)]
        p_av = [psum_at(f"p_av{i}", [128, 512], F32, 6 + i) for i in range(2)]
        p_x = psum_at("p_x", [128, 512], F32, 4)

        for i in range(2):
            E("vector", lambda v, i=i: v.memset(kT2[i][64:128, :], 0.0),
              waits=barrierA if i == 0 else [], inc=V)
            E("vector", lambda v, i=i: v.memset(kT2[i][64:65, :], 1.0),
              waits=[(V, V.n)], inc=V)
            E("vector", lambda v, i=i: v.memset(qA[i][64:128, :], 0.0), inc=V)
            E("vector", lambda v, i=i: v.memset(vA[i][:, 64 * 130:], 0.0), inc=V)
        kones_v = V.n

        E("gpsimd", lambda g: g.dma_start(
            out=wproj_bf[:].rearrange("p (a b) -> p a b", a=8, b=1024),
            in_=wproj_h),
            waits=barrierA, inc=GD, amt=16)
        wproj_gd = GD.n

        kv_ld = [0, 0]
        buf_free_t = [0, 0]
        pt_av_t = [0, 0, 0]
        s_exp = [0, 0, 0]
        av_free_v = [0, 0]
        astg_ld = [0, 0]
        px_free_V = 0
        px_free_S = 0
        ps_sub_V = 0

        def load_pair(p, buf):
            E("sync", lambda s, p=p, buf=buf: s.dma_start(
                out=kT2[buf][0:64, 0:NB], in_=sc_k[128 * p:128 * p + 64, :]),
                waits=barrierA + [(T, buf_free_t[buf])], inc=KV[buf], amt=16)
            E("sync", lambda s, p=p, buf=buf: s.dma_start(
                out=kT2[buf][0:64, NB:2 * NB],
                in_=sc_k[128 * p + 64:128 * p + 128, :]),
                inc=KV[buf], amt=16)
            E("sync", lambda s, p=p, buf=buf: s.dma_start(
                out=qA[buf][0:64, 0:NQ], in_=sc_q[128 * p:128 * p + 64, :]),
                inc=KV[buf], amt=16)
            E("sync", lambda s, p=p, buf=buf: s.dma_start(
                out=qA[buf][0:64, NQ:2 * NQ],
                in_=sc_q[128 * p + 64:128 * p + 128, :]),
                inc=KV[buf], amt=16)
            for c in range(64):
                E("sync", lambda s, p=p, buf=buf, c=c: s.dma_start(
                    out=vA[buf][:, c * 130:(c + 1) * 130],
                    in_=sc_v[c * 128:(c + 1) * 128, 130 * p:130 * p + 130]),
                    inc=KV[buf], amt=16)
            kv_ld[buf] = KV[buf].n

        load_pair(0, 0)
        for p in range(8):
            buf = p % 2
            if p < 7:
                load_pair(p + 1, (p + 1) % 2)
            for hh in range(2):
                h = 2 * p + hh
                koff = hh * NB
                qoff = hh * NQ
                sub_slot_V = [0, 0, 0, 0]
                for qc in range(16):
                    slot = qc % 4
                    w0 = ([(KV[buf], kv_ld[buf]), (V, kones_v),
                           (V, max(av_free_v)), (S, max(s_exp)),
                           (V, ps_sub_V)] if qc == 0 else []) +                          [(V, sub_slot_V[slot])]
                    E("tensor", lambda t, buf=buf, qc=qc, koff=koff, qoff=qoff, slot=slot: t.matmul(
                        p_s[slot // 2][:, (slot % 2) * 512:(slot % 2) * 512 + 512],
                        qA[buf][0:64, qoff + qc * 128: qoff + qc * 128 + 128],
                        kT2[buf][0:64, koff:koff + 512], start=True, stop=True),
                        waits=w0, inc=T)
                    t_sub = T.n
                    E("vector", lambda v, qc=qc, slot=slot: v.reduce_max(
                        cvec[:, qc:qc + 1],
                        p_s[slot // 2][:, (slot % 2) * 512:(slot % 2) * 512 + 512],
                        axis=AX.X, negate=True),
                        waits=[(T, t_sub)], inc=V)
                    sub_slot_V[slot] = V.n
                v_cv = V.n
                ps_sub_V = v_cv
                E("tensor", lambda t: t.transpose(p_x[0:16, 0:128], cvec[:, 0:16], id32[:]),
                  waits=[(V, v_cv), (V, px_free_V), (S, px_free_S),
                         (S, max(s_exp))], inc=T)
                t_cv = T.n
                E("scalar", lambda sc: sc.activation(
                    crow[:], p_x[0:16, 0:128], AF.Copy),
                    waits=[(T, t_cv)], inc=S)
                px_free_S = S.n
                s_cv = S.n
                E("sync", lambda s: s.dma_start(
                    out=sc_c.rearrange("o (a b) -> (o a) b", a=16, b=128),
                    in_=crow[:]),
                    waits=[(S, s_cv)], inc=LD, amt=16)
                c_ld = LD.n
                E("sync", lambda s, buf=buf, qoff=qoff: s.dma_start(
                    out=qA[buf][64:65, qoff:qoff + NQ], in_=sc_c),
                    waits=[(LD, c_ld)], inc=LD, amt=16)
                crow_ld = LD.n

                for sw in range(2):
                    so = qoff + sw * 1024

                    def emit_scores(kj):
                        st_ = kj % 3
                        for half in range(2):
                            w0 = []
                            if half == 0:
                                w0 = [(LD, crow_ld), (S, s_exp[st_]),
                                      (V, ps_sub_V)]
                                if kj == 2:
                                    w0 += [(V, px_free_V), (S, px_free_S)]
                            E("tensor", lambda t, buf=buf, kj=kj, half=half, st_=st_, so=so, koff=koff: t.matmul(
                                p_s[st_][:, half * 512:(half + 1) * 512],
                                kT2[buf][0:128, koff + kj * 128: koff + kj * 128 + 128],
                                qA[buf][0:128, so + half * 512: so + half * 512 + 512],
                                start=True, stop=True),
                                waits=w0,
                                inc=T if half == 1 else None)
                        t_s = T.n
                        E("scalar", lambda sc, st_=st_: sc.activation(
                            PT[st_][:], p_s[st_][:], AF.Exp),
                            waits=[(T, t_s), (T, pt_av_t[st_])], inc=S)
                        s_exp[st_] = S.n

                    def emit_av(kj):
                        st_ = kj % 3
                        voff = 65 * hh
                        for half in range(2):
                            E("tensor", lambda t, buf=buf, kj=kj, half=half, st_=st_, voff=voff: t.matmul(
                                p_av[half][:],
                                vA[buf][:, kj * 130 + voff: kj * 130 + voff + 128],
                                PT[st_][:, half * 512:(half + 1) * 512],
                                start=(kj == 0), stop=(kj == 63)),
                                waits=([(S, s_exp[st_])] +
                                       ([(V, av_free_v[half])] if kj == 0 else [])),
                                inc=T if half == 1 else None)
                        pt_av_t[st_] = T.n

                    emit_scores(0)
                    for kj in range(1, 64):
                        emit_scores(kj)
                        emit_av(kj - 1)
                    emit_av(63)
                    t_avdone = T.n
                    for half in range(2):
                        E("vector", lambda v, half=half: v.reciprocal(
                            rd65[half][64:65, :], p_av[half][64:65, :]),
                            waits=[(T, t_avdone)], inc=V)
                        v_rd = V.n
                        E("tensor", lambda t, half=half: t.matmul(
                            p_x[0:64, :], ones65[64:65, :], rd65[half][64:65, :],
                            start=True, stop=True),
                            waits=[(V, v_rd), (V, ones65_v), (V, px_free_V),
                                   (S, px_free_S), (S, max(s_exp))], inc=T)
                        t_rep = T.n
                        E("vector", lambda v, half=half: v.tensor_copy(
                            rep_sb[half][:], p_x[0:64, :]),
                            waits=[(T, t_rep)], inc=V)
                        v_rep = V.n
                        px_free_V = v_rep
                        E("vector", lambda v, half=half: v.tensor_tensor(
                            astg[half][:], p_av[half][0:64, :], rep_sb[half][:],
                            ALU.mult),
                            waits=[(V, v_rep), (LD, LD.n)], inc=V)
                        av_free_v[half] = V.n
                        E("sync", lambda s, h=h, sw=sw, half=half: s.dma_start(
                            out=sc_at[64 * h:64 * h + 64,
                                      sw * 1024 + half * 512:
                                      sw * 1024 + half * 512 + 512],
                            in_=astg[half][:]),
                            waits=[(V, av_free_v[half])], inc=LD, amt=16)
                        astg_ld[half] = LD.n
            buf_free_t[buf] = T.n

        barrierB = [(T, T.n), (V, V.n), (S, S.n), (LD, LD.n), (GD, GD.n)] + [(c, c.n) for c in KV]
        full_barrier(barrierB)

        if 'C' not in stages:
            for o, t in dbg_outs:
                E("sync", lambda s, o=o, t=t: s.dma_start(out=o, in_=t),
                  waits=[(LD, LD.n), (GD, GD.n)] + barrierB, inc=LD, amt=16)
            E("sync", lambda s: s.nop(),
              waits=[(LD, LD.n), (GD, GD.n)] + barrierB)
            block = En(nc.Block())
            P.realize(block)
            return nc
        reg.reset()
        reg.off += 8 * 1024 * 2
        wproj_bf2 = wproj_bf
        attnc = [reg.alloc(f"attnc{i}", [128, 8 * 512], BF16) for i in range(2)]
        projT = reg.alloc("projT", [128, 8 * 512], F32)
        xq_t = [reg.alloc(f"xq_t{i}", [128, D], F32) for i in range(2)]
        l1_t = reg.alloc("l1_t", [128, D], F32)
        x2stg = [reg.alloc(f"x2stg{i}", [128, D], F32) for i in range(2)]

        p_p = [psum_at(f"p_p{i}", [128, 512], F32, i) for i in range(2)]
        p_t2 = [psum_at(f"p_t2{i}", [128, 512], F32, 2 + i) for i in range(2)]

        at_ld = [0, 0]
        at_mm_t = [0, 0]
        projT_s = 0
        projT_tr_t = 0
        pp_free_S = [0, 0]
        pt2_free_V = [0, 0]
        xqt_free_V = [0, 0]
        x2stg_free_LD = [0, 0]
        for tci in range(4):
            cbuf = tci % 2
            for dc in range(8):
                E("sync", lambda s, cbuf=cbuf, tci=tci, dc=dc: s.dma_start(
                    out=attnc[cbuf][:, dc * 512:(dc + 1) * 512],
                    in_=sc_at[dc * 128:(dc + 1) * 128, tci * 512:(tci + 1) * 512]),
                    waits=(barrierB + [(T, at_mm_t[cbuf])]) if dc == 0 else [],
                    inc=LD, amt=16)
            at_ld[cbuf] = LD.n
            for oc in range(8):
                pb = oc % 2
                for dc in range(8):
                    E("tensor", lambda t, cbuf=cbuf, oc=oc, dc=dc, pb=pb: t.matmul(
                        p_p[pb][:],
                        wproj_bf2[:, dc * 1024 + oc * 128: dc * 1024 + oc * 128 + 128],
                        attnc[cbuf][:, dc * 512:(dc + 1) * 512],
                        start=(dc == 0), stop=(dc == 7)),
                        waits=([(LD, at_ld[cbuf]), (GD, wproj_gd),
                                (S, pp_free_S[pb]), (S, projT_s)] if dc == 0 else []),
                        inc=T if dc == 7 else None)
                t_p = T.n
                E("scalar", lambda sc, oc=oc, pb=pb: sc.activation(
                    projT[:, oc * 512:(oc + 1) * 512], p_p[pb][:], AF.Identity,
                    bias=bpr_sb[:, oc:oc + 1]),
                    waits=[(T, t_p), (T, projT_tr_t)], inc=S)
                pp_free_S[pb] = S.n
            at_mm_t[cbuf] = T.n
            projT_s = S.n
            for sub in range(4):
                g = tci * 4 + sub
                xb = xq_t[g % 2]
                E("sync", lambda s, g=g, xb=xb: s.dma_start(
                    out=xb[:], in_=xq[g * 128:(g + 1) * 128, :]),
                    waits=[(V, xqt_free_V[g % 2])], inc=XQ, amt=16)
                xq_ld = XQ.n
                E("vector", lambda v, g=g, xb=xb: v.tensor_scalar(
                    l1_t[:], xb[:], mu1q[:, g:g + 1], rs1q[:, g:g + 1],
                    ALU.subtract, ALU.mult),
                    waits=[(XQ, xq_ld)], inc=V)
                v_l1 = V.n
                E("vector", lambda v, g=g, xb=xb: v.tensor_tensor(
                    x2stg[g % 2][:], xb[:], l1_t[:], ALU.add),
                    waits=[(V, v_l1), (LD, LD.n)], inc=V)
                v_x2a = V.n
                for oc in range(8):
                    pb2 = oc % 2
                    E("tensor", lambda t, sub=sub, oc=oc, pb2=pb2: t.transpose(
                        p_t2[pb2][:, 0:128],
                        projT[:, oc * 512 + sub * 128: oc * 512 + sub * 128 + 128],
                        id32[:]),
                        waits=([(S, projT_s)] if (sub == 0 and oc == 0) else []) +
                              [(V, pt2_free_V[pb2])],
                        inc=T)
                    t_tr2 = T.n
                    E("vector", lambda v, g=g, oc=oc, pb2=pb2: v.tensor_tensor(
                        x2stg[g % 2][:, oc * 128:(oc + 1) * 128],
                        x2stg[g % 2][:, oc * 128:(oc + 1) * 128],
                        p_t2[pb2][:, 0:128], ALU.add),
                        waits=[(T, t_tr2), (V, v_x2a)], inc=V)
                    pt2_free_V[pb2] = V.n
                projT_tr_t = T.n
                v_x2 = V.n
                xqt_free_V[g % 2] = v_x2
                E("sync", lambda s, g=g: s.dma_start(
                    out=sc_x2[g * 128:(g + 1) * 128, :], in_=x2stg[g % 2][:]),
                    waits=[(V, v_x2)], inc=LD, amt=16)
                x2stg_free_LD[g % 2] = LD.n

        barrierC = [(T, T.n), (V, V.n), (S, S.n), (LD, LD.n), (GD, GD.n), (XQ, XQ.n)]
        full_barrier(barrierC)

        if 'D' not in stages:
            for o, t in dbg_outs:
                E("sync", lambda s, o=o, t=t: s.dma_start(out=o, in_=t),
                  waits=[(LD, LD.n), (GD, GD.n)] + barrierC, inc=LD, amt=16)
            E("sync", lambda s: s.nop(),
              waits=[(LD, LD.n), (GD, GD.n)] + barrierC)
            block = En(nc.Block())
            P.realize(block)
            return nc
        reg.reset()
        x2til = [reg.alloc(f"x2til{i}", [128, D], F32) for i in range(2)]
        x2ln = [reg.alloc("x2ln0", [128, D], F32)] * 2
        hs_t = reg.alloc("hs_t", [128, D], F32)
        hsTb = reg.alloc("hsTb", [128, 8 * 512], BF16)
        hsTf = [reg.alloc(f"hsTf{i}", [128, 128], F32) for i in range(2)]
        base_sb = reg.alloc("base_sb", [128, 32 * 512], BF16)
        hsum = reg.alloc("hsum", [128, 32 * 512], BF16)
        gt = reg.alloc("gt", [128, 512], BF16)
        wiT_c = [reg.alloc(f"wiT_c{i}", [128, 8 * 128], BF16) for i in range(2)]
        woT_c = [reg.alloc(f"woT_c{i}", [128, 32 * 128], BF16) for i in range(2)]
        lbT_c = [reg.alloc(f"lbT_c{i}", [16, DFF], BF16) for i in range(2)]
        a_sb = reg.alloc("a_sb", [16, 8 * 512], BF16)
        grep = [reg.alloc(f"grep{i}", [128, 512], F32) for i in range(2)]
        oacc = reg.alloc("oacc", [128, 8 * 512], F32)
        t3 = reg.alloc("t3", [128, 512], F32)
        lg_sb = reg.alloc("lg_sb", [128, 8], F32)
        pex = reg.alloc("pex", [128, 8], F32)
        pmask = reg.alloc("pmask", [128, 8], F32)
        gates_tok = reg.alloc("gates_tok", [128, 64], F32)
        gatesT = reg.alloc("gatesT", [8, 512], F32)
        vs2 = reg.alloc("vs2", [128, 4], F32)

        p_b = [psum_at(f"p_b{i}", [128, 512], F32, i) for i in range(2)]
        p_u = [psum_at(f"p_u{i}", [128, 512], F32, 2 + i) for i in range(2)]
        p_w = [psum_at(f"p_w{i}", [128, 512], F32, 4 + i) for i in range(2)]
        p_lg = psum_at("p_lg", [128, 512], F32, 6)
        p_d = psum_at("p_d", [128, 512], F32, 7)

        def gc(c):
            return gst[:, c:c + 1]

        pd_free = {"V": 0, "S": 0}

        x2t_free = [[], []]
        x2e_free = [[], []]
        pb_free_S = [0, 0]
        pu_free_V = [0, 0]
        pu_free_S = [0, 0]
        pw_free_V = [0, 0]
        wiT_free_T = [0, 0]
        woT_free_T = [0, 0]
        lbT_free_T = [0, 0]
        hsum_read_T = 0
        hsTf_read_T = [0, 0]
        plg_read_V = [0, 0, 0, 0]
        base_read_V = 0
        base_evac_S = [0] * 32
        hs_read_T = 0
        gatesT_read_T = 0
        gates_tok_read_T = 0
        prev_s_ex = 0
        prev_gate_V = 0
        x3_free_LD = 0

        for tci in range(4):
            s_gT_list = []
            for sub in range(4):
                g = tci * 4 + sub
                xb = x2ln[g % 2]
                E("sync", lambda s, g=g, xb=xb: s.dma_start(
                    out=xb[:], in_=sc_x2[g * 128:(g + 1) * 128, :]),
                    waits=(barrierC if g == 0 else []) + x2t_free[0] + x2t_free[1],
                    inc=XT2[g % 2], amt=16)
                x2_ld = XT2[g % 2].n
                E("scalar", lambda sc, xb=xb, g=g: sc.activation(
                    dmy[:], xb[:], AF.Copy, accum_out=gc(0)),
                    waits=[(XT2[g % 2], x2_ld), (V, prev_gate_V)], inc=S)
                s_s = S.n
                E("scalar", lambda sc, xb=xb, g=g: sc.activation(
                    dmy[:], xb[:], AF.Square, accum_out=gc(2)),
                    waits=[(S, S.n)], inc=S)
                s_sq2 = S.n
                E("vector", lambda v: v.tensor_scalar(
                    gc(1), gc(0), 1.0 / 1024.0, None, ALU.mult),
                    waits=[(S, s_s), (V, V.n)], inc=V)
                E("vector", lambda v: v.tensor_tensor(
                    gc(5), gc(1), gc(1), ALU.mult), waits=[(V, V.n)], inc=V)
                E("vector", lambda v: v.tensor_scalar(
                    gc(3), gc(2), 1.0 / 1024.0, gc(5), ALU.mult, ALU.subtract),
                    waits=[(V, V.n), (S, s_sq2)], inc=V)
                v_var = V.n
                E("scalar", lambda sc: sc.activation(
                    gc(4), gc(3), AF.Ln, bias=eps_sb[:]),
                    waits=[(V, v_var)], inc=S)
                E("scalar", lambda sc: sc.activation(
                    gc(4), gc(4), AF.Exp, scale=-0.5), waits=[(S, S.n)], inc=S)
                s_rs = S.n
                E("vector", lambda v, xb=xb: v.tensor_scalar(
                    hs_t[:], xb[:], gc(1), gc(4), ALU.subtract, ALU.mult),
                    waits=[(S, s_rs), (T, hs_read_T), (V, V.n)], inc=V)
                v_hs = V.n
                x2t_free[g % 2] = [(V, v_hs), (S, s_sq2)]
                for dc in range(8):
                    pf = dc % 2
                    E("tensor", lambda t, dc=dc, pf=pf: t.transpose(
                        p_u[pf][:, 0:128], hs_t[:, dc * 128:(dc + 1) * 128],
                        id32[:]),
                        waits=([(V, v_hs)] if dc == 0 else []) +
                              [(V, pu_free_V[pf]), (S, pu_free_S[pf])], inc=T)
                    t_t = T.n
                    E("vector", lambda v, dc=dc, sub=sub, pf=pf: v.tensor_copy(
                        hsTb[:, dc * 512 + sub * 128: dc * 512 + sub * 128 + 128],
                        p_u[pf][:, 0:128]),
                        waits=[(T, t_t)], inc=V)
                    pu_free_V[pf] = V.n
                    E("vector", lambda v, pf=pf: v.tensor_copy(
                        hsTf[pf][:], p_u[pf][:, 0:128]),
                        waits=[(T, t_t), (T, hsTf_read_T[pf])], inc=V)
                    pu_free_V[pf] = V.n
                    s_hf = V.n
                    E("tensor", lambda t, sub=sub, dc=dc, pf=pf: t.matmul(
                        p_lg[:, sub * 8:(sub + 1) * 8], hsTf[pf][:],
                        gw_sb[:, dc * 8:(dc + 1) * 8],
                        start=(dc == 0), stop=(dc == 7), skip_group_check=True),
                        waits=[(V, s_hf)] +
                              ([(V, plg_read_V[sub])] if dc == 0 else []),
                        inc=T if dc == 7 else None)
                    hsTf_read_T[pf] = T.n
                hs_read_T = T.n
                t_lg = T.n
                E("vector", lambda v, sub=sub: v.tensor_tensor(
                    lg_sb[:], p_lg[:, sub * 8:(sub + 1) * 8], gbr_sb[:], ALU.add),
                    waits=[(T, t_lg), (S, prev_s_ex)], inc=V)
                plg_read_V[sub] = V.n
                E("vector", lambda v: v.reduce_max(
                    gc(5), lg_sb[:], axis=AX.X, negate=True),
                    waits=[(V, V.n)], inc=V)
                v_m1 = V.n
                E("scalar", lambda sc: sc.activation(
                    pex[:], lg_sb[:], AF.Exp, bias=gc(5)),
                    waits=[(V, v_m1)], inc=S)
                s_ex = S.n
                prev_s_ex = s_ex
                E("vector", lambda v: v.reduce_sum(gc(7), pex[:], axis=AX.X),
                  waits=[(S, s_ex), (V, V.n)], inc=V)
                v_ss = V.n
                E("vector", lambda v: v.reciprocal(gc(8), gc(7)),
                  waits=[(V, v_ss), (V, V.n)], inc=V)
                v_rsum = V.n
                E("vector", lambda v: v.tensor_scalar(
                    pex[:], pex[:], gc(8), None, ALU.mult),
                    waits=[(V, v_rsum), (V, V.n)], inc=V)
                v_p = V.n
                E("vector", lambda v: v.reduce_max(gc(6), pex[:], axis=AX.X),
                  waits=[(V, v_p), (V, V.n)], inc=V)
                v_pm1 = V.n
                E("vector", lambda v: v.tensor_scalar(
                    pmask[:], pex[:], gc(6), None, ALU.is_lt),
                    waits=[(V, v_pm1), (V, V.n)], inc=V)
                E("vector", lambda v: v.tensor_tensor(
                    pmask[:], pmask[:], pex[:], ALU.mult),
                    waits=[(V, V.n)], inc=V)
                E("vector", lambda v: v.reduce_max(gc(9), pmask[:], axis=AX.X),
                  waits=[(V, V.n)], inc=V)
                v_m2 = V.n
                E("vector", lambda v: v.tensor_tensor(
                    gc(10), gc(6), gc(9), ALU.add),
                    waits=[(V, v_m2), (V, V.n)], inc=V)
                v_vs = V.n
                E("vector", lambda v, sub=sub: v.tensor_scalar(
                    vs2[:, sub:sub + 1], gc(10), 2.0, None, ALU.mult),
                    waits=[(V, v_vs), (V, V.n)], inc=V)
                E("vector", lambda v: v.tensor_scalar(
                    gates_tok[:, 0:8], pex[:], gc(9), None, ALU.is_ge),
                    waits=[(V, v_m2), (T, gates_tok_read_T), (V, V.n)], inc=V)
                E("vector", lambda v: v.tensor_scalar(
                    gates_tok[:, 0:8], gates_tok[:, 0:8], gc(10), None, ALU.mult),
                    waits=[(V, v_vs), (V, V.n)], inc=V)
                v_g = V.n
                prev_gate_V = v_g
                E("tensor", lambda t: t.transpose(
                    p_d[0:8, 0:128], gates_tok[:, 0:8], id32[:]),
                    waits=[(V, v_g), (V, pd_free["V"]), (S, pd_free["S"])], inc=T)
                t_gT = T.n
                gates_tok_read_T = t_gT
                E("vector", lambda v, sub=sub: v.tensor_copy(
                    gatesT[:, sub * 128:(sub + 1) * 128], p_d[0:8, 0:128]),
                    waits=[(T, t_gT), (T, gatesT_read_T)], inc=V)
                pd_free["V"] = V.n
                s_gT_list.append(V.n)
            s_gatesT = s_gT_list[-1] if s_gT_list else 0
            hsTb_done_V = V.n

            for fcc in range(32):
                wb = fcc % 2
                E("gpsimd", lambda g, fcc=fcc, wb=wb: g.dma_start(
                    out=wiT_c[wb][:].rearrange("p (a b) -> p a b", a=8, b=128),
                    in_=wiT_h[fcc]),
                    waits=[(T, wiT_free_T[wb])], inc=WI2[wb], amt=16)
                g_wi = WI2[wb].n
                for dc in range(8):
                    E("tensor", lambda t, fcc=fcc, dc=dc, wb=wb: t.matmul(
                        p_b[wb][:], wiT_c[wb][:, dc * 128:(dc + 1) * 128],
                        hsTb[:, dc * 512:(dc + 1) * 512],
                        start=(dc == 0), stop=(dc == 7)),
                        waits=([(WI2[wb], g_wi), (S, pb_free_S[wb]),
                                (V, base_read_V), (V, hsTb_done_V)] if dc == 0 else []),
                        inc=T if dc == 7 else None)
                wiT_free_T[wb] = T.n
                t_b = T.n
                E("scalar", lambda sc, fcc=fcc, wb=wb: sc.activation(
                    base_sb[:, fcc * 512:(fcc + 1) * 512], p_b[wb][:], AF.Identity,
                    bias=bi_sb[:, fcc:fcc + 1]),
                    waits=[(T, t_b)], inc=S)
                pb_free_S[wb] = S.n
                base_evac_S[fcc] = S.n
            s_base = S.n

            pd_aV = pd_free["V"]
            for e in range(8):
                for dc in range(8):
                    E("tensor", lambda t, e=e, dc=dc: t.matmul(
                        p_d[0:16, :],
                        laT[:, (dc * 8 + e) * 16:(dc * 8 + e) * 16 + 16],
                        hsTb[:, dc * 512:(dc + 1) * 512],
                        start=(dc == 0), stop=(dc == 7)),
                        waits=([(GD, GD.n), (V, s_gatesT), (V, hsTb_done_V),
                                (V, pd_aV)] if dc == 0 else []),
                        inc=T if dc == 7 else None)
                t_a = T.n
                E("vector", lambda v, e=e: v.tensor_copy(
                    a_sb[:, e * 512:(e + 1) * 512], p_d[0:16, :]),
                    waits=[(T, t_a)], inc=V)
                pd_aV = V.n
            v_a_all = V.n
            pd_free["V"] = V.n
            hsT_read = T.n

            def emit_grep(e):
                E("tensor", lambda t, e=e: t.matmul(
                    p_d[:, :], sel_sb[:, e * 128:(e + 1) * 128], gatesT[:],
                    start=True, stop=True),
                    waits=[(V, s_gatesT), (V, pd_free["V"]),
                           (S, pd_free["S"])], inc=T)
                t_g = T.n
                E("vector", lambda v, e=e: v.tensor_copy(
                    grep[e % 2][:], p_d[:, :]),
                    waits=[(T, t_g)], inc=V)
                pd_free["V"] = V.n
                return V.n, t_g

            grep_done_V = [0] * 8
            g_lb = [0, 0]

            def lbT_load(e):
                eb = e % 2
                E("gpsimd", lambda gg, e=e, eb=eb: gg.dma_start(
                    out=lbT_c[eb][:], in_=lbT_h[e]),
                    waits=[(T, lbT_free_T[eb])], inc=LB2[eb], amt=16)
                g_lb[eb] = LB2[eb].n

            lbT_load(0)
            for e in range(8):
                grep_done_V[e], gatesT_read_T = emit_grep(e)
                eb = e % 2
                if e < 7:
                    lbT_load(e + 1)
                for fcc in range(32):
                    ub = fcc % 2
                    w0 = [(V, pu_free_V[ub]), (S, pu_free_S[ub])]
                    if fcc < 2:
                        w0 += [(LB2[eb], g_lb[eb]), (V, v_a_all)]
                        if e == 0:
                            w0 += [(T, hsum_read_T)]
                    E("tensor", lambda t, e=e, fcc=fcc, ub=ub, eb=eb: t.matmul(
                        p_u[ub][:], lbT_c[eb][:, fcc * 128:(fcc + 1) * 128],
                        a_sb[:, e * 512:(e + 1) * 512],
                        start=True, stop=False),
                        waits=w0, inc=None)
                    E("tensor", lambda t, fcc=fcc, ub=ub: t.matmul(
                        p_u[ub][:], idbf[:],
                        base_sb[:, fcc * 512:(fcc + 1) * 512],
                        start=False, stop=True),
                        waits=[(S, base_evac_S[fcc])] if e == 0 else [],
                        inc=T)
                    t_u = T.n
                    lbT_free_T[eb] = t_u
                    if e == 0:
                        E("vector", lambda v, fcc=fcc, ub=ub, e=e: v.scalar_tensor_tensor(
                            hsum[:, fcc * 512:(fcc + 1) * 512], p_u[ub][:],
                            0.0, grep[e % 2][:], ALU.max, ALU.mult),
                            waits=[(T, t_u), (V, grep_done_V[e])], inc=V)
                    else:
                        E("vector", lambda v, fcc=fcc, ub=ub, e=e: v.scalar_tensor_tensor(
                            gt[:], p_u[ub][:],
                            0.0, grep[e % 2][:], ALU.max, ALU.mult),
                            waits=[(T, t_u), (V, V.n)] +
                                  ([(V, grep_done_V[e])] if fcc == 0 else []),
                            inc=V)
                        E("vector", lambda v, fcc=fcc: v.tensor_tensor(
                            hsum[:, fcc * 512:(fcc + 1) * 512],
                            hsum[:, fcc * 512:(fcc + 1) * 512], gt[:], ALU.add),
                            waits=[(V, V.n)], inc=V)
                    pu_free_V[ub] = V.n
            hsum_done_V = V.n

            for dc in range(8):
                wob = dc % 2
                E("sync", lambda s, dc=dc, wob=wob: s.dma_start(
                    out=woT_c[wob][:].rearrange("p (a b) -> p a b",
                                                a=32, b=128),
                    in_=sc_wo[dc]),
                    waits=[(T, woT_free_T[wob])], inc=WO2[wob], amt=16)
                wo_ld = WO2[wob].n
                for fcc in range(32):
                    E("tensor", lambda t, dc=dc, fcc=fcc, wob=wob: t.matmul(
                        p_w[wob][:],
                        woT_c[wob][:, fcc * 128:(fcc + 1) * 128],
                        hsum[:, fcc * 512:(fcc + 1) * 512],
                        start=(fcc == 0), stop=(fcc == 31)),
                        waits=([(V, hsum_done_V), (WO2[wob], wo_ld),
                                (V, pw_free_V[wob])] if fcc == 0 else []),
                        inc=T if fcc == 31 else None)
                woT_free_T[wob] = T.n
                t_w = T.n
                E("vector", lambda v, dc=dc, wob=wob: v.tensor_copy(
                    oacc[:, dc * 512:(dc + 1) * 512], p_w[wob][:]),
                    waits=[(T, t_w)], inc=V)
                pw_free_V[wob] = V.n
            hsum_read_T = T.n
            v_oacc = V.n
            base_read_V = V.n

            oacc_read_T = 0
            for sub in range(4):
                g = tci * 4 + sub
                xb = x2til[g % 2]
                E("sync", lambda s, g=g, xb=xb: s.dma_start(
                    out=xb[:], in_=sc_x2[g * 128:(g + 1) * 128, :]),
                    waits=x2e_free[g % 2] + [(OS, OS.n)], inc=XE2[g % 2], amt=16)
                x2_ld2 = XE2[g % 2].n
                for dc in range(8):
                    pe = dc % 2
                    E("tensor", lambda t, sub=sub, dc=dc, pe=pe: t.transpose(
                        p_u[pe][:, 0:128],
                        oacc[:, dc * 512 + sub * 128: dc * 512 + sub * 128 + 128],
                        id32[:]),
                        waits=([(V, v_oacc)] if (sub == 0 and dc == 0) else []) +
                              [(V, pu_free_V[pe]), (S, pu_free_S[pe])],
                        inc=T)
                    t_e = T.n
                    E("vector", lambda v, g=g, dc=dc, pe=pe, xb=xb: v.tensor_tensor(
                        xb[:, dc * 128:(dc + 1) * 128],
                        xb[:, dc * 128:(dc + 1) * 128],
                        p_u[pe][:, 0:128], ALU.add),
                        waits=[(T, t_e), (XE2[g % 2], x2_ld2)],
                        inc=V)
                    pu_free_V[pe] = V.n
                v_e1 = V.n
                E("vector", lambda v, sub=sub: v.tensor_scalar(
                    t3[:, 0:512], bo_rep[:, 0:512], vs2[:, sub:sub + 1], None,
                    ALU.mult),
                    waits=[(V, v_e1), (V, V.n)], inc=V)
                E("vector", lambda v, sub=sub, xb=xb: v.tensor_tensor(
                    xb[:, 0:512], xb[:, 0:512], t3[:, 0:512], ALU.add),
                    waits=[(V, V.n)], inc=V)
                E("vector", lambda v, sub=sub: v.tensor_scalar(
                    t3[:, 0:512], bo_rep[:, 512:1024], vs2[:, sub:sub + 1], None,
                    ALU.mult),
                    waits=[(V, V.n)], inc=V)
                E("vector", lambda v, sub=sub, xb=xb: v.tensor_tensor(
                    xb[:, 512:1024], xb[:, 512:1024], t3[:, 0:512], ALU.add),
                    waits=[(V, V.n)], inc=V)
                v_e2 = V.n
                E("sync", lambda s, tci=tci, sub=sub, xb=xb: s.dma_start(
                    out=out[tci * 512 + sub * 128: tci * 512 + sub * 128 + 128, :],
                    in_=xb[:]),
                    waits=[(V, v_e2)], inc=OS, amt=16)
                x2e_free[g % 2] = [(V, v_e2)]
                x3_free_LD = OS.n

        if dbg == 2:
            for nm, t, shp in [("gatesT", gatesT, [8, 512]), ("gst", gst, [128, 16]),
                               ("lg", lg_sb, [128, 8]), ("pex", pex, [128, 8]),
                               ("vs2", vs2, [128, 4]), ("pmask", pmask, [128, 8])]:
                o = nc.dram_tensor("dbg3_" + nm, shp, t.dtype, kind="ExternalOutput").ap()
                E("sync", lambda s, o=o, t=t: s.dma_start(out=o, in_=t[:]),
                  waits=[(c, c.n) for c in [LD, OS]] + [(T, T.n), (V, V.n), (S, S.n)],
                  inc=LD, amt=16)
        if dbg == 1:
            dmap = [("base", base_sb, [128, 32 * 512]), ("hsum", hsum, [128, 32 * 512]),
                    ("asb", a_sb, [16, 8 * 512]),
                    ("gatesT", gatesT, [8, 512]), ("grep0", grep[0], [128, 512]),
                    ("grep1", grep[1], [128, 512]), ("oacc", oacc, [128, 8 * 512]),
                    ("vs2", vs2, [128, 4]), ("hsTb", hsTb, [128, 8 * 512])]
            for nm, t, shp in dmap:
                o = nc.dram_tensor("dbg2_" + nm, shp, t.dtype, kind="ExternalOutput").ap()
                E("sync", lambda s, o=o, t=t: s.dma_start(out=o, in_=t[:]),
                  waits=[(c, c.n) for c in [LD, OS]] + [(T, T.n), (V, V.n), (S, S.n)],
                  inc=LD, amt=16)
        for o, t in dbg_outs:
            E("sync", lambda s, o=o, t=t: s.dma_start(out=o, in_=t),
              waits=[(LD, LD.n)], inc=LD, amt=16)

        allsems = [LD, GD, OS, XQ] + LB2 + WS2 + KV + XD3 + WO2 + XT2 + WI2 + XE2
        E("sync", lambda s: s.nop(), waits=[(c, c.n) for c in allsems])

        block = En(nc.Block())
        P.realize(block)

    return nc


def _layout(inputs):
    w_qkv = np.asarray(inputs["L0_attn_w_qkv"], dtype=np.float32)
    perm = []
    for t in range(3):
        for h in range(16):
            for d in range(64):
                perm.append(192 * h + 3 * d + t)
    wqkv_perm = np.ascontiguousarray(w_qkv[:, perm])
    wqkv_h = np.ascontiguousarray(wqkv_perm.reshape(8, 128, 3072).transpose(1, 0, 2))

    gamma_c = np.ascontiguousarray(
        np.asarray(inputs["L0_attn_ln_scale"], np.float32).reshape(8, 128).T)
    beta_c = np.ascontiguousarray(
        np.asarray(inputs["L0_attn_ln_bias"], np.float32).reshape(8, 128).T)
    wproj_h = np.ascontiguousarray(
        np.asarray(inputs["L0_attn_w_proj"], np.float32).reshape(8, 128, 1024)
        .transpose(1, 0, 2))
    bproj_c = np.ascontiguousarray(
        np.asarray(inputs["L0_attn_b_proj"], np.float32).reshape(8, 128).T)

    wi = np.asarray(inputs["L0_ffn_wi"], np.float32)
    wiT_h = np.ascontiguousarray(wi.reshape(32, 128, 8, 128).transpose(0, 3, 2, 1))
    bi_c = np.ascontiguousarray(
        np.asarray(inputs["L0_ffn_bi"], np.float32).reshape(32, 128).T)
    wo = np.asarray(inputs["L0_ffn_wo"], np.float32)
    woT_h = np.ascontiguousarray(wo.reshape(8, 128, 32, 128).transpose(0, 3, 2, 1))

    laT_h = np.ascontiguousarray(
        np.asarray(inputs["L0_ffn_lora_A"], np.float32).transpose(2, 0, 1)
        .reshape(8, 128, 8, 16).transpose(1, 0, 2, 3))
    lbT_h = np.ascontiguousarray(
        np.asarray(inputs["L0_ffn_lora_B"], np.float32).transpose(0, 2, 1))
    gwT_h = np.ascontiguousarray(
        np.asarray(inputs["L0_ffn_gate_w"], np.float32).T.reshape(8, 128, 8)
        .transpose(1, 0, 2))
    gb_row = np.ascontiguousarray(
        np.asarray(inputs["L0_ffn_gate_b"], np.float32).reshape(1, 8))
    bo_row = np.ascontiguousarray(
        np.asarray(inputs["L0_ffn_bo"], np.float32).reshape(1, D))

    return dict(wqkv_h=wqkv_h, gamma_c=gamma_c, beta_c=beta_c, wproj_h=wproj_h,
                bproj_c=bproj_c, wiT_h=wiT_h, bi_c=bi_c, woT_h=woT_h, laT_h=laT_h,
                lbT_h=lbT_h, gwT_h=gwT_h, gb_row=gb_row, bo_row=bo_row)


def kernel(**inputs):
    x = np.asarray(inputs["L0_attn_z"], dtype=np.float32)
    shared = _layout(inputs)
    nc = build_nc()
    in_maps = []
    for c in range(8):
        b = c // 4
        qoff = (c % 4) * NQ
        m = dict(shared)
        m["x_b"] = np.ascontiguousarray(x[b])
        m["xq"] = np.ascontiguousarray(x[b, qoff:qoff + NQ])
        in_maps.append(m)
    res = run_bass_kernel_spmd(nc, in_maps, core_ids=list(range(8)))
    outp = np.empty((2, NB, D), dtype=np.float32)
    for c in range(8):
        b = c // 4
        qoff = (c % 4) * NQ
        outp[b, qoff:qoff + NQ] = res.results[c]["out"]
    return outp


# version 13
# speedup vs baseline: 1.0053x; 1.0053x over previous
import contextlib
import numpy as np
import ml_dtypes

import concourse.bass as bass
import concourse.mybir as mybir
from concourse.bass_utils import run_bass_kernel_spmd

BFNP = ml_dtypes.bfloat16
F32 = mybir.dt.float32
BF16 = mybir.dt.bfloat16
AF = mybir.ActivationFunctionType
ALU = mybir.AluOpType
AX = mybir.AxisListType

D = 1024
NB = 8192
NQ = 2048
DFF = 4096

PERS_BYTES = 22 * 1024
SBUF_REGION = 185 * 1024


class Ctr:
    def __init__(self, handle):
        self.h = handle
        self.n = 0

    def inc(self, amt):
        self.n += amt
        return self.n


class Prog:
    ENGS = ("tensor", "vector", "scalar", "sync", "gpsimd")

    def __init__(self):
        self.stream = {e: [] for e in self.ENGS}

    def emit(self, eng, fn, waits=(), inc=None, amt=1):
        if inc is not None:
            inc.inc(amt)
        self.stream[eng].append(
            (fn, [w for w in waits if w is not None and w[1]], inc, amt))

    def realize(self, block):
        streams = self.stream

        def run(eng_name, eng):
            waited = {}
            for fn, waits, inc, amt in streams[eng_name]:
                for (ctr, val) in waits:
                    if waited.get(id(ctr), -1) >= val:
                        continue
                    eng.wait_ge(ctr.h, val)
                    waited[id(ctr)] = val
                inst = fn(eng)
                if inc is not None:
                    assert inst is not None, f"{eng_name} op with inc returned None"
                    inst.then_inc(inc.h, amt)

        @block.tensor
        def _(t):
            run("tensor", t)

        @block.vector
        def _(v):
            run("vector", v)

        @block.scalar
        def _(s):
            run("scalar", s)

        @block.sync
        def _(s):
            run("sync", s)

        @block.gpsimd
        def _(g):
            run("gpsimd", g)


class Arena:
    def __init__(self, nc, backing, base_off, limit):
        self.nc = nc
        self.backing = backing
        self.base = nc.lookup_mloc(backing).addr + base_off
        self.off = 0
        self.limit = limit
        self.uid = 0

    def alloc(self, name, shape, dtype):
        nbytes = int(np.prod(shape[1:])) * mybir.dt.size(dtype)
        nbytes = (nbytes + 63) // 64 * 64
        assert self.off + nbytes <= self.limit, \
            f"SBUF region overflow at {name}: {self.off + nbytes} > {self.limit}"
        h = self.nc.alloc_sbuf_tensor_at(
            f"{name}_u{self.uid}", shape, dtype, offset=self.base + self.off)
        self.uid += 1
        self.off += nbytes
        return h

    def reset(self):
        self.off = 0


def build_nc(dbg=False, stages='ABCD', dparts=None):
    nc = bass.Bass("TRN2", target_bir_lowering=False)

    def din(name, shape):
        return nc.dram_tensor(name, shape, F32, kind="ExternalInput").ap()

    x_b = din("x_b", [NB, D])
    xq = din("xq", [NQ, D])
    wqkv_h = din("wqkv_h", [128, 8, 3072])
    beta_c = din("beta_c", [128, 8])
    gamma_c = din("gamma_c", [128, 8])
    wproj_h = din("wproj_h", [128, 8, 1024])
    bproj_c = din("bproj_c", [128, 8])
    wiT_h = din("wiT_h", [32, 128, 8, 128])
    bi_c = din("bi_c", [128, 32])
    woT_h = din("woT_h", [8, 128, 32, 128])
    laT_h = din("laT_h", [128, 8, 8, 16])
    lbT_h = din("lbT_h", [8, 16, DFF])
    gwT_h = din("gwT_h", [128, 8, 8])
    gb_row = din("gb_row", [1, 8])
    bo_row = din("bo_row", [1, D])

    out = nc.dram_tensor("out", [NQ, D], F32, kind="ExternalOutput").ap()

    def scr(name, shape, dt):
        return nc.dram_tensor(name, shape, dt, kind="Internal").ap()

    sc_k = scr("sc_k", [D, NB], BF16)
    sc_v = scr("sc_v", [NB, 16 * 65], BF16)
    sc_q = scr("sc_q", [D, NQ], BF16)
    sc_at = scr("sc_at", [D, NQ], BF16)
    sc_x2 = scr("sc_x2", [NQ, D], F32)
    sc_c = scr("sc_c", [1, NQ], BF16)
    sc_wo = scr("sc_wo", [8, 128, DFF], BF16)

    dbg_outs = []
    if dbg:
        for nm, t in [("dk", sc_k), ("dv", sc_v), ("dq", sc_q), ("dat", sc_at),
                      ("dx2", sc_x2)]:
            o = nc.dram_tensor("dbg_" + nm, [int(s) for s in t.tensor.shape],
                               t.tensor.dtype, kind="ExternalOutput").ap()
            dbg_outs.append((o, t))

    ident_f32 = nc.inline_tensor(np.eye(128, dtype=np.float32), "c_idf")
    ident_bf = nc.inline_tensor(np.eye(128, dtype=BFNP), "c_idb")
    ones_c = nc.inline_tensor(np.ones((1, 128), dtype=np.float32), "c_ones")
    eps_c = nc.inline_tensor(np.full((128, 1), 1e-5, dtype=np.float32), "c_eps")
    selm = np.zeros((8, 8 * 128), dtype=np.float32)
    for e in range(8):
        selm[e, e * 128:(e + 1) * 128] = 1.0
    sel_c = nc.inline_tensor(selm, "c_sel")

    P = Prog()
    E = P.emit

    ctx = contextlib.ExitStack()
    with ctx:
        En = ctx.enter_context
        backing = En(nc.sbuf_tensor("arena", [128, PERS_BYTES + SBUF_REGION],
                                    mybir.dt.uint8))
        pers = Arena(nc, backing, 0, PERS_BYTES)
        reg = Arena(nc, backing, PERS_BYTES, SBUF_REGION)


        def full_barrier(waitlist):
            for eng in ("tensor", "vector", "scalar", "sync", "gpsimd"):
                E(eng, lambda e: e.nop(), waits=list(waitlist))
        _puid = [0]

        def psum_at(name, shape, dtype, bank):
            h = nc.virtual_psum_tensor(f"{name}_u{_puid[0]}", shape, dtype)
            _puid[0] += 1
            nc._tensor(h.name, h.shape, h.dtype, type="PSUM")
            mloc = nc.lookup_mloc(h)
            mloc.allocated = True
            mloc.addr = bank * 2048
            return h

        LD = Ctr(En(nc.semaphore("LD")))
        GD = Ctr(En(nc.semaphore("GD")))
        T = Ctr(En(nc.semaphore("T")))
        V = Ctr(En(nc.semaphore("V")))
        S = Ctr(En(nc.semaphore("S")))
        KV = [Ctr(En(nc.semaphore("KV0"))), Ctr(En(nc.semaphore("KV1")))]
        XD3 = [Ctr(En(nc.semaphore(f"XD{i}"))) for i in range(3)]
        XQ = Ctr(En(nc.semaphore("XQ")))
        OS = Ctr(En(nc.semaphore("OS")))
        WO2 = [Ctr(En(nc.semaphore(f"WO{i}"))) for i in range(2)]
        XT2 = [Ctr(En(nc.semaphore(f"XT{i}"))) for i in range(2)]
        WI2 = [Ctr(En(nc.semaphore(f"WI{i}"))) for i in range(2)]
        XE2 = [Ctr(En(nc.semaphore(f"XE{i}"))) for i in range(2)]
        LB2 = [Ctr(En(nc.semaphore(f"LB{i}"))) for i in range(2)]
        WS2 = [Ctr(En(nc.semaphore(f"WS{i}"))) for i in range(2)]
        VST = Ctr(En(nc.semaphore("VST")))
        SST = Ctr(En(nc.semaphore("SST")))

        id32 = pers.alloc("id32", [128, 128], F32)
        idbf = pers.alloc("idbf", [128, 128], BF16)
        ones128 = pers.alloc("ones128", [2, 128], F32)
        ones65 = pers.alloc("ones65", [65, 64], F32)
        eps_sb = pers.alloc("eps_sb", [128, 1], F32)
        sel_sb = pers.alloc("sel_sb", [8, 8 * 128], F32)
        gam_sb = pers.alloc("gam_sb", [128, 8], F32)
        bet_sb = pers.alloc("bet_sb", [128, 64], F32)
        bpr_sb = pers.alloc("bpr_sb", [128, 8], F32)
        bi_sb = pers.alloc("bi_sb", [128, 32], F32)
        gw_sb = pers.alloc("gw_sb", [128, 64], F32)
        gbr_sb = pers.alloc("gbr_sb", [128, 8], F32)
        bo_rep = pers.alloc("bo_rep", [128, D], F32)
        qb_sb = pers.alloc("qb_sb", [128, 64], F32)
        qbq8 = pers.alloc("qbq8", [128, 8], F32)
        laT = pers.alloc("laT", [128, 1024], BF16)
        mu1q = pers.alloc("mu1q", [128, 16], F32)
        rs1q = pers.alloc("rs1q", [128, 16], F32)
        dmy = pers.alloc("dmy", [128, 1024], BF16)
        dmy2 = pers.alloc("dmy2", [128, 1024], BF16)
        st_sb = pers.alloc("st_sb", [128, 32], F32)
        gst = pers.alloc("gst", [128, 16], F32)
        gbrow = pers.alloc("gbrow", [2, 128], F32)
        borow = pers.alloc("borow", [2, D], F32)

        def ld(dst, src):
            E("sync", lambda s, d=dst, so=src: s.dma_start(out=d, in_=so),
              inc=LD, amt=16)

        ld(id32[:], ident_f32.ap())
        ld(idbf[:], ident_bf.ap())
        ld(ones128[0:1, :], ones_c.ap())
        ld(eps_sb[:], eps_c.ap())
        ld(sel_sb[:], sel_c.ap())
        ld(gam_sb[:], gamma_c)
        ld(bet_sb[:, 0:8], beta_c)
        ld(bpr_sb[:], bproj_c)
        ld(bi_sb[:], bi_c)
        ld(gw_sb[:], gwT_h.rearrange("p a b -> p (a b)"))
        ld(gbrow[0:1, 0:8], gb_row)
        ld(borow[0:1, :], bo_row)
        ld_pers = LD.n

        E("gpsimd", lambda g: g.dma_start(
            out=laT[:], in_=laT_h.rearrange("p a e r -> p (a e r)")),
            inc=GD, amt=16)
        laT_gd = GD.n

        E("vector", lambda v: v.memset(ones65[64:65, :], 1.0),
          waits=[(LD, ld_pers)], inc=V)
        ones65_v = V.n

        p_i0 = psum_at("p_i0", [128, 512], F32, 7)
        E("tensor", lambda t: t.matmul(p_i0[:, 0:8], ones128[0:1, :], gbrow[0:1, 0:8],
                                       start=True, stop=True),
          waits=[(LD, ld_pers)], inc=T)
        tv = T.n
        E("vector", lambda v: v.tensor_copy(gbr_sb[:], p_i0[:, 0:8]),
          waits=[(T, tv)], inc=V)
        vv = V.n
        for i in range(2):
            E("tensor", lambda t, i=i: t.matmul(
                p_i0[:], ones128[0:1, :], borow[0:1, i * 512:(i + 1) * 512],
                start=True, stop=True),
                waits=[(V, vv)], inc=T)
            tv = T.n
            E("vector", lambda v, i=i: v.tensor_copy(
                bo_rep[:, i * 512:(i + 1) * 512], p_i0[:]),
                waits=[(T, tv)], inc=V)
            vv = V.n

        reg.reset()
        wqkv_bf = reg.alloc("wqkv_bf", [128, 8 * 3072], BF16)
        stg_w = [reg.alloc(f"stg_w{i}", [128, 3072], F32) for i in range(2)]
        wostg = [reg.alloc(f"wostg{i}", [128, 4096], BF16) for i in range(2)]
        x_t = [reg.alloc(f"x_t{i}", [128, D], F32) for i in range(3)]
        zn_t = [reg.alloc(f"zn_t{i}", [128, D], BF16) for i in range(2)]
        znT = [reg.alloc(f"znT{i}", [128, 8 * 512], BF16) for i in range(2)]
        kstg = [reg.alloc(f"kstg{i}", [128, 512], BF16) for i in range(2)]
        vstg = [reg.alloc(f"vstg{i}", [128, 520], BF16) for i in range(2)]
        qstg = [reg.alloc(f"qstg{i}", [128, 512], BF16) for i in range(2)]
        vb8 = reg.alloc("vb8", [8, 128], F32)
        vbias_rep = reg.alloc("vbias_rep", [128, D], F32)

        p_qb = psum_at("p_qb", [128, 512], F32, 6)
        p_tr = [psum_at(f"p_tr{i}", [128, 128], F32, i) for i in range(2)]
        p_trb = [psum_at(f"p_trb{i}", [128, 128], BF16, i) for i in range(2)]
        p_k = [psum_at(f"p_k{i}", [128, 512], F32, 2 + i) for i in range(2)]
        p_v = [psum_at(f"p_v{i}", [128, 512], F32, 4 + i) for i in range(2)]

        for dc in range(8):
            wb = dc % 2
            W = WS2[wb]
            E("gpsimd", lambda g, dc=dc, wb=wb: g.dma_start(
                out=wostg[wb][:].rearrange("p (a c) -> p a c", a=32, c=128),
                in_=woT_h[dc]),
                waits=[(W, W.n)], inc=W, amt=16)
            E("gpsimd", lambda g, dc=dc, wb=wb: g.dma_start(
                out=sc_wo[dc],
                in_=wostg[wb][:].rearrange("p (a c) -> p a c", a=32, c=128)),
                waits=[(W, W.n)], inc=W, amt=16)

        stg_free_v = [0, 0]
        qb_t = 0
        qb_acc_v = 0
        for dc in range(8):
            b = dc % 2
            E("gpsimd", lambda g, b=b, dc=dc: g.dma_start(
                out=stg_w[b][:], in_=wqkv_h[:, dc, :]),
                waits=[(T, qb_t), (V, stg_free_v[b])],
                inc=GD, amt=16)
            wld = GD.n
            for cc in range(24):
                E("tensor", lambda t, b=b, cc=cc, dc=dc: t.matmul(
                    p_qb[:, cc:cc + 1], stg_w[b][:, cc * 128:(cc + 1) * 128],
                    bet_sb[:, dc:dc + 1], start=True, stop=True),
                    waits=([(GD, wld), (V, qb_acc_v)] if cc == 0 else []),
                    inc=T if cc == 23 else None)
            qb_t = T.n
            if dc == 0:
                E("vector", lambda v: v.tensor_copy(qb_sb[:, 0:24], p_qb[:, 0:24]),
                  waits=[(T, qb_t)], inc=V)
            else:
                E("vector", lambda v: v.tensor_tensor(
                    qb_sb[:, 0:24], qb_sb[:, 0:24], p_qb[:, 0:24], ALU.add),
                    waits=[(T, qb_t), (V, V.n)], inc=V)
            qb_acc_v = V.n
            E("vector", lambda v, b=b, dc=dc: v.tensor_scalar(
                wqkv_bf[:, dc * 3072:(dc + 1) * 3072], stg_w[b][:],
                gam_sb[:, dc:dc + 1], None, ALU.mult),
                waits=[(GD, wld), (T, qb_t)], inc=V)
            stg_free_v[b] = V.n

        qb_v = qb_acc_v
        E("vector", lambda v: v.tensor_scalar(
            qbq8[:], qb_sb[:, 0:8], 0.125, None, ALU.mult),
            waits=[(V, qb_v), (V, V.n)], inc=V)
        qbq8_v = V.n
        E("tensor", lambda t: t.transpose(p_tr[0][0:8, 0:128], qb_sb[:, 16:24], id32[:]),
          waits=[(V, qb_v)], inc=T)
        vb_t = T.n
        E("vector", lambda v: v.tensor_copy(vb8[:], p_tr[0][0:8, 0:128]),
          waits=[(T, vb_t)], inc=V)
        vb_v = V.n
        for i in range(8):
            E("tensor", lambda t, i=i: t.matmul(
                p_k[0][:, 0:128], sel_sb[:, i * 128:(i + 1) * 128], vb8[:],
                start=True, stop=True),
                waits=[(V, vb_v), (V, V.n)], inc=T)
            t_r = T.n
            E("vector", lambda v, i=i: v.tensor_copy(
                vbias_rep[:, i * 128:(i + 1) * 128], p_k[0][:, 0:128]),
                waits=[(T, t_r)], inc=V)
        vbias_v = V.n

        for i in range(2):
            E("vector", lambda v, i=i: v.memset(
                vstg[i][:].rearrange("p (c w) -> p c w", c=8, w=65)[:, :, 64:65],
                1.0), inc=V)
        vstg_ones_v = V.n

        def st(par, col):
            return st_sb[:, par * 16 + col: par * 16 + col + 1]

        x_free_S = [0, 0, 0]
        x_free_V = [0, 0, 0]
        zn_free_T = [0, 0]
        znT_mm_t = [0, 0]
        st_free_S = [0, 0]
        kst_ld = [0, 0]
        vst_ld = [0, 0]
        qst_ld = [0, 0]
        pk_free_S = [0, 0]
        pv_free_V = [vbias_v, 0]
        ptr_free_V = [0, 0]

        for tc in range(20):
            is_q = tc >= 16
            par_c = tc % 2
            znb = znT[par_c]
            for sub in range(4):
                g = tc * 4 + sub
                xb = x_t[g % 3]
                par = g % 2
                if is_q:
                    src = xq[(tc - 16) * 512 + sub * 128:
                             (tc - 16) * 512 + sub * 128 + 128, :]
                else:
                    src = x_b[tc * 512 + sub * 128: tc * 512 + sub * 128 + 128, :]
                E("sync", lambda s, xb=xb, src=src: s.dma_start(out=xb[:], in_=src),
                  waits=[(S, x_free_S[g % 3]), (V, x_free_V[g % 3])], inc=XD3[g % 3], amt=16)
                x_ld = XD3[g % 3].n
                E("scalar", lambda sc, xb=xb, par=par: sc.activation(
                    dmy[:], xb[:], AF.Copy, accum_out=st(par, 0)),
                    waits=[(XD3[g % 3], x_ld), (V, st_free_S[par])], inc=S)
                s_sum = S.n
                E("scalar", lambda sc, xb=xb, par=par: sc.activation(
                    dmy[:], xb[:], AF.Square, accum_out=st(par, 4)),
                    waits=[(S, S.n)], inc=S)
                s_sq = S.n
                x_free_S[g % 3] = s_sq
                E("vector", lambda v, par=par: v.tensor_scalar(
                    st(par, 1), st(par, 0), 1.0 / 1024.0, None, ALU.mult),
                    waits=[(S, s_sum), (V, V.n)], inc=V)
                E("vector", lambda v, par=par: v.tensor_tensor(
                    st(par, 6), st(par, 1), st(par, 1), ALU.mult),
                    waits=[(V, V.n)], inc=V)
                E("vector", lambda v, par=par: v.tensor_scalar(
                    st(par, 5), st(par, 4), 1.0 / 1024.0, st(par, 6),
                    ALU.mult, ALU.subtract),
                    waits=[(V, V.n), (S, s_sq)], inc=V)
                v_v1 = V.n
                E("scalar", lambda sc, par=par: sc.activation(
                    st(par, 9), st(par, 5), AF.Ln, bias=eps_sb[:]),
                    waits=[(V, v_v1)], inc=S)
                E("scalar", lambda sc, par=par: sc.activation(
                    st(par, 9), st(par, 9), AF.Exp, scale=-0.5),
                    waits=[(S, S.n)], inc=S)
                s_r1 = S.n
                E("vector", lambda v, par=par: v.tensor_tensor(
                    st(par, 8), st(par, 9), st(par, 9), ALU.mult),
                    waits=[(S, s_r1), (V, V.n)], inc=V)
                E("vector", lambda v, par=par: v.tensor_tensor(
                    st(par, 8), st(par, 8), st(par, 5), ALU.mult),
                    waits=[(V, V.n)], inc=V)
                v_var2 = V.n
                E("scalar", lambda sc, par=par: sc.activation(
                    st(par, 10), st(par, 8), AF.Ln, bias=eps_sb[:]),
                    waits=[(V, v_var2)], inc=S)
                E("scalar", lambda sc, par=par: sc.activation(
                    st(par, 10), st(par, 10), AF.Exp, scale=-0.5),
                    waits=[(S, S.n)], inc=S)
                s_r2 = S.n
                E("vector", lambda v, par=par: v.tensor_tensor(
                    st(par, 11), st(par, 9), st(par, 10), ALU.mult),
                    waits=[(S, s_r2), (V, V.n)], inc=V)
                v_a = V.n
                if is_q:
                    qi = (tc - 16) * 4 + sub
                    E("vector", lambda v, par=par, qi=qi: v.tensor_copy(
                        mu1q[:, qi:qi + 1], st(par, 1)), waits=[(V, V.n)], inc=V)
                    E("vector", lambda v, par=par, qi=qi: v.tensor_copy(
                        rs1q[:, qi:qi + 1], st(par, 9)), inc=V)
                E("vector", lambda v, xb=xb, par=par: v.tensor_scalar(
                    zn_t[par][:], xb[:], st(par, 1), st(par, 11),
                    ALU.subtract, ALU.mult),
                    waits=[(V, v_a), (T, zn_free_T[par]), (V, V.n)], inc=V)
                v_zn = V.n
                x_free_V[g % 3] = v_zn
                st_free_S[par] = s_r2
                for dc in range(8):
                    pt = dc % 2
                    E("tensor", lambda t, par=par, dc=dc, pt=pt: t.transpose(
                        p_trb[pt][:], zn_t[par][:, dc * 128:(dc + 1) * 128], idbf[:]),
                        waits=[(V, v_zn), (T, znT_mm_t[par_c]), (V, ptr_free_V[pt])],
                        inc=T)
                    t_tr = T.n
                    E("vector", lambda v, par_c=par_c, dc=dc, sub=sub, pt=pt: v.tensor_copy(
                        znT[par_c][:, dc * 512 + sub * 128: dc * 512 + sub * 128 + 128],
                        p_trb[pt][:]),
                        waits=[(T, t_tr)], inc=V)
                    ptr_free_V[pt] = V.n
                zn_free_T[par] = T.n
            v_znT = V.n

            tok0 = (tc - 16) * 512 if is_q else tc * 512
            if not is_q:
                for cc in range(8):
                    pb = cc % 2
                    for dc in range(8):
                        E("tensor", lambda t, pb=pb, cc=cc, dc=dc, znb=znb: t.matmul(
                            p_k[pb][:],
                            wqkv_bf[:, dc * 3072 + 1024 + cc * 128:
                                    dc * 3072 + 1024 + cc * 128 + 128],
                            znb[:, dc * 512:(dc + 1) * 512],
                            start=(dc == 0), stop=(dc == 7)),
                            waits=([(V, v_znT), (S, pk_free_S[pb])] if dc == 0 else []),
                            inc=T if dc == 7 else None)
                    t_k = T.n
                    E("scalar", lambda sc, pb=pb, cc=cc: sc.activation(
                        kstg[pb][:], p_k[pb][:], AF.Identity,
                        bias=qb_sb[:, 8 + cc: 9 + cc]),
                        waits=[(T, t_k), (SST, SST.n), (V, qb_v)], inc=S)
                    pk_free_S[pb] = S.n
                    s_k = S.n
                    E("scalar", lambda s, pb=pb, cc=cc, tok0=tok0: s.dma_start(
                        out=sc_k[cc * 128:(cc + 1) * 128, tok0:tok0 + 512],
                        in_=kstg[pb][:]),
                        waits=[(S, s_k)], inc=SST, amt=16)
                    kst_ld[pb] = SST.n
                for sub in range(4):
                    for vcc in range(2):
                        pb = (sub * 2 + vcc) % 2
                        for dc in range(8):
                            E("tensor", lambda t, pb=pb, sub=sub, vcc=vcc, dc=dc, znb=znb: t.matmul(
                                p_v[pb][:],
                                znb[:, dc * 512 + sub * 128: dc * 512 + sub * 128 + 128],
                                wqkv_bf[:, dc * 3072 + 2048 + vcc * 512:
                                        dc * 3072 + 2048 + vcc * 512 + 512],
                                start=(dc == 0), stop=(dc == 7)),
                                waits=([(V, v_znT), (V, pv_free_V[pb])] if dc == 0 else []),
                                inc=T if dc == 7 else None)
                        t_v = T.n
                        E("vector", lambda v, pb=pb, vcc=vcc: v.tensor_tensor(
                            vstg[pb][:].rearrange("p (c w) -> p c w", c=8, w=65)[:, :, 0:64],
                            p_v[pb][:].rearrange("p (c w) -> p c w", c=8, w=64),
                            vbias_rep[:, vcc * 512:(vcc + 1) * 512].rearrange(
                                "p (c w) -> p c w", c=8, w=64),
                            ALU.add),
                            waits=[(T, t_v), (V, vstg_ones_v), (V, vbias_v),
                                   (VST, VST.n)], inc=V)
                        pv_free_V[pb] = V.n
                        v_v = V.n
                        E("scalar", lambda s, pb=pb, sub=sub, vcc=vcc, tok0=tok0: s.dma_start(
                            out=sc_v[tok0 + sub * 128: tok0 + sub * 128 + 128,
                                     vcc * 520:(vcc + 1) * 520],
                            in_=vstg[pb][:]),
                            waits=[(V, v_v)], inc=VST, amt=16)
                        vst_ld[pb] = VST.n
            else:
                for cc in range(8):
                    pb = cc % 2
                    for dc in range(8):
                        E("tensor", lambda t, pb=pb, cc=cc, dc=dc, znb=znb: t.matmul(
                            p_k[pb][:],
                            wqkv_bf[:, dc * 3072 + cc * 128: dc * 3072 + cc * 128 + 128],
                            znb[:, dc * 512:(dc + 1) * 512],
                            start=(dc == 0), stop=(dc == 7)),
                            waits=([(V, v_znT), (S, pk_free_S[pb])] if dc == 0 else []),
                            inc=T if dc == 7 else None)
                    t_q = T.n
                    E("scalar", lambda sc, pb=pb, cc=cc: sc.activation(
                        qstg[pb][:], p_k[pb][:], AF.Identity, scale=0.125,
                        bias=qbq8[:, cc:cc + 1]),
                        waits=[(T, t_q), (SST, SST.n), (V, qbq8_v)], inc=S)
                    pk_free_S[pb] = S.n
                    s_q = S.n
                    E("scalar", lambda s, pb=pb, cc=cc, tok0=tok0: s.dma_start(
                        out=sc_q[cc * 128:(cc + 1) * 128, tok0:tok0 + 512],
                        in_=qstg[pb][:]),
                        waits=[(S, s_q)], inc=SST, amt=16)
                    qst_ld[pb] = SST.n
            znT_mm_t[par_c] = T.n

        barrierA = [(T, T.n), (V, V.n), (S, S.n), (LD, LD.n), (GD, GD.n)] + \
            [(x, x.n) for x in XD3] + [(w, w.n) for w in WS2] + \
            [(VST, VST.n), (SST, SST.n)]
        full_barrier(barrierA)

        if 'B' not in stages:
            for o, t in dbg_outs:
                E("sync", lambda s, o=o, t=t: s.dma_start(out=o, in_=t),
                  waits=[(LD, LD.n), (GD, GD.n)], inc=LD, amt=16)
            E("sync", lambda s: s.nop(),
              waits=[(LD, LD.n), (GD, GD.n)] + barrierA)
            block = En(nc.Block())
            P.realize(block)
            return nc
        reg.reset()
        wproj_bf = reg.alloc("wproj_bf", [128, 8 * 1024], BF16)
        kT2 = [reg.alloc(f"kT2_{i}", [128, 2 * NB], BF16) for i in range(2)]
        vA = [reg.alloc(f"vA_{i}", [128, 64 * 130 + 64], BF16) for i in range(2)]
        qA = [reg.alloc(f"qA_{i}", [128, 2 * NQ], BF16) for i in range(2)]
        PT = [reg.alloc(f"PT_{i}", [128, 1024], BF16) for i in range(3)]
        astg = [reg.alloc(f"astg{i}", [64, 512], BF16) for i in range(2)]
        cvec = reg.alloc("cvec", [128, 64], F32)
        crow = reg.alloc("crow", [16, 128], BF16)
        rd65 = [reg.alloc(f"rd65_{i}", [65, 512], F32) for i in range(2)]
        rep_sb = [reg.alloc(f"rep_sb{i}", [64, 512], F32) for i in range(2)]

        p_s = [psum_at(f"p_s{i}", [128, 1024], F32, 2 * i) for i in range(3)]
        p_av = [psum_at(f"p_av{i}", [128, 512], F32, 6 + i) for i in range(2)]
        p_x = psum_at("p_x", [128, 512], F32, 4)

        for i in range(2):
            E("vector", lambda v, i=i: v.memset(kT2[i][64:128, :], 0.0),
              waits=barrierA if i == 0 else [], inc=V)
            E("vector", lambda v, i=i: v.memset(kT2[i][64:65, :], 1.0),
              waits=[(V, V.n)], inc=V)
            E("vector", lambda v, i=i: v.memset(qA[i][64:128, :], 0.0), inc=V)
            E("vector", lambda v, i=i: v.memset(vA[i][:, 64 * 130:], 0.0), inc=V)
        kones_v = V.n

        E("gpsimd", lambda g: g.dma_start(
            out=wproj_bf[:].rearrange("p (a b) -> p a b", a=8, b=1024),
            in_=wproj_h),
            waits=barrierA, inc=GD, amt=16)
        wproj_gd = GD.n

        kv_ld = [0, 0]
        buf_free_t = [0, 0]
        pt_av_t = [0, 0, 0]
        s_exp = [0, 0, 0]
        av_free_v = [0, 0]
        astg_ld = [0, 0]
        px_free_V = 0
        px_free_S = 0
        ps_sub_V = 0

        def load_pair(p, buf):
            E("sync", lambda s, p=p, buf=buf: s.dma_start(
                out=kT2[buf][0:64, 0:NB], in_=sc_k[128 * p:128 * p + 64, :]),
                waits=barrierA + [(T, buf_free_t[buf])], inc=KV[buf], amt=16)
            E("sync", lambda s, p=p, buf=buf: s.dma_start(
                out=kT2[buf][0:64, NB:2 * NB],
                in_=sc_k[128 * p + 64:128 * p + 128, :]),
                inc=KV[buf], amt=16)
            E("sync", lambda s, p=p, buf=buf: s.dma_start(
                out=qA[buf][0:64, 0:NQ], in_=sc_q[128 * p:128 * p + 64, :]),
                inc=KV[buf], amt=16)
            E("sync", lambda s, p=p, buf=buf: s.dma_start(
                out=qA[buf][0:64, NQ:2 * NQ],
                in_=sc_q[128 * p + 64:128 * p + 128, :]),
                inc=KV[buf], amt=16)
            for c in range(64):
                E("sync", lambda s, p=p, buf=buf, c=c: s.dma_start(
                    out=vA[buf][:, c * 130:(c + 1) * 130],
                    in_=sc_v[c * 128:(c + 1) * 128, 130 * p:130 * p + 130]),
                    inc=KV[buf], amt=16)
            kv_ld[buf] = KV[buf].n

        load_pair(0, 0)
        for p in range(8):
            buf = p % 2
            if p < 7:
                load_pair(p + 1, (p + 1) % 2)
            for hh in range(2):
                h = 2 * p + hh
                koff = hh * NB
                qoff = hh * NQ
                sub_slot_V = [0, 0, 0, 0]
                for qc in range(16):
                    slot = qc % 4
                    w0 = ([(KV[buf], kv_ld[buf]), (V, kones_v),
                           (V, max(av_free_v)), (S, max(s_exp)),
                           (V, ps_sub_V)] if qc == 0 else []) +                          [(V, sub_slot_V[slot])]
                    E("tensor", lambda t, buf=buf, qc=qc, koff=koff, qoff=qoff, slot=slot: t.matmul(
                        p_s[slot // 2][:, (slot % 2) * 512:(slot % 2) * 512 + 512],
                        qA[buf][0:64, qoff + qc * 128: qoff + qc * 128 + 128],
                        kT2[buf][0:64, koff:koff + 512], start=True, stop=True),
                        waits=w0, inc=T)
                    t_sub = T.n
                    E("vector", lambda v, qc=qc, slot=slot: v.reduce_max(
                        cvec[:, qc:qc + 1],
                        p_s[slot // 2][:, (slot % 2) * 512:(slot % 2) * 512 + 512],
                        axis=AX.X, negate=True),
                        waits=[(T, t_sub)], inc=V)
                    sub_slot_V[slot] = V.n
                v_cv = V.n
                ps_sub_V = v_cv
                E("tensor", lambda t: t.transpose(p_x[0:16, 0:128], cvec[:, 0:16], id32[:]),
                  waits=[(V, v_cv), (V, px_free_V), (S, px_free_S),
                         (S, max(s_exp))], inc=T)
                t_cv = T.n
                E("scalar", lambda sc: sc.activation(
                    crow[:], p_x[0:16, 0:128], AF.Copy),
                    waits=[(T, t_cv)], inc=S)
                px_free_S = S.n
                s_cv = S.n
                E("sync", lambda s: s.dma_start(
                    out=sc_c.rearrange("o (a b) -> (o a) b", a=16, b=128),
                    in_=crow[:]),
                    waits=[(S, s_cv)], inc=LD, amt=16)
                c_ld = LD.n
                E("sync", lambda s, buf=buf, qoff=qoff: s.dma_start(
                    out=qA[buf][64:65, qoff:qoff + NQ], in_=sc_c),
                    waits=[(LD, c_ld)], inc=LD, amt=16)
                crow_ld = LD.n

                for sw in range(2):
                    so = qoff + sw * 1024

                    def emit_scores(kj):
                        st_ = kj % 3
                        for half in range(2):
                            w0 = []
                            if half == 0:
                                w0 = [(LD, crow_ld), (S, s_exp[st_]),
                                      (V, ps_sub_V)]
                                if kj == 2:
                                    w0 += [(V, px_free_V), (S, px_free_S)]
                            E("tensor", lambda t, buf=buf, kj=kj, half=half, st_=st_, so=so, koff=koff: t.matmul(
                                p_s[st_][:, half * 512:(half + 1) * 512],
                                kT2[buf][0:128, koff + kj * 128: koff + kj * 128 + 128],
                                qA[buf][0:128, so + half * 512: so + half * 512 + 512],
                                start=True, stop=True),
                                waits=w0,
                                inc=T if half == 1 else None)
                        t_s = T.n
                        E("scalar", lambda sc, st_=st_: sc.activation(
                            PT[st_][:], p_s[st_][:], AF.Exp),
                            waits=[(T, t_s), (T, pt_av_t[st_])], inc=S)
                        s_exp[st_] = S.n

                    def emit_av(kj):
                        st_ = kj % 3
                        voff = 65 * hh
                        for half in range(2):
                            E("tensor", lambda t, buf=buf, kj=kj, half=half, st_=st_, voff=voff: t.matmul(
                                p_av[half][:],
                                vA[buf][:, kj * 130 + voff: kj * 130 + voff + 128],
                                PT[st_][:, half * 512:(half + 1) * 512],
                                start=(kj == 0), stop=(kj == 63)),
                                waits=([(S, s_exp[st_])] +
                                       ([(V, av_free_v[half])] if kj == 0 else [])),
                                inc=T if half == 1 else None)
                        pt_av_t[st_] = T.n

                    emit_scores(0)
                    for kj in range(1, 64):
                        emit_scores(kj)
                        emit_av(kj - 1)
                    emit_av(63)
                    t_avdone = T.n
                    for half in range(2):
                        E("vector", lambda v, half=half: v.reciprocal(
                            rd65[half][64:65, :], p_av[half][64:65, :]),
                            waits=[(T, t_avdone)], inc=V)
                        v_rd = V.n
                        E("tensor", lambda t, half=half: t.matmul(
                            p_x[0:64, :], ones65[64:65, :], rd65[half][64:65, :],
                            start=True, stop=True),
                            waits=[(V, v_rd), (V, ones65_v), (V, px_free_V),
                                   (S, px_free_S), (S, max(s_exp))], inc=T)
                        t_rep = T.n
                        E("vector", lambda v, half=half: v.tensor_copy(
                            rep_sb[half][:], p_x[0:64, :]),
                            waits=[(T, t_rep)], inc=V)
                        v_rep = V.n
                        px_free_V = v_rep
                        E("vector", lambda v, half=half: v.tensor_tensor(
                            astg[half][:], p_av[half][0:64, :], rep_sb[half][:],
                            ALU.mult),
                            waits=[(V, v_rep), (LD, LD.n)], inc=V)
                        av_free_v[half] = V.n
                        E("sync", lambda s, h=h, sw=sw, half=half: s.dma_start(
                            out=sc_at[64 * h:64 * h + 64,
                                      sw * 1024 + half * 512:
                                      sw * 1024 + half * 512 + 512],
                            in_=astg[half][:]),
                            waits=[(V, av_free_v[half])], inc=LD, amt=16)
                        astg_ld[half] = LD.n
            buf_free_t[buf] = T.n

        barrierB = [(T, T.n), (V, V.n), (S, S.n), (LD, LD.n), (GD, GD.n)] + [(c, c.n) for c in KV]
        full_barrier(barrierB)

        if 'C' not in stages:
            for o, t in dbg_outs:
                E("sync", lambda s, o=o, t=t: s.dma_start(out=o, in_=t),
                  waits=[(LD, LD.n), (GD, GD.n)] + barrierB, inc=LD, amt=16)
            E("sync", lambda s: s.nop(),
              waits=[(LD, LD.n), (GD, GD.n)] + barrierB)
            block = En(nc.Block())
            P.realize(block)
            return nc
        reg.reset()
        reg.off += 8 * 1024 * 2
        wproj_bf2 = wproj_bf
        attnc = [reg.alloc(f"attnc{i}", [128, 8 * 512], BF16) for i in range(2)]
        projT = reg.alloc("projT", [128, 8 * 512], F32)
        xq_t = [reg.alloc(f"xq_t{i}", [128, D], F32) for i in range(2)]
        l1_t = reg.alloc("l1_t", [128, D], F32)
        x2stg = [reg.alloc(f"x2stg{i}", [128, D], F32) for i in range(2)]

        p_p = [psum_at(f"p_p{i}", [128, 512], F32, i) for i in range(2)]
        p_t2 = [psum_at(f"p_t2{i}", [128, 512], F32, 2 + i) for i in range(2)]

        at_ld = [0, 0]
        at_mm_t = [0, 0]
        projT_s = 0
        projT_tr_t = 0
        pp_free_S = [0, 0]
        pt2_free_V = [0, 0]
        xqt_free_V = [0, 0]
        x2stg_free_LD = [0, 0]
        for tci in range(4):
            cbuf = tci % 2
            for dc in range(8):
                E("sync", lambda s, cbuf=cbuf, tci=tci, dc=dc: s.dma_start(
                    out=attnc[cbuf][:, dc * 512:(dc + 1) * 512],
                    in_=sc_at[dc * 128:(dc + 1) * 128, tci * 512:(tci + 1) * 512]),
                    waits=(barrierB + [(T, at_mm_t[cbuf])]) if dc == 0 else [],
                    inc=LD, amt=16)
            at_ld[cbuf] = LD.n
            for oc in range(8):
                pb = oc % 2
                for dc in range(8):
                    E("tensor", lambda t, cbuf=cbuf, oc=oc, dc=dc, pb=pb: t.matmul(
                        p_p[pb][:],
                        wproj_bf2[:, dc * 1024 + oc * 128: dc * 1024 + oc * 128 + 128],
                        attnc[cbuf][:, dc * 512:(dc + 1) * 512],
                        start=(dc == 0), stop=(dc == 7)),
                        waits=([(LD, at_ld[cbuf]), (GD, wproj_gd),
                                (S, pp_free_S[pb]), (S, projT_s)] if dc == 0 else []),
                        inc=T if dc == 7 else None)
                t_p = T.n
                E("scalar", lambda sc, oc=oc, pb=pb: sc.activation(
                    projT[:, oc * 512:(oc + 1) * 512], p_p[pb][:], AF.Identity,
                    bias=bpr_sb[:, oc:oc + 1]),
                    waits=[(T, t_p), (T, projT_tr_t)], inc=S)
                pp_free_S[pb] = S.n
            at_mm_t[cbuf] = T.n
            projT_s = S.n
            for sub in range(4):
                g = tci * 4 + sub
                xb = xq_t[g % 2]
                E("sync", lambda s, g=g, xb=xb: s.dma_start(
                    out=xb[:], in_=xq[g * 128:(g + 1) * 128, :]),
                    waits=[(V, xqt_free_V[g % 2])], inc=XQ, amt=16)
                xq_ld = XQ.n
                E("vector", lambda v, g=g, xb=xb: v.tensor_scalar(
                    l1_t[:], xb[:], mu1q[:, g:g + 1], rs1q[:, g:g + 1],
                    ALU.subtract, ALU.mult),
                    waits=[(XQ, xq_ld)], inc=V)
                v_l1 = V.n
                E("vector", lambda v, g=g, xb=xb: v.tensor_tensor(
                    x2stg[g % 2][:], xb[:], l1_t[:], ALU.add),
                    waits=[(V, v_l1), (LD, LD.n)], inc=V)
                v_x2a = V.n
                for oc in range(8):
                    pb2 = oc % 2
                    E("tensor", lambda t, sub=sub, oc=oc, pb2=pb2: t.transpose(
                        p_t2[pb2][:, 0:128],
                        projT[:, oc * 512 + sub * 128: oc * 512 + sub * 128 + 128],
                        id32[:]),
                        waits=([(S, projT_s)] if (sub == 0 and oc == 0) else []) +
                              [(V, pt2_free_V[pb2])],
                        inc=T)
                    t_tr2 = T.n
                    E("vector", lambda v, g=g, oc=oc, pb2=pb2: v.tensor_tensor(
                        x2stg[g % 2][:, oc * 128:(oc + 1) * 128],
                        x2stg[g % 2][:, oc * 128:(oc + 1) * 128],
                        p_t2[pb2][:, 0:128], ALU.add),
                        waits=[(T, t_tr2), (V, v_x2a)], inc=V)
                    pt2_free_V[pb2] = V.n
                projT_tr_t = T.n
                v_x2 = V.n
                xqt_free_V[g % 2] = v_x2
                E("sync", lambda s, g=g: s.dma_start(
                    out=sc_x2[g * 128:(g + 1) * 128, :], in_=x2stg[g % 2][:]),
                    waits=[(V, v_x2)], inc=LD, amt=16)
                x2stg_free_LD[g % 2] = LD.n

        barrierC = [(T, T.n), (V, V.n), (S, S.n), (LD, LD.n), (GD, GD.n), (XQ, XQ.n)]
        full_barrier(barrierC)

        if 'D' not in stages:
            for o, t in dbg_outs:
                E("sync", lambda s, o=o, t=t: s.dma_start(out=o, in_=t),
                  waits=[(LD, LD.n), (GD, GD.n)] + barrierC, inc=LD, amt=16)
            E("sync", lambda s: s.nop(),
              waits=[(LD, LD.n), (GD, GD.n)] + barrierC)
            block = En(nc.Block())
            P.realize(block)
            return nc
        reg.reset()
        x2til = [reg.alloc(f"x2til{i}", [128, D], F32) for i in range(2)]
        x2ln = [reg.alloc("x2ln0", [128, D], F32)] * 2
        hs_t = reg.alloc("hs_t", [128, D], F32)
        hsTb = reg.alloc("hsTb", [128, 8 * 512], BF16)
        hsTf = [reg.alloc(f"hsTf{i}", [128, 128], F32) for i in range(2)]
        base_sb = reg.alloc("base_sb", [128, 32 * 512], BF16)
        hsum = reg.alloc("hsum", [128, 32 * 512], BF16)
        gt = reg.alloc("gt", [128, 512], BF16)
        wiT_c = [reg.alloc(f"wiT_c{i}", [128, 8 * 128], BF16) for i in range(2)]
        woT_c = [reg.alloc(f"woT_c{i}", [128, 32 * 128], BF16) for i in range(2)]
        lbT_c = [reg.alloc(f"lbT_c{i}", [16, DFF], BF16) for i in range(2)]
        a_sb = reg.alloc("a_sb", [16, 8 * 512], BF16)
        grep = [reg.alloc(f"grep{i}", [128, 512], F32) for i in range(2)]
        oacc = reg.alloc("oacc", [128, 8 * 512], F32)
        t3 = reg.alloc("t3", [128, 512], F32)
        lg_sb = reg.alloc("lg_sb", [128, 8], F32)
        pex = reg.alloc("pex", [128, 8], F32)
        pmask = reg.alloc("pmask", [128, 8], F32)
        gates_tok = reg.alloc("gates_tok", [128, 64], F32)
        gatesT = reg.alloc("gatesT", [8, 512], F32)
        vs2 = reg.alloc("vs2", [128, 4], F32)

        p_b = [psum_at(f"p_b{i}", [128, 512], F32, i) for i in range(2)]
        p_u = [psum_at(f"p_u{i}", [128, 512], F32, 2 + i) for i in range(2)]
        p_w = [psum_at(f"p_w{i}", [128, 512], F32, 4 + i) for i in range(2)]
        p_lg = psum_at("p_lg", [128, 512], F32, 6)
        p_d = psum_at("p_d", [128, 512], F32, 7)

        def gc(c):
            return gst[:, c:c + 1]

        pd_free = {"V": 0, "S": 0}

        x2t_free = [[], []]
        x2e_free = [[], []]
        pb_free_S = [0, 0]
        pu_free_V = [0, 0]
        pu_free_S = [0, 0]
        pw_free_V = [0, 0]
        wiT_free_T = [0, 0]
        woT_free_T = [0, 0]
        lbT_free_T = [0, 0]
        hsum_read_T = 0
        hsTf_read_T = [0, 0]
        plg_read_V = [0, 0, 0, 0]
        base_read_V = 0
        base_evac_S = [0] * 32
        hs_read_T = 0
        gatesT_read_T = 0
        gates_tok_read_T = 0
        prev_s_ex = 0
        prev_gate_V = 0
        x3_free_LD = 0

        for tci in range(4):
            s_gT_list = []
            for sub in range(4):
                g = tci * 4 + sub
                xb = x2ln[g % 2]
                E("sync", lambda s, g=g, xb=xb: s.dma_start(
                    out=xb[:], in_=sc_x2[g * 128:(g + 1) * 128, :]),
                    waits=(barrierC if g == 0 else []) + x2t_free[0] + x2t_free[1],
                    inc=XT2[g % 2], amt=16)
                x2_ld = XT2[g % 2].n
                E("scalar", lambda sc, xb=xb, g=g: sc.activation(
                    dmy[:], xb[:], AF.Copy, accum_out=gc(0)),
                    waits=[(XT2[g % 2], x2_ld), (V, prev_gate_V)], inc=S)
                s_s = S.n
                E("scalar", lambda sc, xb=xb, g=g: sc.activation(
                    dmy[:], xb[:], AF.Square, accum_out=gc(2)),
                    waits=[(S, S.n)], inc=S)
                s_sq2 = S.n
                E("vector", lambda v: v.tensor_scalar(
                    gc(1), gc(0), 1.0 / 1024.0, None, ALU.mult),
                    waits=[(S, s_s), (V, V.n)], inc=V)
                E("vector", lambda v: v.tensor_tensor(
                    gc(5), gc(1), gc(1), ALU.mult), waits=[(V, V.n)], inc=V)
                E("vector", lambda v: v.tensor_scalar(
                    gc(3), gc(2), 1.0 / 1024.0, gc(5), ALU.mult, ALU.subtract),
                    waits=[(V, V.n), (S, s_sq2)], inc=V)
                v_var = V.n
                E("scalar", lambda sc: sc.activation(
                    gc(4), gc(3), AF.Ln, bias=eps_sb[:]),
                    waits=[(V, v_var)], inc=S)
                E("scalar", lambda sc: sc.activation(
                    gc(4), gc(4), AF.Exp, scale=-0.5), waits=[(S, S.n)], inc=S)
                s_rs = S.n
                E("vector", lambda v, xb=xb: v.tensor_scalar(
                    hs_t[:], xb[:], gc(1), gc(4), ALU.subtract, ALU.mult),
                    waits=[(S, s_rs), (T, hs_read_T), (V, V.n)], inc=V)
                v_hs = V.n
                x2t_free[g % 2] = [(V, v_hs), (S, s_sq2)]
                for dc in range(8):
                    pf = dc % 2
                    E("tensor", lambda t, dc=dc, pf=pf: t.transpose(
                        p_u[pf][:, 0:128], hs_t[:, dc * 128:(dc + 1) * 128],
                        id32[:]),
                        waits=([(V, v_hs)] if dc == 0 else []) +
                              [(V, pu_free_V[pf]), (S, pu_free_S[pf])], inc=T)
                    t_t = T.n
                    E("vector", lambda v, dc=dc, sub=sub, pf=pf: v.tensor_copy(
                        hsTb[:, dc * 512 + sub * 128: dc * 512 + sub * 128 + 128],
                        p_u[pf][:, 0:128]),
                        waits=[(T, t_t)], inc=V)
                    pu_free_V[pf] = V.n
                    E("vector", lambda v, pf=pf: v.tensor_copy(
                        hsTf[pf][:], p_u[pf][:, 0:128]),
                        waits=[(T, t_t), (T, hsTf_read_T[pf])], inc=V)
                    pu_free_V[pf] = V.n
                    s_hf = V.n
                    E("tensor", lambda t, sub=sub, dc=dc, pf=pf: t.matmul(
                        p_lg[:, sub * 8:(sub + 1) * 8], hsTf[pf][:],
                        gw_sb[:, dc * 8:(dc + 1) * 8],
                        start=(dc == 0), stop=(dc == 7), skip_group_check=True),
                        waits=[(V, s_hf)] +
                              ([(V, plg_read_V[sub])] if dc == 0 else []),
                        inc=T if dc == 7 else None)
                    hsTf_read_T[pf] = T.n
                hs_read_T = T.n
                t_lg = T.n
                E("vector", lambda v, sub=sub: v.tensor_tensor(
                    lg_sb[:], p_lg[:, sub * 8:(sub + 1) * 8], gbr_sb[:], ALU.add),
                    waits=[(T, t_lg), (S, prev_s_ex)], inc=V)
                plg_read_V[sub] = V.n
                E("vector", lambda v: v.reduce_max(
                    gc(5), lg_sb[:], axis=AX.X, negate=True),
                    waits=[(V, V.n)], inc=V)
                v_m1 = V.n
                E("scalar", lambda sc: sc.activation(
                    pex[:], lg_sb[:], AF.Exp, bias=gc(5)),
                    waits=[(V, v_m1)], inc=S)
                s_ex = S.n
                prev_s_ex = s_ex
                E("vector", lambda v: v.reduce_sum(gc(7), pex[:], axis=AX.X),
                  waits=[(S, s_ex), (V, V.n)], inc=V)
                v_ss = V.n
                E("vector", lambda v: v.reciprocal(gc(8), gc(7)),
                  waits=[(V, v_ss), (V, V.n)], inc=V)
                v_rsum = V.n
                E("vector", lambda v: v.tensor_scalar(
                    pex[:], pex[:], gc(8), None, ALU.mult),
                    waits=[(V, v_rsum), (V, V.n)], inc=V)
                v_p = V.n
                E("vector", lambda v: v.reduce_max(gc(6), pex[:], axis=AX.X),
                  waits=[(V, v_p), (V, V.n)], inc=V)
                v_pm1 = V.n
                E("vector", lambda v: v.tensor_scalar(
                    pmask[:], pex[:], gc(6), None, ALU.is_lt),
                    waits=[(V, v_pm1), (V, V.n)], inc=V)
                E("vector", lambda v: v.tensor_tensor(
                    pmask[:], pmask[:], pex[:], ALU.mult),
                    waits=[(V, V.n)], inc=V)
                E("vector", lambda v: v.reduce_max(gc(9), pmask[:], axis=AX.X),
                  waits=[(V, V.n)], inc=V)
                v_m2 = V.n
                E("vector", lambda v: v.tensor_tensor(
                    gc(10), gc(6), gc(9), ALU.add),
                    waits=[(V, v_m2), (V, V.n)], inc=V)
                v_vs = V.n
                E("vector", lambda v, sub=sub: v.tensor_scalar(
                    vs2[:, sub:sub + 1], gc(10), 2.0, None, ALU.mult),
                    waits=[(V, v_vs), (V, V.n)], inc=V)
                E("vector", lambda v: v.tensor_scalar(
                    gates_tok[:, 0:8], pex[:], gc(9), None, ALU.is_ge),
                    waits=[(V, v_m2), (T, gates_tok_read_T), (V, V.n)], inc=V)
                E("vector", lambda v: v.tensor_scalar(
                    gates_tok[:, 0:8], gates_tok[:, 0:8], gc(10), None, ALU.mult),
                    waits=[(V, v_vs), (V, V.n)], inc=V)
                v_g = V.n
                prev_gate_V = v_g
                E("tensor", lambda t: t.transpose(
                    p_d[0:8, 0:128], gates_tok[:, 0:8], id32[:]),
                    waits=[(V, v_g), (V, pd_free["V"]), (S, pd_free["S"])], inc=T)
                t_gT = T.n
                gates_tok_read_T = t_gT
                E("vector", lambda v, sub=sub: v.tensor_copy(
                    gatesT[:, sub * 128:(sub + 1) * 128], p_d[0:8, 0:128]),
                    waits=[(T, t_gT), (T, gatesT_read_T)], inc=V)
                pd_free["V"] = V.n
                s_gT_list.append(V.n)
            s_gatesT = s_gT_list[-1] if s_gT_list else 0
            hsTb_done_V = V.n

            for fcc in range(32):
                wb = fcc % 2
                E("gpsimd", lambda g, fcc=fcc, wb=wb: g.dma_start(
                    out=wiT_c[wb][:].rearrange("p (a b) -> p a b", a=8, b=128),
                    in_=wiT_h[fcc]),
                    waits=[(T, wiT_free_T[wb])], inc=WI2[wb], amt=16)
                g_wi = WI2[wb].n
                for dc in range(8):
                    E("tensor", lambda t, fcc=fcc, dc=dc, wb=wb: t.matmul(
                        p_b[wb][:], wiT_c[wb][:, dc * 128:(dc + 1) * 128],
                        hsTb[:, dc * 512:(dc + 1) * 512],
                        start=(dc == 0), stop=(dc == 7)),
                        waits=([(WI2[wb], g_wi), (S, pb_free_S[wb]),
                                (V, base_read_V), (V, hsTb_done_V)] if dc == 0 else []),
                        inc=T if dc == 7 else None)
                wiT_free_T[wb] = T.n
                t_b = T.n
                E("scalar", lambda sc, fcc=fcc, wb=wb: sc.activation(
                    base_sb[:, fcc * 512:(fcc + 1) * 512], p_b[wb][:], AF.Identity,
                    bias=bi_sb[:, fcc:fcc + 1]),
                    waits=[(T, t_b)], inc=S)
                pb_free_S[wb] = S.n
                base_evac_S[fcc] = S.n
            s_base = S.n

            pd_aV = pd_free["V"]
            for e in range(8):
                for dc in range(8):
                    E("tensor", lambda t, e=e, dc=dc: t.matmul(
                        p_d[0:16, :],
                        laT[:, (dc * 8 + e) * 16:(dc * 8 + e) * 16 + 16],
                        hsTb[:, dc * 512:(dc + 1) * 512],
                        start=(dc == 0), stop=(dc == 7)),
                        waits=([(GD, GD.n), (V, s_gatesT), (V, hsTb_done_V),
                                (V, pd_aV)] if dc == 0 else []),
                        inc=T if dc == 7 else None)
                t_a = T.n
                E("vector", lambda v, e=e: v.tensor_copy(
                    a_sb[:, e * 512:(e + 1) * 512], p_d[0:16, :]),
                    waits=[(T, t_a)], inc=V)
                pd_aV = V.n
            v_a_all = V.n
            pd_free["V"] = V.n
            hsT_read = T.n

            def emit_grep(e):
                E("tensor", lambda t, e=e: t.matmul(
                    p_d[:, :], sel_sb[:, e * 128:(e + 1) * 128], gatesT[:],
                    start=True, stop=True),
                    waits=[(V, s_gatesT), (V, pd_free["V"]),
                           (S, pd_free["S"])], inc=T)
                t_g = T.n
                E("vector", lambda v, e=e: v.tensor_copy(
                    grep[e % 2][:], p_d[:, :]),
                    waits=[(T, t_g)], inc=V)
                pd_free["V"] = V.n
                return V.n, t_g

            grep_done_V = [0] * 8
            g_lb = [0, 0]

            def lbT_load(e):
                eb = e % 2
                E("gpsimd", lambda gg, e=e, eb=eb: gg.dma_start(
                    out=lbT_c[eb][:], in_=lbT_h[e]),
                    waits=[(T, lbT_free_T[eb])], inc=LB2[eb], amt=16)
                g_lb[eb] = LB2[eb].n

            lbT_load(0)
            for e in range(8):
                grep_done_V[e], gatesT_read_T = emit_grep(e)
                eb = e % 2
                if e < 7:
                    lbT_load(e + 1)
                for fcc in range(32):
                    ub = fcc % 2
                    w0 = [(V, pu_free_V[ub]), (S, pu_free_S[ub])]
                    if fcc < 2:
                        w0 += [(LB2[eb], g_lb[eb]), (V, v_a_all)]
                        if e == 0:
                            w0 += [(T, hsum_read_T)]
                    E("tensor", lambda t, e=e, fcc=fcc, ub=ub, eb=eb: t.matmul(
                        p_u[ub][:], lbT_c[eb][:, fcc * 128:(fcc + 1) * 128],
                        a_sb[:, e * 512:(e + 1) * 512],
                        start=True, stop=False),
                        waits=w0, inc=None)
                    E("tensor", lambda t, fcc=fcc, ub=ub: t.matmul(
                        p_u[ub][:], idbf[:],
                        base_sb[:, fcc * 512:(fcc + 1) * 512],
                        start=False, stop=True),
                        waits=[(S, base_evac_S[fcc])] if e == 0 else [],
                        inc=T)
                    t_u = T.n
                    lbT_free_T[eb] = t_u
                    if e == 0:
                        E("vector", lambda v, fcc=fcc, ub=ub, e=e: v.scalar_tensor_tensor(
                            hsum[:, fcc * 512:(fcc + 1) * 512], p_u[ub][:],
                            0.0, grep[e % 2][:], ALU.max, ALU.mult),
                            waits=[(T, t_u), (V, grep_done_V[e])], inc=V)
                    else:
                        E("vector", lambda v, fcc=fcc, ub=ub, e=e: v.scalar_tensor_tensor(
                            gt[:], p_u[ub][:],
                            0.0, grep[e % 2][:], ALU.max, ALU.mult),
                            waits=[(T, t_u), (V, V.n)] +
                                  ([(V, grep_done_V[e])] if fcc == 0 else []),
                            inc=V)
                        E("vector", lambda v, fcc=fcc: v.tensor_tensor(
                            hsum[:, fcc * 512:(fcc + 1) * 512],
                            hsum[:, fcc * 512:(fcc + 1) * 512], gt[:], ALU.add),
                            waits=[(V, V.n)], inc=V)
                    pu_free_V[ub] = V.n
            hsum_done_V = V.n

            for dc in range(8):
                wob = dc % 2
                E("sync", lambda s, dc=dc, wob=wob: s.dma_start(
                    out=woT_c[wob][:].rearrange("p (a b) -> p a b",
                                                a=32, b=128),
                    in_=sc_wo[dc]),
                    waits=[(T, woT_free_T[wob])], inc=WO2[wob], amt=16)
                wo_ld = WO2[wob].n
                for fcc in range(32):
                    E("tensor", lambda t, dc=dc, fcc=fcc, wob=wob: t.matmul(
                        p_w[wob][:],
                        woT_c[wob][:, fcc * 128:(fcc + 1) * 128],
                        hsum[:, fcc * 512:(fcc + 1) * 512],
                        start=(fcc == 0), stop=(fcc == 31)),
                        waits=([(V, hsum_done_V), (WO2[wob], wo_ld),
                                (V, pw_free_V[wob])] if fcc == 0 else []),
                        inc=T if fcc == 31 else None)
                woT_free_T[wob] = T.n
                t_w = T.n
                E("vector", lambda v, dc=dc, wob=wob: v.tensor_copy(
                    oacc[:, dc * 512:(dc + 1) * 512], p_w[wob][:]),
                    waits=[(T, t_w)], inc=V)
                pw_free_V[wob] = V.n
            hsum_read_T = T.n
            v_oacc = V.n
            base_read_V = V.n

            oacc_read_T = 0
            for sub in range(4):
                g = tci * 4 + sub
                xb = x2til[g % 2]
                E("sync", lambda s, g=g, xb=xb: s.dma_start(
                    out=xb[:], in_=sc_x2[g * 128:(g + 1) * 128, :]),
                    waits=x2e_free[g % 2] + [(OS, OS.n)], inc=XE2[g % 2], amt=16)
                x2_ld2 = XE2[g % 2].n
                for dc in range(8):
                    pe = dc % 2
                    E("tensor", lambda t, sub=sub, dc=dc, pe=pe: t.transpose(
                        p_u[pe][:, 0:128],
                        oacc[:, dc * 512 + sub * 128: dc * 512 + sub * 128 + 128],
                        id32[:]),
                        waits=([(V, v_oacc)] if (sub == 0 and dc == 0) else []) +
                              [(V, pu_free_V[pe]), (S, pu_free_S[pe])],
                        inc=T)
                    t_e = T.n
                    E("vector", lambda v, g=g, dc=dc, pe=pe, xb=xb: v.tensor_tensor(
                        xb[:, dc * 128:(dc + 1) * 128],
                        xb[:, dc * 128:(dc + 1) * 128],
                        p_u[pe][:, 0:128], ALU.add),
                        waits=[(T, t_e), (XE2[g % 2], x2_ld2)],
                        inc=V)
                    pu_free_V[pe] = V.n
                v_e1 = V.n
                E("vector", lambda v, sub=sub: v.tensor_scalar(
                    t3[:, 0:512], bo_rep[:, 0:512], vs2[:, sub:sub + 1], None,
                    ALU.mult),
                    waits=[(V, v_e1), (V, V.n)], inc=V)
                E("vector", lambda v, sub=sub, xb=xb: v.tensor_tensor(
                    xb[:, 0:512], xb[:, 0:512], t3[:, 0:512], ALU.add),
                    waits=[(V, V.n)], inc=V)
                E("vector", lambda v, sub=sub: v.tensor_scalar(
                    t3[:, 0:512], bo_rep[:, 512:1024], vs2[:, sub:sub + 1], None,
                    ALU.mult),
                    waits=[(V, V.n)], inc=V)
                E("vector", lambda v, sub=sub, xb=xb: v.tensor_tensor(
                    xb[:, 512:1024], xb[:, 512:1024], t3[:, 0:512], ALU.add),
                    waits=[(V, V.n)], inc=V)
                v_e2 = V.n
                E("sync", lambda s, tci=tci, sub=sub, xb=xb: s.dma_start(
                    out=out[tci * 512 + sub * 128: tci * 512 + sub * 128 + 128, :],
                    in_=xb[:]),
                    waits=[(V, v_e2)], inc=OS, amt=16)
                x2e_free[g % 2] = [(V, v_e2)]
                x3_free_LD = OS.n

        if dbg == 2:
            for nm, t, shp in [("gatesT", gatesT, [8, 512]), ("gst", gst, [128, 16]),
                               ("lg", lg_sb, [128, 8]), ("pex", pex, [128, 8]),
                               ("vs2", vs2, [128, 4]), ("pmask", pmask, [128, 8])]:
                o = nc.dram_tensor("dbg3_" + nm, shp, t.dtype, kind="ExternalOutput").ap()
                E("sync", lambda s, o=o, t=t: s.dma_start(out=o, in_=t[:]),
                  waits=[(c, c.n) for c in [LD, OS]] + [(T, T.n), (V, V.n), (S, S.n)],
                  inc=LD, amt=16)
        if dbg == 1:
            dmap = [("base", base_sb, [128, 32 * 512]), ("hsum", hsum, [128, 32 * 512]),
                    ("asb", a_sb, [16, 8 * 512]),
                    ("gatesT", gatesT, [8, 512]), ("grep0", grep[0], [128, 512]),
                    ("grep1", grep[1], [128, 512]), ("oacc", oacc, [128, 8 * 512]),
                    ("vs2", vs2, [128, 4]), ("hsTb", hsTb, [128, 8 * 512])]
            for nm, t, shp in dmap:
                o = nc.dram_tensor("dbg2_" + nm, shp, t.dtype, kind="ExternalOutput").ap()
                E("sync", lambda s, o=o, t=t: s.dma_start(out=o, in_=t[:]),
                  waits=[(c, c.n) for c in [LD, OS]] + [(T, T.n), (V, V.n), (S, S.n)],
                  inc=LD, amt=16)
        for o, t in dbg_outs:
            E("sync", lambda s, o=o, t=t: s.dma_start(out=o, in_=t),
              waits=[(LD, LD.n)], inc=LD, amt=16)

        allsems = [LD, GD, OS, XQ, VST, SST] + LB2 + WS2 + KV + XD3 + WO2 + XT2 + WI2 + XE2
        E("sync", lambda s: s.nop(), waits=[(c, c.n) for c in allsems])

        block = En(nc.Block())
        P.realize(block)

    return nc


def _layout(inputs):
    w_qkv = np.asarray(inputs["L0_attn_w_qkv"], dtype=np.float32)
    perm = []
    for t in range(3):
        for h in range(16):
            for d in range(64):
                perm.append(192 * h + 3 * d + t)
    wqkv_perm = np.ascontiguousarray(w_qkv[:, perm])
    wqkv_h = np.ascontiguousarray(wqkv_perm.reshape(8, 128, 3072).transpose(1, 0, 2))

    gamma_c = np.ascontiguousarray(
        np.asarray(inputs["L0_attn_ln_scale"], np.float32).reshape(8, 128).T)
    beta_c = np.ascontiguousarray(
        np.asarray(inputs["L0_attn_ln_bias"], np.float32).reshape(8, 128).T)
    wproj_h = np.ascontiguousarray(
        np.asarray(inputs["L0_attn_w_proj"], np.float32).reshape(8, 128, 1024)
        .transpose(1, 0, 2))
    bproj_c = np.ascontiguousarray(
        np.asarray(inputs["L0_attn_b_proj"], np.float32).reshape(8, 128).T)

    wi = np.asarray(inputs["L0_ffn_wi"], np.float32)
    wiT_h = np.ascontiguousarray(wi.reshape(32, 128, 8, 128).transpose(0, 3, 2, 1))
    bi_c = np.ascontiguousarray(
        np.asarray(inputs["L0_ffn_bi"], np.float32).reshape(32, 128).T)
    wo = np.asarray(inputs["L0_ffn_wo"], np.float32)
    woT_h = np.ascontiguousarray(wo.reshape(8, 128, 32, 128).transpose(0, 3, 2, 1))

    laT_h = np.ascontiguousarray(
        np.asarray(inputs["L0_ffn_lora_A"], np.float32).transpose(2, 0, 1)
        .reshape(8, 128, 8, 16).transpose(1, 0, 2, 3))
    lbT_h = np.ascontiguousarray(
        np.asarray(inputs["L0_ffn_lora_B"], np.float32).transpose(0, 2, 1))
    gwT_h = np.ascontiguousarray(
        np.asarray(inputs["L0_ffn_gate_w"], np.float32).T.reshape(8, 128, 8)
        .transpose(1, 0, 2))
    gb_row = np.ascontiguousarray(
        np.asarray(inputs["L0_ffn_gate_b"], np.float32).reshape(1, 8))
    bo_row = np.ascontiguousarray(
        np.asarray(inputs["L0_ffn_bo"], np.float32).reshape(1, D))

    return dict(wqkv_h=wqkv_h, gamma_c=gamma_c, beta_c=beta_c, wproj_h=wproj_h,
                bproj_c=bproj_c, wiT_h=wiT_h, bi_c=bi_c, woT_h=woT_h, laT_h=laT_h,
                lbT_h=lbT_h, gwT_h=gwT_h, gb_row=gb_row, bo_row=bo_row)


def kernel(**inputs):
    x = np.asarray(inputs["L0_attn_z"], dtype=np.float32)
    shared = _layout(inputs)
    nc = build_nc()
    in_maps = []
    for c in range(8):
        b = c // 4
        qoff = (c % 4) * NQ
        m = dict(shared)
        m["x_b"] = np.ascontiguousarray(x[b])
        m["xq"] = np.ascontiguousarray(x[b, qoff:qoff + NQ])
        in_maps.append(m)
    res = run_bass_kernel_spmd(nc, in_maps, core_ids=list(range(8)))
    outp = np.empty((2, NB, D), dtype=np.float32)
    for c in range(8):
        b = c // 4
        qoff = (c % 4) * NQ
        outp[b, qoff:qoff + NQ] = res.results[c]["out"]
    return outp


# version 14 (final)
# speedup vs baseline: 1.0159x; 1.0106x over previous
import contextlib
import numpy as np
import ml_dtypes

import concourse.bass as bass
import concourse.mybir as mybir
from concourse.bass_utils import run_bass_kernel_spmd

BFNP = ml_dtypes.bfloat16
F32 = mybir.dt.float32
BF16 = mybir.dt.bfloat16
AF = mybir.ActivationFunctionType
ALU = mybir.AluOpType
AX = mybir.AxisListType

D = 1024
NB = 8192
NQ = 2048
DFF = 4096

PERS_BYTES = 22 * 1024
SBUF_REGION = 185 * 1024


class Ctr:
    def __init__(self, handle):
        self.h = handle
        self.n = 0

    def inc(self, amt):
        self.n += amt
        return self.n


class Prog:
    ENGS = ("tensor", "vector", "scalar", "sync", "gpsimd")

    def __init__(self):
        self.stream = {e: [] for e in self.ENGS}

    def emit(self, eng, fn, waits=(), inc=None, amt=1):
        if inc is not None:
            inc.inc(amt)
        self.stream[eng].append(
            (fn, [w for w in waits if w is not None and w[1]], inc, amt))

    def realize(self, block):
        streams = self.stream

        def run(eng_name, eng):
            waited = {}
            for fn, waits, inc, amt in streams[eng_name]:
                for (ctr, val) in waits:
                    if waited.get(id(ctr), -1) >= val:
                        continue
                    eng.wait_ge(ctr.h, val)
                    waited[id(ctr)] = val
                inst = fn(eng)
                if inc is not None:
                    assert inst is not None, f"{eng_name} op with inc returned None"
                    inst.then_inc(inc.h, amt)

        @block.tensor
        def _(t):
            run("tensor", t)

        @block.vector
        def _(v):
            run("vector", v)

        @block.scalar
        def _(s):
            run("scalar", s)

        @block.sync
        def _(s):
            run("sync", s)

        @block.gpsimd
        def _(g):
            run("gpsimd", g)


class Arena:
    def __init__(self, nc, backing, base_off, limit):
        self.nc = nc
        self.backing = backing
        self.base = nc.lookup_mloc(backing).addr + base_off
        self.off = 0
        self.limit = limit
        self.uid = 0

    def alloc(self, name, shape, dtype):
        nbytes = int(np.prod(shape[1:])) * mybir.dt.size(dtype)
        nbytes = (nbytes + 63) // 64 * 64
        assert self.off + nbytes <= self.limit, \
            f"SBUF region overflow at {name}: {self.off + nbytes} > {self.limit}"
        h = self.nc.alloc_sbuf_tensor_at(
            f"{name}_u{self.uid}", shape, dtype, offset=self.base + self.off)
        self.uid += 1
        self.off += nbytes
        return h

    def reset(self):
        self.off = 0


def build_nc(dbg=False, stages='ABCD', dparts=None):
    nc = bass.Bass("TRN2", target_bir_lowering=False)

    def din(name, shape):
        return nc.dram_tensor(name, shape, F32, kind="ExternalInput").ap()

    x_b = din("x_b", [NB, D])
    xq = din("xq", [NQ, D])
    wqkv_h = din("wqkv_h", [128, 8, 3072])
    beta_c = din("beta_c", [128, 8])
    gamma_c = din("gamma_c", [128, 8])
    wproj_h = din("wproj_h", [128, 8, 1024])
    bproj_c = din("bproj_c", [128, 8])
    wiT_h = din("wiT_h", [32, 128, 8, 128])
    bi_c = din("bi_c", [128, 32])
    woT_h = din("woT_h", [8, 128, 32, 128])
    laT_h = din("laT_h", [128, 8, 8, 16])
    lbT_h = din("lbT_h", [8, 16, DFF])
    gwT_h = din("gwT_h", [128, 8, 8])
    gb_row = din("gb_row", [1, 8])
    bo_row = din("bo_row", [1, D])

    out = nc.dram_tensor("out", [NQ, D], F32, kind="ExternalOutput").ap()

    def scr(name, shape, dt):
        return nc.dram_tensor(name, shape, dt, kind="Internal").ap()

    sc_k = scr("sc_k", [D, NB], BF16)
    sc_v = scr("sc_v", [NB, 16 * 65], BF16)
    sc_q = scr("sc_q", [D, NQ], BF16)
    sc_at = scr("sc_at", [D, NQ], BF16)
    sc_x2 = scr("sc_x2", [NQ, D], F32)
    sc_c = scr("sc_c", [1, NQ], BF16)
    sc_wo = scr("sc_wo", [8, 128, DFF], BF16)

    dbg_outs = []
    if dbg:
        for nm, t in [("dk", sc_k), ("dv", sc_v), ("dq", sc_q), ("dat", sc_at),
                      ("dx2", sc_x2)]:
            o = nc.dram_tensor("dbg_" + nm, [int(s) for s in t.tensor.shape],
                               t.tensor.dtype, kind="ExternalOutput").ap()
            dbg_outs.append((o, t))

    ident_f32 = nc.inline_tensor(np.eye(128, dtype=np.float32), "c_idf")
    ident_bf = nc.inline_tensor(np.eye(128, dtype=BFNP), "c_idb")
    ones_c = nc.inline_tensor(np.ones((1, 128), dtype=np.float32), "c_ones")
    eps_c = nc.inline_tensor(np.full((128, 1), 1e-5, dtype=np.float32), "c_eps")
    selm = np.zeros((8, 8 * 128), dtype=np.float32)
    for e in range(8):
        selm[e, e * 128:(e + 1) * 128] = 1.0
    sel_c = nc.inline_tensor(selm, "c_sel")

    P = Prog()
    E = P.emit

    ctx = contextlib.ExitStack()
    with ctx:
        En = ctx.enter_context
        backing = En(nc.sbuf_tensor("arena", [128, PERS_BYTES + SBUF_REGION],
                                    mybir.dt.uint8))
        pers = Arena(nc, backing, 0, PERS_BYTES)
        reg = Arena(nc, backing, PERS_BYTES, SBUF_REGION)


        def full_barrier(waitlist):
            for eng in ("tensor", "vector", "scalar", "sync", "gpsimd"):
                E(eng, lambda e: e.nop(), waits=list(waitlist))
        _puid = [0]

        def psum_at(name, shape, dtype, bank):
            h = nc.virtual_psum_tensor(f"{name}_u{_puid[0]}", shape, dtype)
            _puid[0] += 1
            nc._tensor(h.name, h.shape, h.dtype, type="PSUM")
            mloc = nc.lookup_mloc(h)
            mloc.allocated = True
            mloc.addr = bank * 2048
            return h

        LD = Ctr(En(nc.semaphore("LD")))
        GD = Ctr(En(nc.semaphore("GD")))
        T = Ctr(En(nc.semaphore("T")))
        V = Ctr(En(nc.semaphore("V")))
        S = Ctr(En(nc.semaphore("S")))
        KV = [Ctr(En(nc.semaphore("KV0"))), Ctr(En(nc.semaphore("KV1")))]
        XD3 = [Ctr(En(nc.semaphore(f"XD{i}"))) for i in range(3)]
        XQ = Ctr(En(nc.semaphore("XQ")))
        OS = Ctr(En(nc.semaphore("OS")))
        WO2 = [Ctr(En(nc.semaphore(f"WO{i}"))) for i in range(2)]
        XT2 = [Ctr(En(nc.semaphore(f"XT{i}"))) for i in range(2)]
        WI2 = [Ctr(En(nc.semaphore(f"WI{i}"))) for i in range(2)]
        XE2 = [Ctr(En(nc.semaphore(f"XE{i}"))) for i in range(2)]
        LB2 = [Ctr(En(nc.semaphore(f"LB{i}"))) for i in range(2)]
        WS2 = [Ctr(En(nc.semaphore(f"WS{i}"))) for i in range(2)]
        VST = Ctr(En(nc.semaphore("VST")))
        SST = Ctr(En(nc.semaphore("SST")))
        XG3 = [Ctr(En(nc.semaphore(f"XG{i}"))) for i in range(3)]

        id32 = pers.alloc("id32", [128, 128], F32)
        idbf = pers.alloc("idbf", [128, 128], BF16)
        ones128 = pers.alloc("ones128", [2, 128], F32)
        ones65 = pers.alloc("ones65", [65, 64], F32)
        eps_sb = pers.alloc("eps_sb", [128, 1], F32)
        sel_sb = pers.alloc("sel_sb", [8, 8 * 128], F32)
        gam_sb = pers.alloc("gam_sb", [128, 8], F32)
        bet_sb = pers.alloc("bet_sb", [128, 64], F32)
        bpr_sb = pers.alloc("bpr_sb", [128, 8], F32)
        bi_sb = pers.alloc("bi_sb", [128, 32], F32)
        gw_sb = pers.alloc("gw_sb", [128, 64], F32)
        gbr_sb = pers.alloc("gbr_sb", [128, 8], F32)
        bo_rep = pers.alloc("bo_rep", [128, D], F32)
        qb_sb = pers.alloc("qb_sb", [128, 64], F32)
        qbq8 = pers.alloc("qbq8", [128, 8], F32)
        laT = pers.alloc("laT", [128, 1024], BF16)
        mu1q = pers.alloc("mu1q", [128, 16], F32)
        rs1q = pers.alloc("rs1q", [128, 16], F32)
        dmy = pers.alloc("dmy", [128, 1024], BF16)
        dmy2 = pers.alloc("dmy2", [128, 1024], BF16)
        st_sb = pers.alloc("st_sb", [128, 32], F32)
        gst = pers.alloc("gst", [128, 16], F32)
        gbrow = pers.alloc("gbrow", [2, 128], F32)
        borow = pers.alloc("borow", [2, D], F32)

        def ld(dst, src):
            E("sync", lambda s, d=dst, so=src: s.dma_start(out=d, in_=so),
              inc=LD, amt=16)

        ld(id32[:], ident_f32.ap())
        ld(idbf[:], ident_bf.ap())
        ld(ones128[0:1, :], ones_c.ap())
        ld(eps_sb[:], eps_c.ap())
        ld(sel_sb[:], sel_c.ap())
        ld(gam_sb[:], gamma_c)
        ld(bet_sb[:, 0:8], beta_c)
        ld(bpr_sb[:], bproj_c)
        ld(bi_sb[:], bi_c)
        ld(gw_sb[:], gwT_h.rearrange("p a b -> p (a b)"))
        ld(gbrow[0:1, 0:8], gb_row)
        ld(borow[0:1, :], bo_row)
        ld_pers = LD.n

        E("gpsimd", lambda g: g.dma_start(
            out=laT[:], in_=laT_h.rearrange("p a e r -> p (a e r)")),
            inc=GD, amt=16)
        laT_gd = GD.n

        E("vector", lambda v: v.memset(ones65[64:65, :], 1.0),
          waits=[(LD, ld_pers)], inc=V)
        ones65_v = V.n

        p_i0 = psum_at("p_i0", [128, 512], F32, 7)
        E("tensor", lambda t: t.matmul(p_i0[:, 0:8], ones128[0:1, :], gbrow[0:1, 0:8],
                                       start=True, stop=True),
          waits=[(LD, ld_pers)], inc=T)
        tv = T.n
        E("vector", lambda v: v.tensor_copy(gbr_sb[:], p_i0[:, 0:8]),
          waits=[(T, tv)], inc=V)
        vv = V.n
        for i in range(2):
            E("tensor", lambda t, i=i: t.matmul(
                p_i0[:], ones128[0:1, :], borow[0:1, i * 512:(i + 1) * 512],
                start=True, stop=True),
                waits=[(V, vv)], inc=T)
            tv = T.n
            E("vector", lambda v, i=i: v.tensor_copy(
                bo_rep[:, i * 512:(i + 1) * 512], p_i0[:]),
                waits=[(T, tv)], inc=V)
            vv = V.n

        reg.reset()
        wqkv_bf = reg.alloc("wqkv_bf", [128, 8 * 3072], BF16)
        stg_w = [reg.alloc(f"stg_w{i}", [128, 3072], F32) for i in range(2)]
        wostg = [reg.alloc(f"wostg{i}", [128, 4096], BF16) for i in range(2)]
        x_t = [reg.alloc(f"x_t{i}", [128, D], F32) for i in range(3)]
        zn_t = [reg.alloc(f"zn_t{i}", [128, D], BF16) for i in range(2)]
        znT = [reg.alloc(f"znT{i}", [128, 8 * 512], BF16) for i in range(2)]
        kstg = [reg.alloc(f"kstg{i}", [128, 512], BF16) for i in range(2)]
        vstg = [reg.alloc(f"vstg{i}", [128, 520], BF16) for i in range(2)]
        qstg = [reg.alloc(f"qstg{i}", [128, 512], BF16) for i in range(2)]
        vb8 = reg.alloc("vb8", [8, 128], F32)
        vbias_rep = reg.alloc("vbias_rep", [128, D], F32)

        p_qb = psum_at("p_qb", [128, 512], F32, 6)
        p_tr = [psum_at(f"p_tr{i}", [128, 128], F32, i) for i in range(2)]
        p_trb = [psum_at(f"p_trb{i}", [128, 128], BF16, i) for i in range(2)]
        p_k = [psum_at(f"p_k{i}", [128, 512], F32, 2 + i) for i in range(2)]
        p_v = [psum_at(f"p_v{i}", [128, 512], F32, 4 + i) for i in range(2)]

        for dc in range(8):
            wb = dc % 2
            W = WS2[wb]
            E("gpsimd", lambda g, dc=dc, wb=wb: g.dma_start(
                out=wostg[wb][:].rearrange("p (a c) -> p a c", a=32, c=128),
                in_=woT_h[dc]),
                waits=[(W, W.n)], inc=W, amt=16)
            E("gpsimd", lambda g, dc=dc, wb=wb: g.dma_start(
                out=sc_wo[dc],
                in_=wostg[wb][:].rearrange("p (a c) -> p a c", a=32, c=128)),
                waits=[(W, W.n)], inc=W, amt=16)

        stg_free_v = [0, 0]
        qb_t = 0
        qb_acc_v = 0
        for dc in range(8):
            b = dc % 2
            E("gpsimd", lambda g, b=b, dc=dc: g.dma_start(
                out=stg_w[b][:], in_=wqkv_h[:, dc, :]),
                waits=[(T, qb_t), (V, stg_free_v[b])],
                inc=GD, amt=16)
            wld = GD.n
            for cc in range(24):
                E("tensor", lambda t, b=b, cc=cc, dc=dc: t.matmul(
                    p_qb[:, cc:cc + 1], stg_w[b][:, cc * 128:(cc + 1) * 128],
                    bet_sb[:, dc:dc + 1], start=True, stop=True),
                    waits=([(GD, wld), (V, qb_acc_v)] if cc == 0 else []),
                    inc=T if cc == 23 else None)
            qb_t = T.n
            if dc == 0:
                E("vector", lambda v: v.tensor_copy(qb_sb[:, 0:24], p_qb[:, 0:24]),
                  waits=[(T, qb_t)], inc=V)
            else:
                E("vector", lambda v: v.tensor_tensor(
                    qb_sb[:, 0:24], qb_sb[:, 0:24], p_qb[:, 0:24], ALU.add),
                    waits=[(T, qb_t), (V, V.n)], inc=V)
            qb_acc_v = V.n
            E("vector", lambda v, b=b, dc=dc: v.tensor_scalar(
                wqkv_bf[:, dc * 3072:(dc + 1) * 3072], stg_w[b][:],
                gam_sb[:, dc:dc + 1], None, ALU.mult),
                waits=[(GD, wld), (T, qb_t)], inc=V)
            stg_free_v[b] = V.n

        qb_v = qb_acc_v
        E("vector", lambda v: v.tensor_scalar(
            qbq8[:], qb_sb[:, 0:8], 0.125, None, ALU.mult),
            waits=[(V, qb_v), (V, V.n)], inc=V)
        qbq8_v = V.n
        E("tensor", lambda t: t.transpose(p_tr[0][0:8, 0:128], qb_sb[:, 16:24], id32[:]),
          waits=[(V, qb_v)], inc=T)
        vb_t = T.n
        E("vector", lambda v: v.tensor_copy(vb8[:], p_tr[0][0:8, 0:128]),
          waits=[(T, vb_t)], inc=V)
        vb_v = V.n
        for i in range(8):
            E("tensor", lambda t, i=i: t.matmul(
                p_k[0][:, 0:128], sel_sb[:, i * 128:(i + 1) * 128], vb8[:],
                start=True, stop=True),
                waits=[(V, vb_v), (V, V.n)], inc=T)
            t_r = T.n
            E("vector", lambda v, i=i: v.tensor_copy(
                vbias_rep[:, i * 128:(i + 1) * 128], p_k[0][:, 0:128]),
                waits=[(T, t_r)], inc=V)
        vbias_v = V.n

        for i in range(2):
            E("vector", lambda v, i=i: v.memset(
                vstg[i][:].rearrange("p (c w) -> p c w", c=8, w=65)[:, :, 64:65],
                1.0), inc=V)
        vstg_ones_v = V.n

        def st(par, col):
            return st_sb[:, par * 16 + col: par * 16 + col + 1]

        x_free_S = [0, 0, 0]
        x_free_V = [0, 0, 0]
        zn_free_T = [0, 0]
        znT_mm_t = [0, 0]
        st_free_S = [0, 0]
        kst_ld = [0, 0]
        vst_ld = [0, 0]
        qst_ld = [0, 0]
        pk_free_S = [0, 0]
        pv_free_V = [vbias_v, 0]
        ptr_free_V = [0, 0]

        for tc in range(20):
            is_q = tc >= 16
            par_c = tc % 2
            znb = znT[par_c]
            for sub in range(4):
                g = tc * 4 + sub
                xb = x_t[g % 3]
                par = g % 2
                if is_q:
                    src = xq[(tc - 16) * 512 + sub * 128:
                             (tc - 16) * 512 + sub * 128 + 128, :]
                else:
                    src = x_b[tc * 512 + sub * 128: tc * 512 + sub * 128 + 128, :]
                xsem = XD3[g % 3] if tc < 10 else XG3[g % 3]
                E("sync" if tc < 10 else "gpsimd",
                  lambda s, xb=xb, src=src: s.dma_start(out=xb[:], in_=src),
                  waits=[(S, x_free_S[g % 3]), (V, x_free_V[g % 3])], inc=xsem, amt=16)
                x_ld = xsem.n
                E("scalar", lambda sc, xb=xb, par=par: sc.activation(
                    dmy[:], xb[:], AF.Copy, accum_out=st(par, 0)),
                    waits=[(xsem, x_ld), (V, st_free_S[par])], inc=S)
                s_sum = S.n
                E("scalar", lambda sc, xb=xb, par=par: sc.activation(
                    dmy[:], xb[:], AF.Square, accum_out=st(par, 4)),
                    waits=[(S, S.n)], inc=S)
                s_sq = S.n
                x_free_S[g % 3] = s_sq
                E("vector", lambda v, par=par: v.tensor_scalar(
                    st(par, 1), st(par, 0), 1.0 / 1024.0, None, ALU.mult),
                    waits=[(S, s_sum), (V, V.n)], inc=V)
                E("vector", lambda v, par=par: v.tensor_tensor(
                    st(par, 6), st(par, 1), st(par, 1), ALU.mult),
                    waits=[(V, V.n)], inc=V)
                E("vector", lambda v, par=par: v.tensor_scalar(
                    st(par, 5), st(par, 4), 1.0 / 1024.0, st(par, 6),
                    ALU.mult, ALU.subtract),
                    waits=[(V, V.n), (S, s_sq)], inc=V)
                v_v1 = V.n
                E("scalar", lambda sc, par=par: sc.activation(
                    st(par, 9), st(par, 5), AF.Ln, bias=eps_sb[:]),
                    waits=[(V, v_v1)], inc=S)
                E("scalar", lambda sc, par=par: sc.activation(
                    st(par, 9), st(par, 9), AF.Exp, scale=-0.5),
                    waits=[(S, S.n)], inc=S)
                s_r1 = S.n
                E("vector", lambda v, par=par: v.tensor_tensor(
                    st(par, 8), st(par, 9), st(par, 9), ALU.mult),
                    waits=[(S, s_r1), (V, V.n)], inc=V)
                E("vector", lambda v, par=par: v.tensor_tensor(
                    st(par, 8), st(par, 8), st(par, 5), ALU.mult),
                    waits=[(V, V.n)], inc=V)
                v_var2 = V.n
                E("scalar", lambda sc, par=par: sc.activation(
                    st(par, 10), st(par, 8), AF.Ln, bias=eps_sb[:]),
                    waits=[(V, v_var2)], inc=S)
                E("scalar", lambda sc, par=par: sc.activation(
                    st(par, 10), st(par, 10), AF.Exp, scale=-0.5),
                    waits=[(S, S.n)], inc=S)
                s_r2 = S.n
                E("vector", lambda v, par=par: v.tensor_tensor(
                    st(par, 11), st(par, 9), st(par, 10), ALU.mult),
                    waits=[(S, s_r2), (V, V.n)], inc=V)
                v_a = V.n
                if is_q:
                    qi = (tc - 16) * 4 + sub
                    E("vector", lambda v, par=par, qi=qi: v.tensor_copy(
                        mu1q[:, qi:qi + 1], st(par, 1)), waits=[(V, V.n)], inc=V)
                    E("vector", lambda v, par=par, qi=qi: v.tensor_copy(
                        rs1q[:, qi:qi + 1], st(par, 9)), inc=V)
                E("vector", lambda v, xb=xb, par=par: v.tensor_scalar(
                    zn_t[par][:], xb[:], st(par, 1), st(par, 11),
                    ALU.subtract, ALU.mult),
                    waits=[(V, v_a), (T, zn_free_T[par]), (V, V.n)], inc=V)
                v_zn = V.n
                x_free_V[g % 3] = v_zn
                st_free_S[par] = s_r2
                for dc in range(8):
                    pt = dc % 2
                    E("tensor", lambda t, par=par, dc=dc, pt=pt: t.transpose(
                        p_trb[pt][:], zn_t[par][:, dc * 128:(dc + 1) * 128], idbf[:]),
                        waits=[(V, v_zn), (T, znT_mm_t[par_c]), (V, ptr_free_V[pt])],
                        inc=T)
                    t_tr = T.n
                    E("vector", lambda v, par_c=par_c, dc=dc, sub=sub, pt=pt: v.tensor_copy(
                        znT[par_c][:, dc * 512 + sub * 128: dc * 512 + sub * 128 + 128],
                        p_trb[pt][:]),
                        waits=[(T, t_tr)], inc=V)
                    ptr_free_V[pt] = V.n
                zn_free_T[par] = T.n
            v_znT = V.n

            tok0 = (tc - 16) * 512 if is_q else tc * 512
            if not is_q:
                for cc in range(8):
                    pb = cc % 2
                    for dc in range(8):
                        E("tensor", lambda t, pb=pb, cc=cc, dc=dc, znb=znb: t.matmul(
                            p_k[pb][:],
                            wqkv_bf[:, dc * 3072 + 1024 + cc * 128:
                                    dc * 3072 + 1024 + cc * 128 + 128],
                            znb[:, dc * 512:(dc + 1) * 512],
                            start=(dc == 0), stop=(dc == 7)),
                            waits=([(V, v_znT), (S, pk_free_S[pb])] if dc == 0 else []),
                            inc=T if dc == 7 else None)
                    t_k = T.n
                    E("scalar", lambda sc, pb=pb, cc=cc: sc.activation(
                        kstg[pb][:], p_k[pb][:], AF.Identity,
                        bias=qb_sb[:, 8 + cc: 9 + cc]),
                        waits=[(T, t_k), (SST, SST.n), (V, qb_v)], inc=S)
                    pk_free_S[pb] = S.n
                    s_k = S.n
                    E("scalar", lambda s, pb=pb, cc=cc, tok0=tok0: s.dma_start(
                        out=sc_k[cc * 128:(cc + 1) * 128, tok0:tok0 + 512],
                        in_=kstg[pb][:]),
                        waits=[(S, s_k)], inc=SST, amt=16)
                    kst_ld[pb] = SST.n
                for sub in range(4):
                    for vcc in range(2):
                        pb = (sub * 2 + vcc) % 2
                        for dc in range(8):
                            E("tensor", lambda t, pb=pb, sub=sub, vcc=vcc, dc=dc, znb=znb: t.matmul(
                                p_v[pb][:],
                                znb[:, dc * 512 + sub * 128: dc * 512 + sub * 128 + 128],
                                wqkv_bf[:, dc * 3072 + 2048 + vcc * 512:
                                        dc * 3072 + 2048 + vcc * 512 + 512],
                                start=(dc == 0), stop=(dc == 7)),
                                waits=([(V, v_znT), (V, pv_free_V[pb])] if dc == 0 else []),
                                inc=T if dc == 7 else None)
                        t_v = T.n
                        E("vector", lambda v, pb=pb, vcc=vcc: v.tensor_tensor(
                            vstg[pb][:].rearrange("p (c w) -> p c w", c=8, w=65)[:, :, 0:64],
                            p_v[pb][:].rearrange("p (c w) -> p c w", c=8, w=64),
                            vbias_rep[:, vcc * 512:(vcc + 1) * 512].rearrange(
                                "p (c w) -> p c w", c=8, w=64),
                            ALU.add),
                            waits=[(T, t_v), (V, vstg_ones_v), (V, vbias_v),
                                   (VST, VST.n)], inc=V)
                        pv_free_V[pb] = V.n
                        v_v = V.n
                        E("scalar", lambda s, pb=pb, sub=sub, vcc=vcc, tok0=tok0: s.dma_start(
                            out=sc_v[tok0 + sub * 128: tok0 + sub * 128 + 128,
                                     vcc * 520:(vcc + 1) * 520],
                            in_=vstg[pb][:]),
                            waits=[(V, v_v)], inc=VST, amt=16)
                        vst_ld[pb] = VST.n
            else:
                for cc in range(8):
                    pb = cc % 2
                    for dc in range(8):
                        E("tensor", lambda t, pb=pb, cc=cc, dc=dc, znb=znb: t.matmul(
                            p_k[pb][:],
                            wqkv_bf[:, dc * 3072 + cc * 128: dc * 3072 + cc * 128 + 128],
                            znb[:, dc * 512:(dc + 1) * 512],
                            start=(dc == 0), stop=(dc == 7)),
                            waits=([(V, v_znT), (S, pk_free_S[pb])] if dc == 0 else []),
                            inc=T if dc == 7 else None)
                    t_q = T.n
                    E("scalar", lambda sc, pb=pb, cc=cc: sc.activation(
                        qstg[pb][:], p_k[pb][:], AF.Identity, scale=0.125,
                        bias=qbq8[:, cc:cc + 1]),
                        waits=[(T, t_q), (SST, SST.n), (V, qbq8_v)], inc=S)
                    pk_free_S[pb] = S.n
                    s_q = S.n
                    E("scalar", lambda s, pb=pb, cc=cc, tok0=tok0: s.dma_start(
                        out=sc_q[cc * 128:(cc + 1) * 128, tok0:tok0 + 512],
                        in_=qstg[pb][:]),
                        waits=[(S, s_q)], inc=SST, amt=16)
                    qst_ld[pb] = SST.n
            znT_mm_t[par_c] = T.n

        barrierA = [(T, T.n), (V, V.n), (S, S.n), (LD, LD.n), (GD, GD.n)] + \
            [(x, x.n) for x in XD3 + XG3] + [(w, w.n) for w in WS2] + \
            [(VST, VST.n), (SST, SST.n)]
        full_barrier(barrierA)

        if 'B' not in stages:
            for o, t in dbg_outs:
                E("sync", lambda s, o=o, t=t: s.dma_start(out=o, in_=t),
                  waits=[(LD, LD.n), (GD, GD.n)], inc=LD, amt=16)
            E("sync", lambda s: s.nop(),
              waits=[(LD, LD.n), (GD, GD.n)] + barrierA)
            block = En(nc.Block())
            P.realize(block)
            return nc
        reg.reset()
        wproj_bf = reg.alloc("wproj_bf", [128, 8 * 1024], BF16)
        kT2 = [reg.alloc(f"kT2_{i}", [128, 2 * NB], BF16) for i in range(2)]
        vA = [reg.alloc(f"vA_{i}", [128, 64 * 130 + 64], BF16) for i in range(2)]
        qA = [reg.alloc(f"qA_{i}", [128, 2 * NQ], BF16) for i in range(2)]
        PT = [reg.alloc(f"PT_{i}", [128, 1024], BF16) for i in range(3)]
        astg = [reg.alloc(f"astg{i}", [64, 512], BF16) for i in range(2)]
        cvec = reg.alloc("cvec", [128, 64], F32)
        crow = reg.alloc("crow", [16, 128], BF16)
        rd65 = [reg.alloc(f"rd65_{i}", [65, 512], F32) for i in range(2)]
        rep_sb = [reg.alloc(f"rep_sb{i}", [64, 512], F32) for i in range(2)]

        p_s = [psum_at(f"p_s{i}", [128, 1024], F32, 2 * i) for i in range(3)]
        p_av = [psum_at(f"p_av{i}", [128, 512], F32, 6 + i) for i in range(2)]
        p_x = psum_at("p_x", [128, 512], F32, 4)

        for i in range(2):
            E("vector", lambda v, i=i: v.memset(kT2[i][64:128, :], 0.0),
              waits=barrierA if i == 0 else [], inc=V)
            E("vector", lambda v, i=i: v.memset(kT2[i][64:65, :], 1.0),
              waits=[(V, V.n)], inc=V)
            E("vector", lambda v, i=i: v.memset(qA[i][64:128, :], 0.0), inc=V)
            E("vector", lambda v, i=i: v.memset(vA[i][:, 64 * 130:], 0.0), inc=V)
        kones_v = V.n

        E("gpsimd", lambda g: g.dma_start(
            out=wproj_bf[:].rearrange("p (a b) -> p a b", a=8, b=1024),
            in_=wproj_h),
            waits=barrierA, inc=GD, amt=16)
        wproj_gd = GD.n

        kv_ld = [0, 0]
        buf_free_t = [0, 0]
        pt_av_t = [0, 0, 0]
        s_exp = [0, 0, 0]
        av_free_v = [0, 0]
        astg_ld = [0, 0]
        px_free_V = 0
        px_free_S = 0
        ps_sub_V = 0

        def load_pair(p, buf):
            E("sync", lambda s, p=p, buf=buf: s.dma_start(
                out=kT2[buf][0:64, 0:NB], in_=sc_k[128 * p:128 * p + 64, :]),
                waits=barrierA + [(T, buf_free_t[buf])], inc=KV[buf], amt=16)
            E("sync", lambda s, p=p, buf=buf: s.dma_start(
                out=kT2[buf][0:64, NB:2 * NB],
                in_=sc_k[128 * p + 64:128 * p + 128, :]),
                inc=KV[buf], amt=16)
            E("sync", lambda s, p=p, buf=buf: s.dma_start(
                out=qA[buf][0:64, 0:NQ], in_=sc_q[128 * p:128 * p + 64, :]),
                inc=KV[buf], amt=16)
            E("sync", lambda s, p=p, buf=buf: s.dma_start(
                out=qA[buf][0:64, NQ:2 * NQ],
                in_=sc_q[128 * p + 64:128 * p + 128, :]),
                inc=KV[buf], amt=16)
            for c in range(64):
                E("sync", lambda s, p=p, buf=buf, c=c: s.dma_start(
                    out=vA[buf][:, c * 130:(c + 1) * 130],
                    in_=sc_v[c * 128:(c + 1) * 128, 130 * p:130 * p + 130]),
                    inc=KV[buf], amt=16)
            kv_ld[buf] = KV[buf].n

        load_pair(0, 0)
        for p in range(8):
            buf = p % 2
            if p < 7:
                load_pair(p + 1, (p + 1) % 2)
            for hh in range(2):
                h = 2 * p + hh
                koff = hh * NB
                qoff = hh * NQ
                sub_slot_V = [0, 0, 0, 0]
                for qc in range(16):
                    slot = qc % 4
                    w0 = ([(KV[buf], kv_ld[buf]), (V, kones_v),
                           (V, max(av_free_v)), (S, max(s_exp)),
                           (V, ps_sub_V)] if qc == 0 else []) +                          [(V, sub_slot_V[slot])]
                    E("tensor", lambda t, buf=buf, qc=qc, koff=koff, qoff=qoff, slot=slot: t.matmul(
                        p_s[slot // 2][:, (slot % 2) * 512:(slot % 2) * 512 + 512],
                        qA[buf][0:64, qoff + qc * 128: qoff + qc * 128 + 128],
                        kT2[buf][0:64, koff:koff + 512], start=True, stop=True),
                        waits=w0, inc=T)
                    t_sub = T.n
                    E("vector", lambda v, qc=qc, slot=slot: v.reduce_max(
                        cvec[:, qc:qc + 1],
                        p_s[slot // 2][:, (slot % 2) * 512:(slot % 2) * 512 + 512],
                        axis=AX.X, negate=True),
                        waits=[(T, t_sub)], inc=V)
                    sub_slot_V[slot] = V.n
                v_cv = V.n
                ps_sub_V = v_cv
                E("tensor", lambda t: t.transpose(p_x[0:16, 0:128], cvec[:, 0:16], id32[:]),
                  waits=[(V, v_cv), (V, px_free_V), (S, px_free_S),
                         (S, max(s_exp))], inc=T)
                t_cv = T.n
                E("scalar", lambda sc: sc.activation(
                    crow[:], p_x[0:16, 0:128], AF.Copy),
                    waits=[(T, t_cv)], inc=S)
                px_free_S = S.n
                s_cv = S.n
                E("sync", lambda s: s.dma_start(
                    out=sc_c.rearrange("o (a b) -> (o a) b", a=16, b=128),
                    in_=crow[:]),
                    waits=[(S, s_cv)], inc=LD, amt=16)
                c_ld = LD.n
                E("sync", lambda s, buf=buf, qoff=qoff: s.dma_start(
                    out=qA[buf][64:65, qoff:qoff + NQ], in_=sc_c),
                    waits=[(LD, c_ld)], inc=LD, amt=16)
                crow_ld = LD.n

                for sw in range(2):
                    so = qoff + sw * 1024

                    def emit_scores(kj):
                        st_ = kj % 3
                        for half in range(2):
                            w0 = []
                            if half == 0:
                                w0 = [(LD, crow_ld), (S, s_exp[st_]),
                                      (V, ps_sub_V)]
                                if kj == 2:
                                    w0 += [(V, px_free_V), (S, px_free_S)]
                            E("tensor", lambda t, buf=buf, kj=kj, half=half, st_=st_, so=so, koff=koff: t.matmul(
                                p_s[st_][:, half * 512:(half + 1) * 512],
                                kT2[buf][0:128, koff + kj * 128: koff + kj * 128 + 128],
                                qA[buf][0:128, so + half * 512: so + half * 512 + 512],
                                start=True, stop=True),
                                waits=w0,
                                inc=T if half == 1 else None)
                        t_s = T.n
                        E("scalar", lambda sc, st_=st_: sc.activation(
                            PT[st_][:], p_s[st_][:], AF.Exp),
                            waits=[(T, t_s), (T, pt_av_t[st_])], inc=S)
                        s_exp[st_] = S.n

                    def emit_av(kj):
                        st_ = kj % 3
                        voff = 65 * hh
                        for half in range(2):
                            E("tensor", lambda t, buf=buf, kj=kj, half=half, st_=st_, voff=voff: t.matmul(
                                p_av[half][:],
                                vA[buf][:, kj * 130 + voff: kj * 130 + voff + 128],
                                PT[st_][:, half * 512:(half + 1) * 512],
                                start=(kj == 0), stop=(kj == 63)),
                                waits=([(S, s_exp[st_])] +
                                       ([(V, av_free_v[half])] if kj == 0 else [])),
                                inc=T if half == 1 else None)
                        pt_av_t[st_] = T.n

                    emit_scores(0)
                    for kj in range(1, 64):
                        emit_scores(kj)
                        emit_av(kj - 1)
                    emit_av(63)
                    t_avdone = T.n
                    for half in range(2):
                        E("vector", lambda v, half=half: v.reciprocal(
                            rd65[half][64:65, :], p_av[half][64:65, :]),
                            waits=[(T, t_avdone)], inc=V)
                        v_rd = V.n
                        E("tensor", lambda t, half=half: t.matmul(
                            p_x[0:64, :], ones65[64:65, :], rd65[half][64:65, :],
                            start=True, stop=True),
                            waits=[(V, v_rd), (V, ones65_v), (V, px_free_V),
                                   (S, px_free_S), (S, max(s_exp))], inc=T)
                        t_rep = T.n
                        E("vector", lambda v, half=half: v.tensor_copy(
                            rep_sb[half][:], p_x[0:64, :]),
                            waits=[(T, t_rep)], inc=V)
                        v_rep = V.n
                        px_free_V = v_rep
                        E("vector", lambda v, half=half: v.tensor_tensor(
                            astg[half][:], p_av[half][0:64, :], rep_sb[half][:],
                            ALU.mult),
                            waits=[(V, v_rep), (LD, LD.n)], inc=V)
                        av_free_v[half] = V.n
                        E("sync", lambda s, h=h, sw=sw, half=half: s.dma_start(
                            out=sc_at[64 * h:64 * h + 64,
                                      sw * 1024 + half * 512:
                                      sw * 1024 + half * 512 + 512],
                            in_=astg[half][:]),
                            waits=[(V, av_free_v[half])], inc=LD, amt=16)
                        astg_ld[half] = LD.n
            buf_free_t[buf] = T.n

        barrierB = [(T, T.n), (V, V.n), (S, S.n), (LD, LD.n), (GD, GD.n)] + [(c, c.n) for c in KV]
        full_barrier(barrierB)

        if 'C' not in stages:
            for o, t in dbg_outs:
                E("sync", lambda s, o=o, t=t: s.dma_start(out=o, in_=t),
                  waits=[(LD, LD.n), (GD, GD.n)] + barrierB, inc=LD, amt=16)
            E("sync", lambda s: s.nop(),
              waits=[(LD, LD.n), (GD, GD.n)] + barrierB)
            block = En(nc.Block())
            P.realize(block)
            return nc
        reg.reset()
        reg.off += 8 * 1024 * 2
        wproj_bf2 = wproj_bf
        attnc = [reg.alloc(f"attnc{i}", [128, 8 * 512], BF16) for i in range(2)]
        projT = reg.alloc("projT", [128, 8 * 512], F32)
        xq_t = [reg.alloc(f"xq_t{i}", [128, D], F32) for i in range(2)]
        l1_t = reg.alloc("l1_t", [128, D], F32)
        x2stg = [reg.alloc(f"x2stg{i}", [128, D], F32) for i in range(2)]

        p_p = [psum_at(f"p_p{i}", [128, 512], F32, i) for i in range(2)]
        p_t2 = [psum_at(f"p_t2{i}", [128, 512], F32, 2 + i) for i in range(2)]

        at_ld = [0, 0]
        at_mm_t = [0, 0]
        projT_s = 0
        projT_tr_t = 0
        pp_free_S = [0, 0]
        pt2_free_V = [0, 0]
        xqt_free_V = [0, 0]
        x2stg_free_LD = [0, 0]
        for tci in range(4):
            cbuf = tci % 2
            for dc in range(8):
                E("sync", lambda s, cbuf=cbuf, tci=tci, dc=dc: s.dma_start(
                    out=attnc[cbuf][:, dc * 512:(dc + 1) * 512],
                    in_=sc_at[dc * 128:(dc + 1) * 128, tci * 512:(tci + 1) * 512]),
                    waits=(barrierB + [(T, at_mm_t[cbuf])]) if dc == 0 else [],
                    inc=LD, amt=16)
            at_ld[cbuf] = LD.n
            for oc in range(8):
                pb = oc % 2
                for dc in range(8):
                    E("tensor", lambda t, cbuf=cbuf, oc=oc, dc=dc, pb=pb: t.matmul(
                        p_p[pb][:],
                        wproj_bf2[:, dc * 1024 + oc * 128: dc * 1024 + oc * 128 + 128],
                        attnc[cbuf][:, dc * 512:(dc + 1) * 512],
                        start=(dc == 0), stop=(dc == 7)),
                        waits=([(LD, at_ld[cbuf]), (GD, wproj_gd),
                                (S, pp_free_S[pb]), (S, projT_s)] if dc == 0 else []),
                        inc=T if dc == 7 else None)
                t_p = T.n
                E("scalar", lambda sc, oc=oc, pb=pb: sc.activation(
                    projT[:, oc * 512:(oc + 1) * 512], p_p[pb][:], AF.Identity,
                    bias=bpr_sb[:, oc:oc + 1]),
                    waits=[(T, t_p), (T, projT_tr_t)], inc=S)
                pp_free_S[pb] = S.n
            at_mm_t[cbuf] = T.n
            projT_s = S.n
            for sub in range(4):
                g = tci * 4 + sub
                xb = xq_t[g % 2]
                E("sync", lambda s, g=g, xb=xb: s.dma_start(
                    out=xb[:], in_=xq[g * 128:(g + 1) * 128, :]),
                    waits=[(V, xqt_free_V[g % 2])], inc=XQ, amt=16)
                xq_ld = XQ.n
                E("vector", lambda v, g=g, xb=xb: v.tensor_scalar(
                    l1_t[:], xb[:], mu1q[:, g:g + 1], rs1q[:, g:g + 1],
                    ALU.subtract, ALU.mult),
                    waits=[(XQ, xq_ld)], inc=V)
                v_l1 = V.n
                E("vector", lambda v, g=g, xb=xb: v.tensor_tensor(
                    x2stg[g % 2][:], xb[:], l1_t[:], ALU.add),
                    waits=[(V, v_l1), (LD, LD.n)], inc=V)
                v_x2a = V.n
                for oc in range(8):
                    pb2 = oc % 2
                    E("tensor", lambda t, sub=sub, oc=oc, pb2=pb2: t.transpose(
                        p_t2[pb2][:, 0:128],
                        projT[:, oc * 512 + sub * 128: oc * 512 + sub * 128 + 128],
                        id32[:]),
                        waits=([(S, projT_s)] if (sub == 0 and oc == 0) else []) +
                              [(V, pt2_free_V[pb2])],
                        inc=T)
                    t_tr2 = T.n
                    E("vector", lambda v, g=g, oc=oc, pb2=pb2: v.tensor_tensor(
                        x2stg[g % 2][:, oc * 128:(oc + 1) * 128],
                        x2stg[g % 2][:, oc * 128:(oc + 1) * 128],
                        p_t2[pb2][:, 0:128], ALU.add),
                        waits=[(T, t_tr2), (V, v_x2a)], inc=V)
                    pt2_free_V[pb2] = V.n
                projT_tr_t = T.n
                v_x2 = V.n
                xqt_free_V[g % 2] = v_x2
                E("sync", lambda s, g=g: s.dma_start(
                    out=sc_x2[g * 128:(g + 1) * 128, :], in_=x2stg[g % 2][:]),
                    waits=[(V, v_x2)], inc=LD, amt=16)
                x2stg_free_LD[g % 2] = LD.n

        barrierC = [(T, T.n), (V, V.n), (S, S.n), (LD, LD.n), (GD, GD.n), (XQ, XQ.n)]
        full_barrier(barrierC)

        if 'D' not in stages:
            for o, t in dbg_outs:
                E("sync", lambda s, o=o, t=t: s.dma_start(out=o, in_=t),
                  waits=[(LD, LD.n), (GD, GD.n)] + barrierC, inc=LD, amt=16)
            E("sync", lambda s: s.nop(),
              waits=[(LD, LD.n), (GD, GD.n)] + barrierC)
            block = En(nc.Block())
            P.realize(block)
            return nc
        reg.reset()
        x2til = [reg.alloc(f"x2til{i}", [128, D], F32) for i in range(2)]
        x2ln = [reg.alloc("x2ln0", [128, D], F32)] * 2
        hs_t = reg.alloc("hs_t", [128, D], F32)
        hsTb = reg.alloc("hsTb", [128, 8 * 512], BF16)
        hsTf = [reg.alloc(f"hsTf{i}", [128, 128], F32) for i in range(2)]
        base_sb = reg.alloc("base_sb", [128, 32 * 512], BF16)
        hsum = reg.alloc("hsum", [128, 32 * 512], BF16)
        gt = reg.alloc("gt", [128, 512], BF16)
        wiT_c = [reg.alloc(f"wiT_c{i}", [128, 8 * 128], BF16) for i in range(2)]
        woT_c = [reg.alloc(f"woT_c{i}", [128, 32 * 128], BF16) for i in range(2)]
        lbT_c = [reg.alloc(f"lbT_c{i}", [16, DFF], BF16) for i in range(2)]
        a_sb = reg.alloc("a_sb", [16, 8 * 512], BF16)
        grep = [reg.alloc(f"grep{i}", [128, 512], F32) for i in range(2)]
        oacc = reg.alloc("oacc", [128, 8 * 512], F32)
        t3 = reg.alloc("t3", [128, 512], F32)
        lg_sb = reg.alloc("lg_sb", [128, 8], F32)
        pex = reg.alloc("pex", [128, 8], F32)
        pmask = reg.alloc("pmask", [128, 8], F32)
        gates_tok = reg.alloc("gates_tok", [128, 64], F32)
        gatesT = reg.alloc("gatesT", [8, 512], F32)
        vs2 = reg.alloc("vs2", [128, 4], F32)

        p_b = [psum_at(f"p_b{i}", [128, 512], F32, i) for i in range(2)]
        p_u = [psum_at(f"p_u{i}", [128, 512], F32, 2 + i) for i in range(2)]
        p_w = [psum_at(f"p_w{i}", [128, 512], F32, 4 + i) for i in range(2)]
        p_lg = psum_at("p_lg", [128, 512], F32, 6)
        p_d = psum_at("p_d", [128, 512], F32, 7)

        def gc(c):
            return gst[:, c:c + 1]

        pd_free = {"V": 0, "S": 0}

        x2t_free = [[], []]
        x2e_free = [[], []]
        pb_free_S = [0, 0]
        pu_free_V = [0, 0]
        pu_free_S = [0, 0]
        pw_free_V = [0, 0]
        wiT_free_T = [0, 0]
        woT_free_T = [0, 0]
        lbT_free_T = [0, 0]
        hsum_read_T = 0
        hsTf_read_T = [0, 0]
        plg_read_V = [0, 0, 0, 0]
        base_read_V = 0
        base_evac_S = [0] * 32
        hs_read_T = 0
        gatesT_read_T = 0
        gates_tok_read_T = 0
        prev_s_ex = 0
        prev_gate_V = 0
        x3_free_LD = 0

        for tci in range(4):
            s_gT_list = []
            for sub in range(4):
                g = tci * 4 + sub
                xb = x2ln[g % 2]
                E("sync", lambda s, g=g, xb=xb: s.dma_start(
                    out=xb[:], in_=sc_x2[g * 128:(g + 1) * 128, :]),
                    waits=(barrierC if g == 0 else []) + x2t_free[0] + x2t_free[1],
                    inc=XT2[g % 2], amt=16)
                x2_ld = XT2[g % 2].n
                E("scalar", lambda sc, xb=xb, g=g: sc.activation(
                    dmy[:], xb[:], AF.Copy, accum_out=gc(0)),
                    waits=[(XT2[g % 2], x2_ld), (V, prev_gate_V)], inc=S)
                s_s = S.n
                E("scalar", lambda sc, xb=xb, g=g: sc.activation(
                    dmy[:], xb[:], AF.Square, accum_out=gc(2)),
                    waits=[(S, S.n)], inc=S)
                s_sq2 = S.n
                E("vector", lambda v: v.tensor_scalar(
                    gc(1), gc(0), 1.0 / 1024.0, None, ALU.mult),
                    waits=[(S, s_s), (V, V.n)], inc=V)
                E("vector", lambda v: v.tensor_tensor(
                    gc(5), gc(1), gc(1), ALU.mult), waits=[(V, V.n)], inc=V)
                E("vector", lambda v: v.tensor_scalar(
                    gc(3), gc(2), 1.0 / 1024.0, gc(5), ALU.mult, ALU.subtract),
                    waits=[(V, V.n), (S, s_sq2)], inc=V)
                v_var = V.n
                E("scalar", lambda sc: sc.activation(
                    gc(4), gc(3), AF.Ln, bias=eps_sb[:]),
                    waits=[(V, v_var)], inc=S)
                E("scalar", lambda sc: sc.activation(
                    gc(4), gc(4), AF.Exp, scale=-0.5), waits=[(S, S.n)], inc=S)
                s_rs = S.n
                E("vector", lambda v, xb=xb: v.tensor_scalar(
                    hs_t[:], xb[:], gc(1), gc(4), ALU.subtract, ALU.mult),
                    waits=[(S, s_rs), (T, hs_read_T), (V, V.n)], inc=V)
                v_hs = V.n
                x2t_free[g % 2] = [(V, v_hs), (S, s_sq2)]
                for dc in range(8):
                    pf = dc % 2
                    E("tensor", lambda t, dc=dc, pf=pf: t.transpose(
                        p_u[pf][:, 0:128], hs_t[:, dc * 128:(dc + 1) * 128],
                        id32[:]),
                        waits=([(V, v_hs)] if dc == 0 else []) +
                              [(V, pu_free_V[pf]), (S, pu_free_S[pf])], inc=T)
                    t_t = T.n
                    E("vector", lambda v, dc=dc, sub=sub, pf=pf: v.tensor_copy(
                        hsTb[:, dc * 512 + sub * 128: dc * 512 + sub * 128 + 128],
                        p_u[pf][:, 0:128]),
                        waits=[(T, t_t)], inc=V)
                    pu_free_V[pf] = V.n
                    E("vector", lambda v, pf=pf: v.tensor_copy(
                        hsTf[pf][:], p_u[pf][:, 0:128]),
                        waits=[(T, t_t), (T, hsTf_read_T[pf])], inc=V)
                    pu_free_V[pf] = V.n
                    s_hf = V.n
                    E("tensor", lambda t, sub=sub, dc=dc, pf=pf: t.matmul(
                        p_lg[:, sub * 8:(sub + 1) * 8], hsTf[pf][:],
                        gw_sb[:, dc * 8:(dc + 1) * 8],
                        start=(dc == 0), stop=(dc == 7), skip_group_check=True),
                        waits=[(V, s_hf)] +
                              ([(V, plg_read_V[sub])] if dc == 0 else []),
                        inc=T if dc == 7 else None)
                    hsTf_read_T[pf] = T.n
                hs_read_T = T.n
                t_lg = T.n
                E("vector", lambda v, sub=sub: v.tensor_tensor(
                    lg_sb[:], p_lg[:, sub * 8:(sub + 1) * 8], gbr_sb[:], ALU.add),
                    waits=[(T, t_lg), (S, prev_s_ex)], inc=V)
                plg_read_V[sub] = V.n
                E("vector", lambda v: v.reduce_max(
                    gc(5), lg_sb[:], axis=AX.X, negate=True),
                    waits=[(V, V.n)], inc=V)
                v_m1 = V.n
                E("scalar", lambda sc: sc.activation(
                    pex[:], lg_sb[:], AF.Exp, bias=gc(5)),
                    waits=[(V, v_m1)], inc=S)
                s_ex = S.n
                prev_s_ex = s_ex
                E("vector", lambda v: v.reduce_sum(gc(7), pex[:], axis=AX.X),
                  waits=[(S, s_ex), (V, V.n)], inc=V)
                v_ss = V.n
                E("vector", lambda v: v.reciprocal(gc(8), gc(7)),
                  waits=[(V, v_ss), (V, V.n)], inc=V)
                v_rsum = V.n
                E("vector", lambda v: v.tensor_scalar(
                    pex[:], pex[:], gc(8), None, ALU.mult),
                    waits=[(V, v_rsum), (V, V.n)], inc=V)
                v_p = V.n
                E("vector", lambda v: v.reduce_max(gc(6), pex[:], axis=AX.X),
                  waits=[(V, v_p), (V, V.n)], inc=V)
                v_pm1 = V.n
                E("vector", lambda v: v.tensor_scalar(
                    pmask[:], pex[:], gc(6), None, ALU.is_lt),
                    waits=[(V, v_pm1), (V, V.n)], inc=V)
                E("vector", lambda v: v.tensor_tensor(
                    pmask[:], pmask[:], pex[:], ALU.mult),
                    waits=[(V, V.n)], inc=V)
                E("vector", lambda v: v.reduce_max(gc(9), pmask[:], axis=AX.X),
                  waits=[(V, V.n)], inc=V)
                v_m2 = V.n
                E("vector", lambda v: v.tensor_tensor(
                    gc(10), gc(6), gc(9), ALU.add),
                    waits=[(V, v_m2), (V, V.n)], inc=V)
                v_vs = V.n
                E("vector", lambda v, sub=sub: v.tensor_scalar(
                    vs2[:, sub:sub + 1], gc(10), 2.0, None, ALU.mult),
                    waits=[(V, v_vs), (V, V.n)], inc=V)
                E("vector", lambda v: v.tensor_scalar(
                    gates_tok[:, 0:8], pex[:], gc(9), None, ALU.is_ge),
                    waits=[(V, v_m2), (T, gates_tok_read_T), (V, V.n)], inc=V)
                E("vector", lambda v: v.tensor_scalar(
                    gates_tok[:, 0:8], gates_tok[:, 0:8], gc(10), None, ALU.mult),
                    waits=[(V, v_vs), (V, V.n)], inc=V)
                v_g = V.n
                prev_gate_V = v_g
                E("tensor", lambda t: t.transpose(
                    p_d[0:8, 0:128], gates_tok[:, 0:8], id32[:]),
                    waits=[(V, v_g), (V, pd_free["V"]), (S, pd_free["S"])], inc=T)
                t_gT = T.n
                gates_tok_read_T = t_gT
                E("vector", lambda v, sub=sub: v.tensor_copy(
                    gatesT[:, sub * 128:(sub + 1) * 128], p_d[0:8, 0:128]),
                    waits=[(T, t_gT), (T, gatesT_read_T)], inc=V)
                pd_free["V"] = V.n
                s_gT_list.append(V.n)
            s_gatesT = s_gT_list[-1] if s_gT_list else 0
            hsTb_done_V = V.n

            for fcc in range(32):
                wb = fcc % 2
                E("gpsimd", lambda g, fcc=fcc, wb=wb: g.dma_start(
                    out=wiT_c[wb][:].rearrange("p (a b) -> p a b", a=8, b=128),
                    in_=wiT_h[fcc]),
                    waits=[(T, wiT_free_T[wb])], inc=WI2[wb], amt=16)
                g_wi = WI2[wb].n
                for dc in range(8):
                    E("tensor", lambda t, fcc=fcc, dc=dc, wb=wb: t.matmul(
                        p_b[wb][:], wiT_c[wb][:, dc * 128:(dc + 1) * 128],
                        hsTb[:, dc * 512:(dc + 1) * 512],
                        start=(dc == 0), stop=(dc == 7)),
                        waits=([(WI2[wb], g_wi), (S, pb_free_S[wb]),
                                (V, base_read_V), (V, hsTb_done_V)] if dc == 0 else []),
                        inc=T if dc == 7 else None)
                wiT_free_T[wb] = T.n
                t_b = T.n
                E("scalar", lambda sc, fcc=fcc, wb=wb: sc.activation(
                    base_sb[:, fcc * 512:(fcc + 1) * 512], p_b[wb][:], AF.Identity,
                    bias=bi_sb[:, fcc:fcc + 1]),
                    waits=[(T, t_b)], inc=S)
                pb_free_S[wb] = S.n
                base_evac_S[fcc] = S.n
            s_base = S.n

            pd_aV = pd_free["V"]
            for e in range(8):
                for dc in range(8):
                    E("tensor", lambda t, e=e, dc=dc: t.matmul(
                        p_d[0:16, :],
                        laT[:, (dc * 8 + e) * 16:(dc * 8 + e) * 16 + 16],
                        hsTb[:, dc * 512:(dc + 1) * 512],
                        start=(dc == 0), stop=(dc == 7)),
                        waits=([(GD, GD.n), (V, s_gatesT), (V, hsTb_done_V),
                                (V, pd_aV)] if dc == 0 else []),
                        inc=T if dc == 7 else None)
                t_a = T.n
                E("vector", lambda v, e=e: v.tensor_copy(
                    a_sb[:, e * 512:(e + 1) * 512], p_d[0:16, :]),
                    waits=[(T, t_a)], inc=V)
                pd_aV = V.n
            v_a_all = V.n
            pd_free["V"] = V.n
            hsT_read = T.n

            def emit_grep(e):
                E("tensor", lambda t, e=e: t.matmul(
                    p_d[:, :], sel_sb[:, e * 128:(e + 1) * 128], gatesT[:],
                    start=True, stop=True),
                    waits=[(V, s_gatesT), (V, pd_free["V"]),
                           (S, pd_free["S"])], inc=T)
                t_g = T.n
                E("vector", lambda v, e=e: v.tensor_copy(
                    grep[e % 2][:], p_d[:, :]),
                    waits=[(T, t_g)], inc=V)
                pd_free["V"] = V.n
                return V.n, t_g

            grep_done_V = [0] * 8
            g_lb = [0, 0]

            def lbT_load(e):
                eb = e % 2
                E("gpsimd", lambda gg, e=e, eb=eb: gg.dma_start(
                    out=lbT_c[eb][:], in_=lbT_h[e]),
                    waits=[(T, lbT_free_T[eb])], inc=LB2[eb], amt=16)
                g_lb[eb] = LB2[eb].n

            lbT_load(0)
            for e in range(8):
                grep_done_V[e], gatesT_read_T = emit_grep(e)
                eb = e % 2
                if e < 7:
                    lbT_load(e + 1)
                for fcc in range(32):
                    ub = fcc % 2
                    w0 = [(V, pu_free_V[ub]), (S, pu_free_S[ub])]
                    if fcc < 2:
                        w0 += [(LB2[eb], g_lb[eb]), (V, v_a_all)]
                        if e == 0:
                            w0 += [(T, hsum_read_T)]
                    E("tensor", lambda t, e=e, fcc=fcc, ub=ub, eb=eb: t.matmul(
                        p_u[ub][:], lbT_c[eb][:, fcc * 128:(fcc + 1) * 128],
                        a_sb[:, e * 512:(e + 1) * 512],
                        start=True, stop=False),
                        waits=w0, inc=None)
                    E("tensor", lambda t, fcc=fcc, ub=ub: t.matmul(
                        p_u[ub][:], idbf[:],
                        base_sb[:, fcc * 512:(fcc + 1) * 512],
                        start=False, stop=True),
                        waits=[(S, base_evac_S[fcc])] if e == 0 else [],
                        inc=T)
                    t_u = T.n
                    lbT_free_T[eb] = t_u
                    if e == 0:
                        E("vector", lambda v, fcc=fcc, ub=ub, e=e: v.scalar_tensor_tensor(
                            hsum[:, fcc * 512:(fcc + 1) * 512], p_u[ub][:],
                            0.0, grep[e % 2][:], ALU.max, ALU.mult),
                            waits=[(T, t_u), (V, grep_done_V[e])], inc=V)
                    else:
                        E("vector", lambda v, fcc=fcc, ub=ub, e=e: v.scalar_tensor_tensor(
                            gt[:], p_u[ub][:],
                            0.0, grep[e % 2][:], ALU.max, ALU.mult),
                            waits=[(T, t_u), (V, V.n)] +
                                  ([(V, grep_done_V[e])] if fcc == 0 else []),
                            inc=V)
                        E("vector", lambda v, fcc=fcc: v.tensor_tensor(
                            hsum[:, fcc * 512:(fcc + 1) * 512],
                            hsum[:, fcc * 512:(fcc + 1) * 512], gt[:], ALU.add),
                            waits=[(V, V.n)], inc=V)
                    pu_free_V[ub] = V.n
            hsum_done_V = V.n

            for dc in range(8):
                wob = dc % 2
                E("sync", lambda s, dc=dc, wob=wob: s.dma_start(
                    out=woT_c[wob][:].rearrange("p (a b) -> p a b",
                                                a=32, b=128),
                    in_=sc_wo[dc]),
                    waits=[(T, woT_free_T[wob])], inc=WO2[wob], amt=16)
                wo_ld = WO2[wob].n
                for fcc in range(32):
                    E("tensor", lambda t, dc=dc, fcc=fcc, wob=wob: t.matmul(
                        p_w[wob][:],
                        woT_c[wob][:, fcc * 128:(fcc + 1) * 128],
                        hsum[:, fcc * 512:(fcc + 1) * 512],
                        start=(fcc == 0), stop=(fcc == 31)),
                        waits=([(V, hsum_done_V), (WO2[wob], wo_ld),
                                (V, pw_free_V[wob])] if fcc == 0 else []),
                        inc=T if fcc == 31 else None)
                woT_free_T[wob] = T.n
                t_w = T.n
                E("vector", lambda v, dc=dc, wob=wob: v.tensor_copy(
                    oacc[:, dc * 512:(dc + 1) * 512], p_w[wob][:]),
                    waits=[(T, t_w)], inc=V)
                pw_free_V[wob] = V.n
            hsum_read_T = T.n
            v_oacc = V.n
            base_read_V = V.n

            oacc_read_T = 0
            for sub in range(4):
                g = tci * 4 + sub
                xb = x2til[g % 2]
                E("sync", lambda s, g=g, xb=xb: s.dma_start(
                    out=xb[:], in_=sc_x2[g * 128:(g + 1) * 128, :]),
                    waits=x2e_free[g % 2] + [(OS, OS.n)], inc=XE2[g % 2], amt=16)
                x2_ld2 = XE2[g % 2].n
                for dc in range(8):
                    pe = dc % 2
                    E("tensor", lambda t, sub=sub, dc=dc, pe=pe: t.transpose(
                        p_u[pe][:, 0:128],
                        oacc[:, dc * 512 + sub * 128: dc * 512 + sub * 128 + 128],
                        id32[:]),
                        waits=([(V, v_oacc)] if (sub == 0 and dc == 0) else []) +
                              [(V, pu_free_V[pe]), (S, pu_free_S[pe])],
                        inc=T)
                    t_e = T.n
                    E("vector", lambda v, g=g, dc=dc, pe=pe, xb=xb: v.tensor_tensor(
                        xb[:, dc * 128:(dc + 1) * 128],
                        xb[:, dc * 128:(dc + 1) * 128],
                        p_u[pe][:, 0:128], ALU.add),
                        waits=[(T, t_e), (XE2[g % 2], x2_ld2)],
                        inc=V)
                    pu_free_V[pe] = V.n
                v_e1 = V.n
                E("vector", lambda v, sub=sub: v.tensor_scalar(
                    t3[:, 0:512], bo_rep[:, 0:512], vs2[:, sub:sub + 1], None,
                    ALU.mult),
                    waits=[(V, v_e1), (V, V.n)], inc=V)
                E("vector", lambda v, sub=sub, xb=xb: v.tensor_tensor(
                    xb[:, 0:512], xb[:, 0:512], t3[:, 0:512], ALU.add),
                    waits=[(V, V.n)], inc=V)
                E("vector", lambda v, sub=sub: v.tensor_scalar(
                    t3[:, 0:512], bo_rep[:, 512:1024], vs2[:, sub:sub + 1], None,
                    ALU.mult),
                    waits=[(V, V.n)], inc=V)
                E("vector", lambda v, sub=sub, xb=xb: v.tensor_tensor(
                    xb[:, 512:1024], xb[:, 512:1024], t3[:, 0:512], ALU.add),
                    waits=[(V, V.n)], inc=V)
                v_e2 = V.n
                E("sync", lambda s, tci=tci, sub=sub, xb=xb: s.dma_start(
                    out=out[tci * 512 + sub * 128: tci * 512 + sub * 128 + 128, :],
                    in_=xb[:]),
                    waits=[(V, v_e2)], inc=OS, amt=16)
                x2e_free[g % 2] = [(V, v_e2)]
                x3_free_LD = OS.n

        if dbg == 2:
            for nm, t, shp in [("gatesT", gatesT, [8, 512]), ("gst", gst, [128, 16]),
                               ("lg", lg_sb, [128, 8]), ("pex", pex, [128, 8]),
                               ("vs2", vs2, [128, 4]), ("pmask", pmask, [128, 8])]:
                o = nc.dram_tensor("dbg3_" + nm, shp, t.dtype, kind="ExternalOutput").ap()
                E("sync", lambda s, o=o, t=t: s.dma_start(out=o, in_=t[:]),
                  waits=[(c, c.n) for c in [LD, OS]] + [(T, T.n), (V, V.n), (S, S.n)],
                  inc=LD, amt=16)
        if dbg == 1:
            dmap = [("base", base_sb, [128, 32 * 512]), ("hsum", hsum, [128, 32 * 512]),
                    ("asb", a_sb, [16, 8 * 512]),
                    ("gatesT", gatesT, [8, 512]), ("grep0", grep[0], [128, 512]),
                    ("grep1", grep[1], [128, 512]), ("oacc", oacc, [128, 8 * 512]),
                    ("vs2", vs2, [128, 4]), ("hsTb", hsTb, [128, 8 * 512])]
            for nm, t, shp in dmap:
                o = nc.dram_tensor("dbg2_" + nm, shp, t.dtype, kind="ExternalOutput").ap()
                E("sync", lambda s, o=o, t=t: s.dma_start(out=o, in_=t[:]),
                  waits=[(c, c.n) for c in [LD, OS]] + [(T, T.n), (V, V.n), (S, S.n)],
                  inc=LD, amt=16)
        for o, t in dbg_outs:
            E("sync", lambda s, o=o, t=t: s.dma_start(out=o, in_=t),
              waits=[(LD, LD.n)], inc=LD, amt=16)

        allsems = [LD, GD, OS, XQ, VST, SST] + LB2 + WS2 + KV + XD3 + XG3 + WO2 + XT2 + WI2 + XE2
        E("sync", lambda s: s.nop(), waits=[(c, c.n) for c in allsems])

        block = En(nc.Block())
        P.realize(block)

    return nc


def _layout(inputs):
    w_qkv = np.asarray(inputs["L0_attn_w_qkv"], dtype=np.float32)
    perm = []
    for t in range(3):
        for h in range(16):
            for d in range(64):
                perm.append(192 * h + 3 * d + t)
    wqkv_perm = np.ascontiguousarray(w_qkv[:, perm])
    wqkv_h = np.ascontiguousarray(wqkv_perm.reshape(8, 128, 3072).transpose(1, 0, 2))

    gamma_c = np.ascontiguousarray(
        np.asarray(inputs["L0_attn_ln_scale"], np.float32).reshape(8, 128).T)
    beta_c = np.ascontiguousarray(
        np.asarray(inputs["L0_attn_ln_bias"], np.float32).reshape(8, 128).T)
    wproj_h = np.ascontiguousarray(
        np.asarray(inputs["L0_attn_w_proj"], np.float32).reshape(8, 128, 1024)
        .transpose(1, 0, 2))
    bproj_c = np.ascontiguousarray(
        np.asarray(inputs["L0_attn_b_proj"], np.float32).reshape(8, 128).T)

    wi = np.asarray(inputs["L0_ffn_wi"], np.float32)
    wiT_h = np.ascontiguousarray(wi.reshape(32, 128, 8, 128).transpose(0, 3, 2, 1))
    bi_c = np.ascontiguousarray(
        np.asarray(inputs["L0_ffn_bi"], np.float32).reshape(32, 128).T)
    wo = np.asarray(inputs["L0_ffn_wo"], np.float32)
    woT_h = np.ascontiguousarray(wo.reshape(8, 128, 32, 128).transpose(0, 3, 2, 1))

    laT_h = np.ascontiguousarray(
        np.asarray(inputs["L0_ffn_lora_A"], np.float32).transpose(2, 0, 1)
        .reshape(8, 128, 8, 16).transpose(1, 0, 2, 3))
    lbT_h = np.ascontiguousarray(
        np.asarray(inputs["L0_ffn_lora_B"], np.float32).transpose(0, 2, 1))
    gwT_h = np.ascontiguousarray(
        np.asarray(inputs["L0_ffn_gate_w"], np.float32).T.reshape(8, 128, 8)
        .transpose(1, 0, 2))
    gb_row = np.ascontiguousarray(
        np.asarray(inputs["L0_ffn_gate_b"], np.float32).reshape(1, 8))
    bo_row = np.ascontiguousarray(
        np.asarray(inputs["L0_ffn_bo"], np.float32).reshape(1, D))

    return dict(wqkv_h=wqkv_h, gamma_c=gamma_c, beta_c=beta_c, wproj_h=wproj_h,
                bproj_c=bproj_c, wiT_h=wiT_h, bi_c=bi_c, woT_h=woT_h, laT_h=laT_h,
                lbT_h=lbT_h, gwT_h=gwT_h, gb_row=gb_row, bo_row=bo_row)


def kernel(**inputs):
    x = np.asarray(inputs["L0_attn_z"], dtype=np.float32)
    shared = _layout(inputs)
    nc = build_nc()
    in_maps = []
    for c in range(8):
        b = c // 4
        qoff = (c % 4) * NQ
        m = dict(shared)
        m["x_b"] = np.ascontiguousarray(x[b])
        m["xq"] = np.ascontiguousarray(x[b, qoff:qoff + NQ])
        in_maps.append(m)
    res = run_bass_kernel_spmd(nc, in_maps, core_ids=list(range(8)))
    outp = np.empty((2, NB, D), dtype=np.float32)
    for c in range(8):
        b = c // 4
        qoff = (c % 4) * NQ
        outp[b, qoff:qoff + NQ] = res.results[c]["out"]
    return outp


# version 15
# speedup vs baseline: 1.0173x; 1.0013x over previous
import contextlib
import numpy as np
import ml_dtypes

import concourse.bass as bass
import concourse.mybir as mybir
from concourse.bass_utils import run_bass_kernel_spmd

BFNP = ml_dtypes.bfloat16
F32 = mybir.dt.float32
BF16 = mybir.dt.bfloat16
AF = mybir.ActivationFunctionType
ALU = mybir.AluOpType
AX = mybir.AxisListType

D = 1024
NB = 8192
NQ = 2048
DFF = 4096

PERS_BYTES = 22 * 1024
SBUF_REGION = 185 * 1024


class Ctr:
    def __init__(self, handle):
        self.h = handle
        self.n = 0

    def inc(self, amt):
        self.n += amt
        return self.n


class Prog:
    ENGS = ("tensor", "vector", "scalar", "sync", "gpsimd")

    def __init__(self):
        self.stream = {e: [] for e in self.ENGS}

    def emit(self, eng, fn, waits=(), inc=None, amt=1):
        if inc is not None:
            inc.inc(amt)
        self.stream[eng].append(
            (fn, [w for w in waits if w is not None and w[1]], inc, amt))

    def realize(self, block):
        streams = self.stream

        def run(eng_name, eng):
            waited = {}
            for fn, waits, inc, amt in streams[eng_name]:
                for (ctr, val) in waits:
                    if waited.get(id(ctr), -1) >= val:
                        continue
                    eng.wait_ge(ctr.h, val)
                    waited[id(ctr)] = val
                inst = fn(eng)
                if inc is not None:
                    assert inst is not None, f"{eng_name} op with inc returned None"
                    inst.then_inc(inc.h, amt)

        @block.tensor
        def _(t):
            run("tensor", t)

        @block.vector
        def _(v):
            run("vector", v)

        @block.scalar
        def _(s):
            run("scalar", s)

        @block.sync
        def _(s):
            run("sync", s)

        @block.gpsimd
        def _(g):
            run("gpsimd", g)


class Arena:
    def __init__(self, nc, backing, base_off, limit):
        self.nc = nc
        self.backing = backing
        self.base = nc.lookup_mloc(backing).addr + base_off
        self.off = 0
        self.limit = limit
        self.uid = 0

    def alloc(self, name, shape, dtype):
        nbytes = int(np.prod(shape[1:])) * mybir.dt.size(dtype)
        nbytes = (nbytes + 63) // 64 * 64
        assert self.off + nbytes <= self.limit, \
            f"SBUF region overflow at {name}: {self.off + nbytes} > {self.limit}"
        h = self.nc.alloc_sbuf_tensor_at(
            f"{name}_u{self.uid}", shape, dtype, offset=self.base + self.off)
        self.uid += 1
        self.off += nbytes
        return h

    def reset(self):
        self.off = 0


def build_nc(dbg=False, stages='ABCD', dparts=None):
    nc = bass.Bass("TRN2", target_bir_lowering=False)

    def din(name, shape):
        return nc.dram_tensor(name, shape, F32, kind="ExternalInput").ap()

    x_b = din("x_b", [NB, D])
    xq = din("xq", [NQ, D])
    wqkv_h = din("wqkv_h", [128, 8, 3072])
    beta_c = din("beta_c", [128, 8])
    gamma_c = din("gamma_c", [128, 8])
    wproj_h = din("wproj_h", [128, 8, 1024])
    bproj_c = din("bproj_c", [128, 8])
    wiT_h = din("wiT_h", [32, 128, 8, 128])
    bi_c = din("bi_c", [128, 32])
    woT_h = din("woT_h", [8, 128, 32, 128])
    laT_h = din("laT_h", [128, 8, 8, 16])
    lbT_h = din("lbT_h", [8, 16, DFF])
    gwT_h = din("gwT_h", [128, 8, 8])
    gb_row = din("gb_row", [1, 8])
    bo_row = din("bo_row", [1, D])

    out = nc.dram_tensor("out", [NQ, D], F32, kind="ExternalOutput").ap()

    def scr(name, shape, dt):
        return nc.dram_tensor(name, shape, dt, kind="Internal").ap()

    sc_k = scr("sc_k", [D, NB], BF16)
    sc_v = scr("sc_v", [NB, 16 * 65], BF16)
    sc_q = scr("sc_q", [D, NQ], BF16)
    sc_at = scr("sc_at", [D, NQ], BF16)
    sc_x2 = scr("sc_x2", [NQ, D], F32)
    sc_c = scr("sc_c", [1, NQ], BF16)
    sc_wo = scr("sc_wo", [8, 128, DFF], BF16)

    dbg_outs = []
    if dbg:
        for nm, t in [("dk", sc_k), ("dv", sc_v), ("dq", sc_q), ("dat", sc_at),
                      ("dx2", sc_x2)]:
            o = nc.dram_tensor("dbg_" + nm, [int(s) for s in t.tensor.shape],
                               t.tensor.dtype, kind="ExternalOutput").ap()
            dbg_outs.append((o, t))

    ident_f32 = nc.inline_tensor(np.eye(128, dtype=np.float32), "c_idf")
    ident_bf = nc.inline_tensor(np.eye(128, dtype=BFNP), "c_idb")
    ones_c = nc.inline_tensor(np.ones((1, 128), dtype=np.float32), "c_ones")
    eps_c = nc.inline_tensor(np.full((128, 1), 1e-5, dtype=np.float32), "c_eps")
    selm = np.zeros((8, 8 * 128), dtype=np.float32)
    for e in range(8):
        selm[e, e * 128:(e + 1) * 128] = 1.0
    sel_c = nc.inline_tensor(selm, "c_sel")

    P = Prog()
    E = P.emit

    ctx = contextlib.ExitStack()
    with ctx:
        En = ctx.enter_context
        backing = En(nc.sbuf_tensor("arena", [128, PERS_BYTES + SBUF_REGION],
                                    mybir.dt.uint8))
        pers = Arena(nc, backing, 0, PERS_BYTES)
        reg = Arena(nc, backing, PERS_BYTES, SBUF_REGION)


        def full_barrier(waitlist):
            for eng in ("tensor", "vector", "scalar", "sync", "gpsimd"):
                E(eng, lambda e: e.nop(), waits=list(waitlist))
        _puid = [0]

        def psum_at(name, shape, dtype, bank):
            h = nc.virtual_psum_tensor(f"{name}_u{_puid[0]}", shape, dtype)
            _puid[0] += 1
            nc._tensor(h.name, h.shape, h.dtype, type="PSUM")
            mloc = nc.lookup_mloc(h)
            mloc.allocated = True
            mloc.addr = bank * 2048
            return h

        LD = Ctr(En(nc.semaphore("LD")))
        GD = Ctr(En(nc.semaphore("GD")))
        T = Ctr(En(nc.semaphore("T")))
        V = Ctr(En(nc.semaphore("V")))
        S = Ctr(En(nc.semaphore("S")))
        KV = [Ctr(En(nc.semaphore("KV0"))), Ctr(En(nc.semaphore("KV1")))]
        XD3 = [Ctr(En(nc.semaphore(f"XD{i}"))) for i in range(3)]
        XQ = Ctr(En(nc.semaphore("XQ")))
        OS = Ctr(En(nc.semaphore("OS")))
        WO2 = [Ctr(En(nc.semaphore(f"WO{i}"))) for i in range(2)]
        XT2 = [Ctr(En(nc.semaphore(f"XT{i}"))) for i in range(2)]
        WI2 = [Ctr(En(nc.semaphore(f"WI{i}"))) for i in range(2)]
        XE2 = [Ctr(En(nc.semaphore(f"XE{i}"))) for i in range(2)]
        LB2 = [Ctr(En(nc.semaphore(f"LB{i}"))) for i in range(2)]
        WS2 = [Ctr(En(nc.semaphore(f"WS{i}"))) for i in range(2)]
        VST = Ctr(En(nc.semaphore("VST")))
        SST = Ctr(En(nc.semaphore("SST")))
        XG3 = [Ctr(En(nc.semaphore(f"XG{i}"))) for i in range(3)]

        id32 = pers.alloc("id32", [128, 128], F32)
        idbf = pers.alloc("idbf", [128, 128], BF16)
        ones128 = pers.alloc("ones128", [2, 128], F32)
        ones65 = pers.alloc("ones65", [65, 64], F32)
        eps_sb = pers.alloc("eps_sb", [128, 1], F32)
        sel_sb = pers.alloc("sel_sb", [8, 8 * 128], F32)
        gam_sb = pers.alloc("gam_sb", [128, 8], F32)
        bet_sb = pers.alloc("bet_sb", [128, 64], F32)
        bpr_sb = pers.alloc("bpr_sb", [128, 8], F32)
        bi_sb = pers.alloc("bi_sb", [128, 32], F32)
        gw_sb = pers.alloc("gw_sb", [128, 64], F32)
        gbr_sb = pers.alloc("gbr_sb", [128, 8], F32)
        bo_rep = pers.alloc("bo_rep", [128, D], F32)
        qb_sb = pers.alloc("qb_sb", [128, 64], F32)
        qbq8 = pers.alloc("qbq8", [128, 8], F32)
        laT = pers.alloc("laT", [128, 1024], BF16)
        mu1q = pers.alloc("mu1q", [128, 16], F32)
        rs1q = pers.alloc("rs1q", [128, 16], F32)
        dmy = pers.alloc("dmy", [128, 1024], BF16)
        dmy2 = pers.alloc("dmy2", [128, 1024], BF16)
        st_sb = pers.alloc("st_sb", [128, 32], F32)
        gst = pers.alloc("gst", [128, 16], F32)
        gbrow = pers.alloc("gbrow", [2, 128], F32)
        borow = pers.alloc("borow", [2, D], F32)

        def ld(dst, src):
            E("sync", lambda s, d=dst, so=src: s.dma_start(out=d, in_=so),
              inc=LD, amt=16)

        ld(id32[:], ident_f32.ap())
        ld(idbf[:], ident_bf.ap())
        ld(ones128[0:1, :], ones_c.ap())
        ld(eps_sb[:], eps_c.ap())
        ld(sel_sb[:], sel_c.ap())
        ld(gam_sb[:], gamma_c)
        ld(bet_sb[:, 0:8], beta_c)
        ld(bpr_sb[:], bproj_c)
        ld(bi_sb[:], bi_c)
        ld(gw_sb[:], gwT_h.rearrange("p a b -> p (a b)"))
        ld(gbrow[0:1, 0:8], gb_row)
        ld(borow[0:1, :], bo_row)
        ld_pers = LD.n

        E("gpsimd", lambda g: g.dma_start(
            out=laT[:], in_=laT_h.rearrange("p a e r -> p (a e r)")),
            inc=GD, amt=16)
        laT_gd = GD.n

        E("vector", lambda v: v.memset(ones65[64:65, :], 1.0),
          waits=[(LD, ld_pers)], inc=V)
        ones65_v = V.n

        p_i0 = psum_at("p_i0", [128, 512], F32, 7)
        E("tensor", lambda t: t.matmul(p_i0[:, 0:8], ones128[0:1, :], gbrow[0:1, 0:8],
                                       start=True, stop=True),
          waits=[(LD, ld_pers)], inc=T)
        tv = T.n
        E("vector", lambda v: v.tensor_copy(gbr_sb[:], p_i0[:, 0:8]),
          waits=[(T, tv)], inc=V)
        vv = V.n
        for i in range(2):
            E("tensor", lambda t, i=i: t.matmul(
                p_i0[:], ones128[0:1, :], borow[0:1, i * 512:(i + 1) * 512],
                start=True, stop=True),
                waits=[(V, vv)], inc=T)
            tv = T.n
            E("vector", lambda v, i=i: v.tensor_copy(
                bo_rep[:, i * 512:(i + 1) * 512], p_i0[:]),
                waits=[(T, tv)], inc=V)
            vv = V.n

        reg.reset()
        wqkv_bf = reg.alloc("wqkv_bf", [128, 8 * 3072], BF16)
        stg_w = [reg.alloc(f"stg_w{i}", [128, 3072], F32) for i in range(2)]
        wostg = [reg.alloc(f"wostg{i}", [128, 4096], BF16) for i in range(2)]
        x_t = [reg.alloc(f"x_t{i}", [128, D], F32) for i in range(3)]
        zn_t = [reg.alloc(f"zn_t{i}", [128, D], BF16) for i in range(2)]
        znT = [reg.alloc(f"znT{i}", [128, 8 * 512], BF16) for i in range(2)]
        kstg = [reg.alloc(f"kstg{i}", [128, 512], BF16) for i in range(2)]
        vstg = [reg.alloc(f"vstg{i}", [128, 520], BF16) for i in range(2)]
        qstg = [reg.alloc(f"qstg{i}", [128, 512], BF16) for i in range(2)]
        vb8 = reg.alloc("vb8", [8, 128], F32)
        vbias_rep = reg.alloc("vbias_rep", [128, D], F32)

        p_qb = psum_at("p_qb", [128, 512], F32, 6)
        p_tr = [psum_at(f"p_tr{i}", [128, 128], F32, i) for i in range(2)]
        p_trb = [psum_at(f"p_trb{i}", [128, 128], BF16, i) for i in range(2)]
        p_k = [psum_at(f"p_k{i}", [128, 512], F32, 2 + i) for i in range(2)]
        p_v = [psum_at(f"p_v{i}", [128, 512], F32, 4 + i) for i in range(2)]

        for dc in range(8):
            wb = dc % 2
            W = WS2[wb]
            E("gpsimd", lambda g, dc=dc, wb=wb: g.dma_start(
                out=wostg[wb][:].rearrange("p (a c) -> p a c", a=32, c=128),
                in_=woT_h[dc]),
                waits=[(W, W.n)], inc=W, amt=16)
            E("gpsimd", lambda g, dc=dc, wb=wb: g.dma_start(
                out=sc_wo[dc],
                in_=wostg[wb][:].rearrange("p (a c) -> p a c", a=32, c=128)),
                waits=[(W, W.n)], inc=W, amt=16)

        stg_free_v = [0, 0]
        qb_t = 0
        qb_acc_v = 0
        for dc in range(8):
            b = dc % 2
            E("gpsimd", lambda g, b=b, dc=dc: g.dma_start(
                out=stg_w[b][:], in_=wqkv_h[:, dc, :]),
                waits=[(T, qb_t), (V, stg_free_v[b])],
                inc=GD, amt=16)
            wld = GD.n
            for cc in range(24):
                E("tensor", lambda t, b=b, cc=cc, dc=dc: t.matmul(
                    p_qb[:, cc:cc + 1], stg_w[b][:, cc * 128:(cc + 1) * 128],
                    bet_sb[:, dc:dc + 1], start=True, stop=True),
                    waits=([(GD, wld), (V, qb_acc_v)] if cc == 0 else []),
                    inc=T if cc == 23 else None)
            qb_t = T.n
            if dc == 0:
                E("vector", lambda v: v.tensor_copy(qb_sb[:, 0:24], p_qb[:, 0:24]),
                  waits=[(T, qb_t)], inc=V)
            else:
                E("vector", lambda v: v.tensor_tensor(
                    qb_sb[:, 0:24], qb_sb[:, 0:24], p_qb[:, 0:24], ALU.add),
                    waits=[(T, qb_t), (V, V.n)], inc=V)
            qb_acc_v = V.n
            E("vector", lambda v, b=b, dc=dc: v.tensor_scalar(
                wqkv_bf[:, dc * 3072:(dc + 1) * 3072], stg_w[b][:],
                gam_sb[:, dc:dc + 1], None, ALU.mult),
                waits=[(GD, wld), (T, qb_t)], inc=V)
            stg_free_v[b] = V.n

        qb_v = qb_acc_v
        E("vector", lambda v: v.tensor_scalar(
            qbq8[:], qb_sb[:, 0:8], 0.125, None, ALU.mult),
            waits=[(V, qb_v), (V, V.n)], inc=V)
        qbq8_v = V.n
        E("tensor", lambda t: t.transpose(p_tr[0][0:8, 0:128], qb_sb[:, 16:24], id32[:]),
          waits=[(V, qb_v)], inc=T)
        vb_t = T.n
        E("vector", lambda v: v.tensor_copy(vb8[:], p_tr[0][0:8, 0:128]),
          waits=[(T, vb_t)], inc=V)
        vb_v = V.n
        for i in range(8):
            E("tensor", lambda t, i=i: t.matmul(
                p_k[0][:, 0:128], sel_sb[:, i * 128:(i + 1) * 128], vb8[:],
                start=True, stop=True),
                waits=[(V, vb_v), (V, V.n)], inc=T)
            t_r = T.n
            E("vector", lambda v, i=i: v.tensor_copy(
                vbias_rep[:, i * 128:(i + 1) * 128], p_k[0][:, 0:128]),
                waits=[(T, t_r)], inc=V)
        vbias_v = V.n

        for i in range(2):
            E("vector", lambda v, i=i: v.memset(
                vstg[i][:].rearrange("p (c w) -> p c w", c=8, w=65)[:, :, 64:65],
                1.0), inc=V)
        vstg_ones_v = V.n

        def st(par, col):
            return st_sb[:, par * 16 + col: par * 16 + col + 1]

        x_free_S = [0, 0, 0]
        x_free_V = [0, 0, 0]
        zn_free_T = [0, 0]
        znT_mm_t = [0, 0]
        st_free_S = [0, 0]
        kst_ld = [0, 0]
        vst_ld = [0, 0]
        qst_ld = [0, 0]
        pk_free_S = [0, 0]
        pv_free_V = [vbias_v, 0]
        ptr_free_V = [0, 0]

        for tc in range(20):
            is_q = tc >= 16
            par_c = tc % 2
            znb = znT[par_c]
            for sub in range(4):
                g = tc * 4 + sub
                xb = x_t[g % 3]
                par = g % 2
                if is_q:
                    src = xq[(tc - 16) * 512 + sub * 128:
                             (tc - 16) * 512 + sub * 128 + 128, :]
                else:
                    src = x_b[tc * 512 + sub * 128: tc * 512 + sub * 128 + 128, :]
                xsem = XD3[g % 3] if tc < 10 else XG3[g % 3]
                E("sync" if tc < 10 else "gpsimd",
                  lambda s, xb=xb, src=src: s.dma_start(out=xb[:], in_=src),
                  waits=[(S, x_free_S[g % 3]), (V, x_free_V[g % 3])], inc=xsem, amt=16)
                x_ld = xsem.n
                E("scalar", lambda sc, xb=xb, par=par: sc.activation(
                    dmy[:], xb[:], AF.Copy, accum_out=st(par, 0)),
                    waits=[(xsem, x_ld), (V, st_free_S[par])], inc=S)
                s_sum = S.n
                E("scalar", lambda sc, xb=xb, par=par: sc.activation(
                    dmy[:], xb[:], AF.Square, accum_out=st(par, 4)),
                    waits=[(S, S.n)], inc=S)
                s_sq = S.n
                x_free_S[g % 3] = s_sq
                E("vector", lambda v, par=par: v.tensor_scalar(
                    st(par, 1), st(par, 0), 1.0 / 1024.0, None, ALU.mult),
                    waits=[(S, s_sum), (V, V.n)], inc=V)
                E("vector", lambda v, par=par: v.tensor_tensor(
                    st(par, 6), st(par, 1), st(par, 1), ALU.mult),
                    waits=[(V, V.n)], inc=V)
                E("vector", lambda v, par=par: v.tensor_scalar(
                    st(par, 5), st(par, 4), 1.0 / 1024.0, st(par, 6),
                    ALU.mult, ALU.subtract),
                    waits=[(V, V.n), (S, s_sq)], inc=V)
                v_v1 = V.n
                E("scalar", lambda sc, par=par: sc.activation(
                    st(par, 9), st(par, 5), AF.Ln, bias=eps_sb[:]),
                    waits=[(V, v_v1)], inc=S)
                E("scalar", lambda sc, par=par: sc.activation(
                    st(par, 9), st(par, 9), AF.Exp, scale=-0.5),
                    waits=[(S, S.n)], inc=S)
                s_r1 = S.n
                E("vector", lambda v, par=par: v.tensor_tensor(
                    st(par, 8), st(par, 9), st(par, 9), ALU.mult),
                    waits=[(S, s_r1), (V, V.n)], inc=V)
                E("vector", lambda v, par=par: v.tensor_tensor(
                    st(par, 8), st(par, 8), st(par, 5), ALU.mult),
                    waits=[(V, V.n)], inc=V)
                v_var2 = V.n
                E("scalar", lambda sc, par=par: sc.activation(
                    st(par, 10), st(par, 8), AF.Ln, bias=eps_sb[:]),
                    waits=[(V, v_var2)], inc=S)
                E("scalar", lambda sc, par=par: sc.activation(
                    st(par, 10), st(par, 10), AF.Exp, scale=-0.5),
                    waits=[(S, S.n)], inc=S)
                s_r2 = S.n
                E("vector", lambda v, par=par: v.tensor_tensor(
                    st(par, 11), st(par, 9), st(par, 10), ALU.mult),
                    waits=[(S, s_r2), (V, V.n)], inc=V)
                v_a = V.n
                if is_q:
                    qi = (tc - 16) * 4 + sub
                    E("vector", lambda v, par=par, qi=qi: v.tensor_copy(
                        mu1q[:, qi:qi + 1], st(par, 1)), waits=[(V, V.n)], inc=V)
                    E("vector", lambda v, par=par, qi=qi: v.tensor_copy(
                        rs1q[:, qi:qi + 1], st(par, 9)), inc=V)
                E("vector", lambda v, xb=xb, par=par: v.tensor_scalar(
                    zn_t[par][:], xb[:], st(par, 1), st(par, 11),
                    ALU.subtract, ALU.mult),
                    waits=[(V, v_a), (T, zn_free_T[par]), (V, V.n)], inc=V)
                v_zn = V.n
                x_free_V[g % 3] = v_zn
                st_free_S[par] = s_r2
                for dc in range(8):
                    pt = dc % 2
                    E("tensor", lambda t, par=par, dc=dc, pt=pt: t.transpose(
                        p_trb[pt][:], zn_t[par][:, dc * 128:(dc + 1) * 128], idbf[:]),
                        waits=[(V, v_zn), (T, znT_mm_t[par_c]), (V, ptr_free_V[pt])],
                        inc=T)
                    t_tr = T.n
                    E("vector", lambda v, par_c=par_c, dc=dc, sub=sub, pt=pt: v.tensor_copy(
                        znT[par_c][:, dc * 512 + sub * 128: dc * 512 + sub * 128 + 128],
                        p_trb[pt][:]),
                        waits=[(T, t_tr)], inc=V)
                    ptr_free_V[pt] = V.n
                zn_free_T[par] = T.n
            v_znT = V.n

            tok0 = (tc - 16) * 512 if is_q else tc * 512
            if not is_q:
                for cc in range(8):
                    pb = cc % 2
                    for dc in range(8):
                        E("tensor", lambda t, pb=pb, cc=cc, dc=dc, znb=znb: t.matmul(
                            p_k[pb][:],
                            wqkv_bf[:, dc * 3072 + 1024 + cc * 128:
                                    dc * 3072 + 1024 + cc * 128 + 128],
                            znb[:, dc * 512:(dc + 1) * 512],
                            start=(dc == 0), stop=(dc == 7)),
                            waits=([(V, v_znT), (S, pk_free_S[pb])] if dc == 0 else []),
                            inc=T if dc == 7 else None)
                    t_k = T.n
                    E("scalar", lambda sc, pb=pb, cc=cc: sc.activation(
                        kstg[pb][:], p_k[pb][:], AF.Identity,
                        bias=qb_sb[:, 8 + cc: 9 + cc]),
                        waits=[(T, t_k), (SST, SST.n), (V, qb_v)], inc=S)
                    pk_free_S[pb] = S.n
                    s_k = S.n
                    E("scalar", lambda s, pb=pb, cc=cc, tok0=tok0: s.dma_start(
                        out=sc_k[cc * 128:(cc + 1) * 128, tok0:tok0 + 512],
                        in_=kstg[pb][:]),
                        waits=[(S, s_k)], inc=SST, amt=16)
                    kst_ld[pb] = SST.n
                for sub in range(4):
                    for vcc in range(2):
                        pb = (sub * 2 + vcc) % 2
                        for dc in range(8):
                            E("tensor", lambda t, pb=pb, sub=sub, vcc=vcc, dc=dc, znb=znb: t.matmul(
                                p_v[pb][:],
                                znb[:, dc * 512 + sub * 128: dc * 512 + sub * 128 + 128],
                                wqkv_bf[:, dc * 3072 + 2048 + vcc * 512:
                                        dc * 3072 + 2048 + vcc * 512 + 512],
                                start=(dc == 0), stop=(dc == 7)),
                                waits=([(V, v_znT), (V, pv_free_V[pb])] if dc == 0 else []),
                                inc=T if dc == 7 else None)
                        t_v = T.n
                        E("vector", lambda v, pb=pb, vcc=vcc: v.tensor_tensor(
                            vstg[pb][:].rearrange("p (c w) -> p c w", c=8, w=65)[:, :, 0:64],
                            p_v[pb][:].rearrange("p (c w) -> p c w", c=8, w=64),
                            vbias_rep[:, vcc * 512:(vcc + 1) * 512].rearrange(
                                "p (c w) -> p c w", c=8, w=64),
                            ALU.add),
                            waits=[(T, t_v), (V, vstg_ones_v), (V, vbias_v),
                                   (VST, VST.n)], inc=V)
                        pv_free_V[pb] = V.n
                        v_v = V.n
                        E("scalar", lambda s, pb=pb, sub=sub, vcc=vcc, tok0=tok0: s.dma_start(
                            out=sc_v[tok0 + sub * 128: tok0 + sub * 128 + 128,
                                     vcc * 520:(vcc + 1) * 520],
                            in_=vstg[pb][:]),
                            waits=[(V, v_v)], inc=VST, amt=16)
                        vst_ld[pb] = VST.n
            else:
                for cc in range(8):
                    pb = cc % 2
                    for dc in range(8):
                        E("tensor", lambda t, pb=pb, cc=cc, dc=dc, znb=znb: t.matmul(
                            p_k[pb][:],
                            wqkv_bf[:, dc * 3072 + cc * 128: dc * 3072 + cc * 128 + 128],
                            znb[:, dc * 512:(dc + 1) * 512],
                            start=(dc == 0), stop=(dc == 7)),
                            waits=([(V, v_znT), (S, pk_free_S[pb])] if dc == 0 else []),
                            inc=T if dc == 7 else None)
                    t_q = T.n
                    E("scalar", lambda sc, pb=pb, cc=cc: sc.activation(
                        qstg[pb][:], p_k[pb][:], AF.Identity, scale=0.125,
                        bias=qbq8[:, cc:cc + 1]),
                        waits=[(T, t_q), (SST, SST.n), (V, qbq8_v)], inc=S)
                    pk_free_S[pb] = S.n
                    s_q = S.n
                    E("scalar", lambda s, pb=pb, cc=cc, tok0=tok0: s.dma_start(
                        out=sc_q[cc * 128:(cc + 1) * 128, tok0:tok0 + 512],
                        in_=qstg[pb][:]),
                        waits=[(S, s_q)], inc=SST, amt=16)
                    qst_ld[pb] = SST.n
            znT_mm_t[par_c] = T.n

        barrierA = [(T, T.n), (V, V.n), (S, S.n), (LD, LD.n), (GD, GD.n)] + \
            [(x, x.n) for x in XD3 + XG3] + [(w, w.n) for w in WS2] + \
            [(VST, VST.n), (SST, SST.n)]
        full_barrier(barrierA)

        if 'B' not in stages:
            for o, t in dbg_outs:
                E("sync", lambda s, o=o, t=t: s.dma_start(out=o, in_=t),
                  waits=[(LD, LD.n), (GD, GD.n)], inc=LD, amt=16)
            E("sync", lambda s: s.nop(),
              waits=[(LD, LD.n), (GD, GD.n)] + barrierA)
            block = En(nc.Block())
            P.realize(block)
            return nc
        reg.reset()
        wproj_bf = reg.alloc("wproj_bf", [128, 8 * 1024], BF16)
        kT2 = [reg.alloc(f"kT2_{i}", [128, 2 * NB], BF16) for i in range(2)]
        vA = [reg.alloc(f"vA_{i}", [128, 64 * 130 + 64], BF16) for i in range(2)]
        qA = [reg.alloc(f"qA_{i}", [128, 2 * NQ], BF16) for i in range(2)]
        PT = [reg.alloc(f"PT_{i}", [128, 1024], BF16) for i in range(3)]
        astg = [reg.alloc(f"astg{i}", [64, 512], BF16) for i in range(2)]
        cvec = reg.alloc("cvec", [128, 64], F32)
        crow = reg.alloc("crow", [16, 128], BF16)
        rd65 = [reg.alloc(f"rd65_{i}", [65, 512], F32) for i in range(2)]
        rep_sb = [reg.alloc(f"rep_sb{i}", [64, 512], F32) for i in range(2)]

        p_s = [psum_at(f"p_s{i}", [128, 1024], F32, 2 * i) for i in range(3)]
        p_av = [psum_at(f"p_av{i}", [128, 512], F32, 6 + i) for i in range(2)]
        p_x = psum_at("p_x", [128, 512], F32, 4)

        for i in range(2):
            E("vector", lambda v, i=i: v.memset(kT2[i][64:128, :], 0.0),
              waits=barrierA if i == 0 else [], inc=V)
            E("vector", lambda v, i=i: v.memset(kT2[i][64:65, :], 1.0),
              waits=[(V, V.n)], inc=V)
            E("vector", lambda v, i=i: v.memset(qA[i][64:128, :], 0.0), inc=V)
            E("vector", lambda v, i=i: v.memset(vA[i][:, 64 * 130:], 0.0), inc=V)
        kones_v = V.n

        E("gpsimd", lambda g: g.dma_start(
            out=wproj_bf[:].rearrange("p (a b) -> p a b", a=8, b=1024),
            in_=wproj_h),
            waits=barrierA, inc=GD, amt=16)
        wproj_gd = GD.n

        kv_ld = [0, 0]
        buf_free_t = [0, 0]
        pt_av_t = [0, 0, 0]
        s_exp = [0, 0, 0]
        av_free_v = [0, 0]
        astg_ld = [0, 0]
        px_free_V = 0
        px_free_S = 0
        ps_sub_V = 0

        def load_pair(p, buf):
            E("sync", lambda s, p=p, buf=buf: s.dma_start(
                out=kT2[buf][0:64, 0:NB], in_=sc_k[128 * p:128 * p + 64, :]),
                waits=barrierA + [(T, buf_free_t[buf])], inc=KV[buf], amt=16)
            E("sync", lambda s, p=p, buf=buf: s.dma_start(
                out=kT2[buf][0:64, NB:2 * NB],
                in_=sc_k[128 * p + 64:128 * p + 128, :]),
                inc=KV[buf], amt=16)
            E("sync", lambda s, p=p, buf=buf: s.dma_start(
                out=qA[buf][0:64, 0:NQ], in_=sc_q[128 * p:128 * p + 64, :]),
                inc=KV[buf], amt=16)
            E("sync", lambda s, p=p, buf=buf: s.dma_start(
                out=qA[buf][0:64, NQ:2 * NQ],
                in_=sc_q[128 * p + 64:128 * p + 128, :]),
                inc=KV[buf], amt=16)
            for c in range(64):
                E("sync", lambda s, p=p, buf=buf, c=c: s.dma_start(
                    out=vA[buf][:, c * 130:(c + 1) * 130],
                    in_=sc_v[c * 128:(c + 1) * 128, 130 * p:130 * p + 130]),
                    inc=KV[buf], amt=16)
            kv_ld[buf] = KV[buf].n

        load_pair(0, 0)
        for p in range(8):
            buf = p % 2
            if p < 7:
                load_pair(p + 1, (p + 1) % 2)
            for hh in range(2):
                h = 2 * p + hh
                koff = hh * NB
                qoff = hh * NQ
                sub_slot_V = [0, 0, 0, 0]
                for qc in range(16):
                    slot = qc % 4
                    w0 = ([(KV[buf], kv_ld[buf]), (V, kones_v),
                           (V, max(av_free_v)), (S, max(s_exp)),
                           (V, ps_sub_V)] if qc == 0 else []) +                          [(V, sub_slot_V[slot])]
                    E("tensor", lambda t, buf=buf, qc=qc, koff=koff, qoff=qoff, slot=slot: t.matmul(
                        p_s[slot // 2][:, (slot % 2) * 512:(slot % 2) * 512 + 512],
                        qA[buf][0:64, qoff + qc * 128: qoff + qc * 128 + 128],
                        kT2[buf][0:64, koff:koff + 512], start=True, stop=True),
                        waits=w0, inc=T)
                    t_sub = T.n
                    E("vector", lambda v, qc=qc, slot=slot: v.reduce_max(
                        cvec[:, qc:qc + 1],
                        p_s[slot // 2][:, (slot % 2) * 512:(slot % 2) * 512 + 512],
                        axis=AX.X, negate=True),
                        waits=[(T, t_sub)], inc=V)
                    sub_slot_V[slot] = V.n
                v_cv = V.n
                ps_sub_V = v_cv
                E("tensor", lambda t: t.transpose(p_x[0:16, 0:128], cvec[:, 0:16], id32[:]),
                  waits=[(V, v_cv), (V, px_free_V), (S, px_free_S),
                         (S, max(s_exp))], inc=T)
                t_cv = T.n
                E("scalar", lambda sc: sc.activation(
                    crow[:], p_x[0:16, 0:128], AF.Copy),
                    waits=[(T, t_cv)], inc=S)
                px_free_S = S.n
                s_cv = S.n
                E("sync", lambda s, buf=buf, qoff=qoff: s.dma_start(
                    out=qA[buf][64:65, qoff:qoff + NQ],
                    in_=crow[:]),
                    waits=[(S, s_cv)], inc=LD, amt=16)
                crow_ld = LD.n

                for sw in range(2):
                    so = qoff + sw * 1024

                    def emit_scores(kj):
                        st_ = kj % 3
                        for half in range(2):
                            w0 = []
                            if half == 0:
                                w0 = [(LD, crow_ld), (S, s_exp[st_]),
                                      (V, ps_sub_V)]
                                if kj == 2:
                                    w0 += [(V, px_free_V), (S, px_free_S)]
                            E("tensor", lambda t, buf=buf, kj=kj, half=half, st_=st_, so=so, koff=koff: t.matmul(
                                p_s[st_][:, half * 512:(half + 1) * 512],
                                kT2[buf][0:128, koff + kj * 128: koff + kj * 128 + 128],
                                qA[buf][0:128, so + half * 512: so + half * 512 + 512],
                                start=True, stop=True),
                                waits=w0,
                                inc=T if half == 1 else None)
                        t_s = T.n
                        E("scalar", lambda sc, st_=st_: sc.activation(
                            PT[st_][:], p_s[st_][:], AF.Exp),
                            waits=[(T, t_s), (T, pt_av_t[st_])], inc=S)
                        s_exp[st_] = S.n

                    def emit_av(kj):
                        st_ = kj % 3
                        voff = 65 * hh
                        for half in range(2):
                            E("tensor", lambda t, buf=buf, kj=kj, half=half, st_=st_, voff=voff: t.matmul(
                                p_av[half][:],
                                vA[buf][:, kj * 130 + voff: kj * 130 + voff + 128],
                                PT[st_][:, half * 512:(half + 1) * 512],
                                start=(kj == 0), stop=(kj == 63)),
                                waits=([(S, s_exp[st_])] +
                                       ([(V, av_free_v[half])] if kj == 0 else [])),
                                inc=T if half == 1 else None)
                        pt_av_t[st_] = T.n

                    emit_scores(0)
                    for kj in range(1, 64):
                        emit_scores(kj)
                        emit_av(kj - 1)
                    emit_av(63)
                    t_avdone = T.n
                    for half in range(2):
                        E("vector", lambda v, half=half: v.reciprocal(
                            rd65[half][64:65, :], p_av[half][64:65, :]),
                            waits=[(T, t_avdone)], inc=V)
                        v_rd = V.n
                        E("tensor", lambda t, half=half: t.matmul(
                            p_x[0:64, :], ones65[64:65, :], rd65[half][64:65, :],
                            start=True, stop=True),
                            waits=[(V, v_rd), (V, ones65_v), (V, px_free_V),
                                   (S, px_free_S), (S, max(s_exp))], inc=T)
                        t_rep = T.n
                        E("vector", lambda v, half=half: v.tensor_copy(
                            rep_sb[half][:], p_x[0:64, :]),
                            waits=[(T, t_rep)], inc=V)
                        v_rep = V.n
                        px_free_V = v_rep
                        E("vector", lambda v, half=half: v.tensor_tensor(
                            astg[half][:], p_av[half][0:64, :], rep_sb[half][:],
                            ALU.mult),
                            waits=[(V, v_rep), (LD, LD.n)], inc=V)
                        av_free_v[half] = V.n
                        E("sync", lambda s, h=h, sw=sw, half=half: s.dma_start(
                            out=sc_at[64 * h:64 * h + 64,
                                      sw * 1024 + half * 512:
                                      sw * 1024 + half * 512 + 512],
                            in_=astg[half][:]),
                            waits=[(V, av_free_v[half])], inc=LD, amt=16)
                        astg_ld[half] = LD.n
            buf_free_t[buf] = T.n

        barrierB = [(T, T.n), (V, V.n), (S, S.n), (LD, LD.n), (GD, GD.n)] + [(c, c.n) for c in KV]
        full_barrier(barrierB)

        if 'C' not in stages:
            for o, t in dbg_outs:
                E("sync", lambda s, o=o, t=t: s.dma_start(out=o, in_=t),
                  waits=[(LD, LD.n), (GD, GD.n)] + barrierB, inc=LD, amt=16)
            E("sync", lambda s: s.nop(),
              waits=[(LD, LD.n), (GD, GD.n)] + barrierB)
            block = En(nc.Block())
            P.realize(block)
            return nc
        reg.reset()
        reg.off += 8 * 1024 * 2
        wproj_bf2 = wproj_bf
        attnc = [reg.alloc(f"attnc{i}", [128, 8 * 512], BF16) for i in range(2)]
        projT = reg.alloc("projT", [128, 8 * 512], F32)
        xq_t = [reg.alloc(f"xq_t{i}", [128, D], F32) for i in range(2)]
        l1_t = reg.alloc("l1_t", [128, D], F32)
        x2stg = [reg.alloc(f"x2stg{i}", [128, D], F32) for i in range(2)]

        p_p = [psum_at(f"p_p{i}", [128, 512], F32, i) for i in range(2)]
        p_t2 = [psum_at(f"p_t2{i}", [128, 512], F32, 2 + i) for i in range(2)]

        at_ld = [0, 0]
        at_mm_t = [0, 0]
        projT_s = 0
        projT_tr_t = 0
        pp_free_S = [0, 0]
        pt2_free_V = [0, 0]
        xqt_free_V = [0, 0]
        x2stg_free_LD = [0, 0]
        for tci in range(4):
            cbuf = tci % 2
            for dc in range(8):
                E("sync", lambda s, cbuf=cbuf, tci=tci, dc=dc: s.dma_start(
                    out=attnc[cbuf][:, dc * 512:(dc + 1) * 512],
                    in_=sc_at[dc * 128:(dc + 1) * 128, tci * 512:(tci + 1) * 512]),
                    waits=(barrierB + [(T, at_mm_t[cbuf])]) if dc == 0 else [],
                    inc=LD, amt=16)
            at_ld[cbuf] = LD.n
            for oc in range(8):
                pb = oc % 2
                for dc in range(8):
                    E("tensor", lambda t, cbuf=cbuf, oc=oc, dc=dc, pb=pb: t.matmul(
                        p_p[pb][:],
                        wproj_bf2[:, dc * 1024 + oc * 128: dc * 1024 + oc * 128 + 128],
                        attnc[cbuf][:, dc * 512:(dc + 1) * 512],
                        start=(dc == 0), stop=(dc == 7)),
                        waits=([(LD, at_ld[cbuf]), (GD, wproj_gd),
                                (S, pp_free_S[pb]), (S, projT_s)] if dc == 0 else []),
                        inc=T if dc == 7 else None)
                t_p = T.n
                E("scalar", lambda sc, oc=oc, pb=pb: sc.activation(
                    projT[:, oc * 512:(oc + 1) * 512], p_p[pb][:], AF.Identity,
                    bias=bpr_sb[:, oc:oc + 1]),
                    waits=[(T, t_p), (T, projT_tr_t)], inc=S)
                pp_free_S[pb] = S.n
            at_mm_t[cbuf] = T.n
            projT_s = S.n
            for sub in range(4):
                g = tci * 4 + sub
                xb = xq_t[g % 2]
                E("sync", lambda s, g=g, xb=xb: s.dma_start(
                    out=xb[:], in_=xq[g * 128:(g + 1) * 128, :]),
                    waits=[(V, xqt_free_V[g % 2])], inc=XQ, amt=16)
                xq_ld = XQ.n
                E("vector", lambda v, g=g, xb=xb: v.tensor_scalar(
                    l1_t[:], xb[:], mu1q[:, g:g + 1], rs1q[:, g:g + 1],
                    ALU.subtract, ALU.mult),
                    waits=[(XQ, xq_ld)], inc=V)
                v_l1 = V.n
                E("vector", lambda v, g=g, xb=xb: v.tensor_tensor(
                    x2stg[g % 2][:], xb[:], l1_t[:], ALU.add),
                    waits=[(V, v_l1), (LD, LD.n)], inc=V)
                v_x2a = V.n
                for oc in range(8):
                    pb2 = oc % 2
                    E("tensor", lambda t, sub=sub, oc=oc, pb2=pb2: t.transpose(
                        p_t2[pb2][:, 0:128],
                        projT[:, oc * 512 + sub * 128: oc * 512 + sub * 128 + 128],
                        id32[:]),
                        waits=([(S, projT_s)] if (sub == 0 and oc == 0) else []) +
                              [(V, pt2_free_V[pb2])],
                        inc=T)
                    t_tr2 = T.n
                    E("vector", lambda v, g=g, oc=oc, pb2=pb2: v.tensor_tensor(
                        x2stg[g % 2][:, oc * 128:(oc + 1) * 128],
                        x2stg[g % 2][:, oc * 128:(oc + 1) * 128],
                        p_t2[pb2][:, 0:128], ALU.add),
                        waits=[(T, t_tr2), (V, v_x2a)], inc=V)
                    pt2_free_V[pb2] = V.n
                projT_tr_t = T.n
                v_x2 = V.n
                xqt_free_V[g % 2] = v_x2
                E("sync", lambda s, g=g: s.dma_start(
                    out=sc_x2[g * 128:(g + 1) * 128, :], in_=x2stg[g % 2][:]),
                    waits=[(V, v_x2)], inc=LD, amt=16)
                x2stg_free_LD[g % 2] = LD.n

        barrierC = [(T, T.n), (V, V.n), (S, S.n), (LD, LD.n), (GD, GD.n), (XQ, XQ.n)]
        full_barrier(barrierC)

        if 'D' not in stages:
            for o, t in dbg_outs:
                E("sync", lambda s, o=o, t=t: s.dma_start(out=o, in_=t),
                  waits=[(LD, LD.n), (GD, GD.n)] + barrierC, inc=LD, amt=16)
            E("sync", lambda s: s.nop(),
              waits=[(LD, LD.n), (GD, GD.n)] + barrierC)
            block = En(nc.Block())
            P.realize(block)
            return nc
        reg.reset()
        x2til = [reg.alloc(f"x2til{i}", [128, D], F32) for i in range(2)]
        x2ln = [reg.alloc("x2ln0", [128, D], F32)] * 2
        hs_t = reg.alloc("hs_t", [128, D], F32)
        hsTb = reg.alloc("hsTb", [128, 8 * 512], BF16)
        hsTf = [reg.alloc(f"hsTf{i}", [128, 128], F32) for i in range(2)]
        base_sb = reg.alloc("base_sb", [128, 32 * 512], BF16)
        hsum = reg.alloc("hsum", [128, 32 * 512], BF16)
        gt = reg.alloc("gt", [128, 512], BF16)
        wiT_c = [reg.alloc(f"wiT_c{i}", [128, 8 * 128], BF16) for i in range(2)]
        woT_c = [reg.alloc(f"woT_c{i}", [128, 32 * 128], BF16) for i in range(2)]
        lbT_c = [reg.alloc(f"lbT_c{i}", [16, DFF], BF16) for i in range(2)]
        a_sb = reg.alloc("a_sb", [16, 8 * 512], BF16)
        grep = [reg.alloc(f"grep{i}", [128, 512], F32) for i in range(2)]
        oacc = reg.alloc("oacc", [128, 8 * 512], F32)
        t3 = reg.alloc("t3", [128, 512], F32)
        lg_sb = reg.alloc("lg_sb", [128, 8], F32)
        pex = reg.alloc("pex", [128, 8], F32)
        pmask = reg.alloc("pmask", [128, 8], F32)
        gates_tok = reg.alloc("gates_tok", [128, 64], F32)
        gatesT = reg.alloc("gatesT", [8, 512], F32)
        vs2 = reg.alloc("vs2", [128, 4], F32)

        p_b = [psum_at(f"p_b{i}", [128, 512], F32, i) for i in range(2)]
        p_u = [psum_at(f"p_u{i}", [128, 512], F32, 2 + i) for i in range(2)]
        p_w = [psum_at(f"p_w{i}", [128, 512], F32, 4 + i) for i in range(2)]
        p_lg = psum_at("p_lg", [128, 512], F32, 6)
        p_d = psum_at("p_d", [128, 512], F32, 7)

        def gc(c):
            return gst[:, c:c + 1]

        pd_free = {"V": 0, "S": 0}

        x2t_free = [[], []]
        x2e_free = [[], []]
        pb_free_S = [0, 0]
        pu_free_V = [0, 0]
        pu_free_S = [0, 0]
        pw_free_V = [0, 0]
        wiT_free_T = [0, 0]
        woT_free_T = [0, 0]
        lbT_free_T = [0, 0]
        hsum_read_T = 0
        hsTf_read_T = [0, 0]
        plg_read_V = [0, 0, 0, 0]
        base_read_V = 0
        base_evac_S = [0] * 32
        hs_read_T = 0
        gatesT_read_T = 0
        gates_tok_read_T = 0
        prev_s_ex = 0
        prev_gate_V = 0
        x3_free_LD = 0

        for tci in range(4):
            s_gT_list = []
            for sub in range(4):
                g = tci * 4 + sub
                xb = x2ln[g % 2]
                E("sync", lambda s, g=g, xb=xb: s.dma_start(
                    out=xb[:], in_=sc_x2[g * 128:(g + 1) * 128, :]),
                    waits=(barrierC if g == 0 else []) + x2t_free[0] + x2t_free[1],
                    inc=XT2[g % 2], amt=16)
                x2_ld = XT2[g % 2].n
                E("scalar", lambda sc, xb=xb, g=g: sc.activation(
                    dmy[:], xb[:], AF.Copy, accum_out=gc(0)),
                    waits=[(XT2[g % 2], x2_ld), (V, prev_gate_V)], inc=S)
                s_s = S.n
                E("scalar", lambda sc, xb=xb, g=g: sc.activation(
                    dmy[:], xb[:], AF.Square, accum_out=gc(2)),
                    waits=[(S, S.n)], inc=S)
                s_sq2 = S.n
                E("vector", lambda v: v.tensor_scalar(
                    gc(1), gc(0), 1.0 / 1024.0, None, ALU.mult),
                    waits=[(S, s_s), (V, V.n)], inc=V)
                E("vector", lambda v: v.tensor_tensor(
                    gc(5), gc(1), gc(1), ALU.mult), waits=[(V, V.n)], inc=V)
                E("vector", lambda v: v.tensor_scalar(
                    gc(3), gc(2), 1.0 / 1024.0, gc(5), ALU.mult, ALU.subtract),
                    waits=[(V, V.n), (S, s_sq2)], inc=V)
                v_var = V.n
                E("scalar", lambda sc: sc.activation(
                    gc(4), gc(3), AF.Ln, bias=eps_sb[:]),
                    waits=[(V, v_var)], inc=S)
                E("scalar", lambda sc: sc.activation(
                    gc(4), gc(4), AF.Exp, scale=-0.5), waits=[(S, S.n)], inc=S)
                s_rs = S.n
                E("vector", lambda v, xb=xb: v.tensor_scalar(
                    hs_t[:], xb[:], gc(1), gc(4), ALU.subtract, ALU.mult),
                    waits=[(S, s_rs), (T, hs_read_T), (V, V.n)], inc=V)
                v_hs = V.n
                x2t_free[g % 2] = [(V, v_hs), (S, s_sq2)]
                for dc in range(8):
                    pf = dc % 2
                    E("tensor", lambda t, dc=dc, pf=pf: t.transpose(
                        p_u[pf][:, 0:128], hs_t[:, dc * 128:(dc + 1) * 128],
                        id32[:]),
                        waits=([(V, v_hs)] if dc == 0 else []) +
                              [(V, pu_free_V[pf]), (S, pu_free_S[pf])], inc=T)
                    t_t = T.n
                    E("vector", lambda v, dc=dc, sub=sub, pf=pf: v.tensor_copy(
                        hsTb[:, dc * 512 + sub * 128: dc * 512 + sub * 128 + 128],
                        p_u[pf][:, 0:128]),
                        waits=[(T, t_t)], inc=V)
                    pu_free_V[pf] = V.n
                    E("vector", lambda v, pf=pf: v.tensor_copy(
                        hsTf[pf][:], p_u[pf][:, 0:128]),
                        waits=[(T, t_t), (T, hsTf_read_T[pf])], inc=V)
                    pu_free_V[pf] = V.n
                    s_hf = V.n
                    E("tensor", lambda t, sub=sub, dc=dc, pf=pf: t.matmul(
                        p_lg[:, sub * 8:(sub + 1) * 8], hsTf[pf][:],
                        gw_sb[:, dc * 8:(dc + 1) * 8],
                        start=(dc == 0), stop=(dc == 7), skip_group_check=True),
                        waits=[(V, s_hf)] +
                              ([(V, plg_read_V[sub])] if dc == 0 else []),
                        inc=T if dc == 7 else None)
                    hsTf_read_T[pf] = T.n
                hs_read_T = T.n
                t_lg = T.n
                E("vector", lambda v, sub=sub: v.tensor_tensor(
                    lg_sb[:], p_lg[:, sub * 8:(sub + 1) * 8], gbr_sb[:], ALU.add),
                    waits=[(T, t_lg), (S, prev_s_ex)], inc=V)
                plg_read_V[sub] = V.n
                E("vector", lambda v: v.reduce_max(
                    gc(5), lg_sb[:], axis=AX.X, negate=True),
                    waits=[(V, V.n)], inc=V)
                v_m1 = V.n
                E("scalar", lambda sc: sc.activation(
                    pex[:], lg_sb[:], AF.Exp, bias=gc(5)),
                    waits=[(V, v_m1)], inc=S)
                s_ex = S.n
                prev_s_ex = s_ex
                E("vector", lambda v: v.reduce_sum(gc(7), pex[:], axis=AX.X),
                  waits=[(S, s_ex), (V, V.n)], inc=V)
                v_ss = V.n
                E("vector", lambda v: v.reciprocal(gc(8), gc(7)),
                  waits=[(V, v_ss), (V, V.n)], inc=V)
                v_rsum = V.n
                E("vector", lambda v: v.tensor_scalar(
                    pex[:], pex[:], gc(8), None, ALU.mult),
                    waits=[(V, v_rsum), (V, V.n)], inc=V)
                v_p = V.n
                E("vector", lambda v: v.reduce_max(gc(6), pex[:], axis=AX.X),
                  waits=[(V, v_p), (V, V.n)], inc=V)
                v_pm1 = V.n
                E("vector", lambda v: v.tensor_scalar(
                    pmask[:], pex[:], gc(6), None, ALU.is_lt),
                    waits=[(V, v_pm1), (V, V.n)], inc=V)
                E("vector", lambda v: v.tensor_tensor(
                    pmask[:], pmask[:], pex[:], ALU.mult),
                    waits=[(V, V.n)], inc=V)
                E("vector", lambda v: v.reduce_max(gc(9), pmask[:], axis=AX.X),
                  waits=[(V, V.n)], inc=V)
                v_m2 = V.n
                E("vector", lambda v: v.tensor_tensor(
                    gc(10), gc(6), gc(9), ALU.add),
                    waits=[(V, v_m2), (V, V.n)], inc=V)
                v_vs = V.n
                E("vector", lambda v, sub=sub: v.tensor_scalar(
                    vs2[:, sub:sub + 1], gc(10), 2.0, None, ALU.mult),
                    waits=[(V, v_vs), (V, V.n)], inc=V)
                E("vector", lambda v: v.tensor_scalar(
                    gates_tok[:, 0:8], pex[:], gc(9), None, ALU.is_ge),
                    waits=[(V, v_m2), (T, gates_tok_read_T), (V, V.n)], inc=V)
                E("vector", lambda v: v.tensor_scalar(
                    gates_tok[:, 0:8], gates_tok[:, 0:8], gc(10), None, ALU.mult),
                    waits=[(V, v_vs), (V, V.n)], inc=V)
                v_g = V.n
                prev_gate_V = v_g
                E("tensor", lambda t: t.transpose(
                    p_d[0:8, 0:128], gates_tok[:, 0:8], id32[:]),
                    waits=[(V, v_g), (V, pd_free["V"]), (S, pd_free["S"])], inc=T)
                t_gT = T.n
                gates_tok_read_T = t_gT
                E("vector", lambda v, sub=sub: v.tensor_copy(
                    gatesT[:, sub * 128:(sub + 1) * 128], p_d[0:8, 0:128]),
                    waits=[(T, t_gT), (T, gatesT_read_T)], inc=V)
                pd_free["V"] = V.n
                s_gT_list.append(V.n)
            s_gatesT = s_gT_list[-1] if s_gT_list else 0
            hsTb_done_V = V.n

            for fcc in range(32):
                wb = fcc % 2
                E("gpsimd", lambda g, fcc=fcc, wb=wb: g.dma_start(
                    out=wiT_c[wb][:].rearrange("p (a b) -> p a b", a=8, b=128),
                    in_=wiT_h[fcc]),
                    waits=[(T, wiT_free_T[wb])], inc=WI2[wb], amt=16)
                g_wi = WI2[wb].n
                for dc in range(8):
                    E("tensor", lambda t, fcc=fcc, dc=dc, wb=wb: t.matmul(
                        p_b[wb][:], wiT_c[wb][:, dc * 128:(dc + 1) * 128],
                        hsTb[:, dc * 512:(dc + 1) * 512],
                        start=(dc == 0), stop=(dc == 7)),
                        waits=([(WI2[wb], g_wi), (S, pb_free_S[wb]),
                                (V, base_read_V), (V, hsTb_done_V)] if dc == 0 else []),
                        inc=T if dc == 7 else None)
                wiT_free_T[wb] = T.n
                t_b = T.n
                E("scalar", lambda sc, fcc=fcc, wb=wb: sc.activation(
                    base_sb[:, fcc * 512:(fcc + 1) * 512], p_b[wb][:], AF.Identity,
                    bias=bi_sb[:, fcc:fcc + 1]),
                    waits=[(T, t_b)], inc=S)
                pb_free_S[wb] = S.n
                base_evac_S[fcc] = S.n
            s_base = S.n

            pd_aV = pd_free["V"]
            for e in range(8):
                for dc in range(8):
                    E("tensor", lambda t, e=e, dc=dc: t.matmul(
                        p_d[0:16, :],
                        laT[:, (dc * 8 + e) * 16:(dc * 8 + e) * 16 + 16],
                        hsTb[:, dc * 512:(dc + 1) * 512],
                        start=(dc == 0), stop=(dc == 7)),
                        waits=([(GD, GD.n), (V, s_gatesT), (V, hsTb_done_V),
                                (V, pd_aV)] if dc == 0 else []),
                        inc=T if dc == 7 else None)
                t_a = T.n
                E("vector", lambda v, e=e: v.tensor_copy(
                    a_sb[:, e * 512:(e + 1) * 512], p_d[0:16, :]),
                    waits=[(T, t_a)], inc=V)
                pd_aV = V.n
            v_a_all = V.n
            pd_free["V"] = V.n
            hsT_read = T.n

            def emit_grep(e):
                E("tensor", lambda t, e=e: t.matmul(
                    p_d[:, :], sel_sb[:, e * 128:(e + 1) * 128], gatesT[:],
                    start=True, stop=True),
                    waits=[(V, s_gatesT), (V, pd_free["V"]),
                           (S, pd_free["S"])], inc=T)
                t_g = T.n
                E("vector", lambda v, e=e: v.tensor_copy(
                    grep[e % 2][:], p_d[:, :]),
                    waits=[(T, t_g)], inc=V)
                pd_free["V"] = V.n
                return V.n, t_g

            grep_done_V = [0] * 8
            g_lb = [0, 0]

            def lbT_load(e):
                eb = e % 2
                E("gpsimd", lambda gg, e=e, eb=eb: gg.dma_start(
                    out=lbT_c[eb][:], in_=lbT_h[e]),
                    waits=[(T, lbT_free_T[eb])], inc=LB2[eb], amt=16)
                g_lb[eb] = LB2[eb].n

            lbT_load(0)
            for e in range(8):
                grep_done_V[e], gatesT_read_T = emit_grep(e)
                eb = e % 2
                if e < 7:
                    lbT_load(e + 1)
                for fcc in range(32):
                    ub = fcc % 2
                    w0 = [(V, pu_free_V[ub]), (S, pu_free_S[ub])]
                    if fcc < 2:
                        w0 += [(LB2[eb], g_lb[eb]), (V, v_a_all)]
                        if e == 0:
                            w0 += [(T, hsum_read_T)]
                    E("tensor", lambda t, e=e, fcc=fcc, ub=ub, eb=eb: t.matmul(
                        p_u[ub][:], lbT_c[eb][:, fcc * 128:(fcc + 1) * 128],
                        a_sb[:, e * 512:(e + 1) * 512],
                        start=True, stop=False),
                        waits=w0, inc=None)
                    E("tensor", lambda t, fcc=fcc, ub=ub: t.matmul(
                        p_u[ub][:], idbf[:],
                        base_sb[:, fcc * 512:(fcc + 1) * 512],
                        start=False, stop=True),
                        waits=[(S, base_evac_S[fcc])] if e == 0 else [],
                        inc=T)
                    t_u = T.n
                    lbT_free_T[eb] = t_u
                    if e == 0:
                        E("vector", lambda v, fcc=fcc, ub=ub, e=e: v.scalar_tensor_tensor(
                            hsum[:, fcc * 512:(fcc + 1) * 512], p_u[ub][:],
                            0.0, grep[e % 2][:], ALU.max, ALU.mult),
                            waits=[(T, t_u), (V, grep_done_V[e])], inc=V)
                    else:
                        E("vector", lambda v, fcc=fcc, ub=ub, e=e: v.scalar_tensor_tensor(
                            gt[:], p_u[ub][:],
                            0.0, grep[e % 2][:], ALU.max, ALU.mult),
                            waits=[(T, t_u), (V, V.n)] +
                                  ([(V, grep_done_V[e])] if fcc == 0 else []),
                            inc=V)
                        E("vector", lambda v, fcc=fcc: v.tensor_tensor(
                            hsum[:, fcc * 512:(fcc + 1) * 512],
                            hsum[:, fcc * 512:(fcc + 1) * 512], gt[:], ALU.add),
                            waits=[(V, V.n)], inc=V)
                    pu_free_V[ub] = V.n
            hsum_done_V = V.n

            for dc in range(8):
                wob = dc % 2
                E("sync", lambda s, dc=dc, wob=wob: s.dma_start(
                    out=woT_c[wob][:].rearrange("p (a b) -> p a b",
                                                a=32, b=128),
                    in_=sc_wo[dc]),
                    waits=[(T, woT_free_T[wob])], inc=WO2[wob], amt=16)
                wo_ld = WO2[wob].n
                for fcc in range(32):
                    E("tensor", lambda t, dc=dc, fcc=fcc, wob=wob: t.matmul(
                        p_w[wob][:],
                        woT_c[wob][:, fcc * 128:(fcc + 1) * 128],
                        hsum[:, fcc * 512:(fcc + 1) * 512],
                        start=(fcc == 0), stop=(fcc == 31)),
                        waits=([(V, hsum_done_V), (WO2[wob], wo_ld),
                                (V, pw_free_V[wob])] if fcc == 0 else []),
                        inc=T if fcc == 31 else None)
                woT_free_T[wob] = T.n
                t_w = T.n
                E("vector", lambda v, dc=dc, wob=wob: v.tensor_copy(
                    oacc[:, dc * 512:(dc + 1) * 512], p_w[wob][:]),
                    waits=[(T, t_w)], inc=V)
                pw_free_V[wob] = V.n
            hsum_read_T = T.n
            v_oacc = V.n
            base_read_V = V.n

            oacc_read_T = 0
            for sub in range(4):
                g = tci * 4 + sub
                xb = x2til[g % 2]
                E("sync", lambda s, g=g, xb=xb: s.dma_start(
                    out=xb[:], in_=sc_x2[g * 128:(g + 1) * 128, :]),
                    waits=x2e_free[g % 2] + [(OS, OS.n)], inc=XE2[g % 2], amt=16)
                x2_ld2 = XE2[g % 2].n
                for dc in range(8):
                    pe = dc % 2
                    E("tensor", lambda t, sub=sub, dc=dc, pe=pe: t.transpose(
                        p_u[pe][:, 0:128],
                        oacc[:, dc * 512 + sub * 128: dc * 512 + sub * 128 + 128],
                        id32[:]),
                        waits=([(V, v_oacc)] if (sub == 0 and dc == 0) else []) +
                              [(V, pu_free_V[pe]), (S, pu_free_S[pe])],
                        inc=T)
                    t_e = T.n
                    E("vector", lambda v, g=g, dc=dc, pe=pe, xb=xb: v.tensor_tensor(
                        xb[:, dc * 128:(dc + 1) * 128],
                        xb[:, dc * 128:(dc + 1) * 128],
                        p_u[pe][:, 0:128], ALU.add),
                        waits=[(T, t_e), (XE2[g % 2], x2_ld2)],
                        inc=V)
                    pu_free_V[pe] = V.n
                v_e1 = V.n
                E("vector", lambda v, sub=sub: v.tensor_scalar(
                    t3[:, 0:512], bo_rep[:, 0:512], vs2[:, sub:sub + 1], None,
                    ALU.mult),
                    waits=[(V, v_e1), (V, V.n)], inc=V)
                E("vector", lambda v, sub=sub, xb=xb: v.tensor_tensor(
                    xb[:, 0:512], xb[:, 0:512], t3[:, 0:512], ALU.add),
                    waits=[(V, V.n)], inc=V)
                E("vector", lambda v, sub=sub: v.tensor_scalar(
                    t3[:, 0:512], bo_rep[:, 512:1024], vs2[:, sub:sub + 1], None,
                    ALU.mult),
                    waits=[(V, V.n)], inc=V)
                E("vector", lambda v, sub=sub, xb=xb: v.tensor_tensor(
                    xb[:, 512:1024], xb[:, 512:1024], t3[:, 0:512], ALU.add),
                    waits=[(V, V.n)], inc=V)
                v_e2 = V.n
                E("sync", lambda s, tci=tci, sub=sub, xb=xb: s.dma_start(
                    out=out[tci * 512 + sub * 128: tci * 512 + sub * 128 + 128, :],
                    in_=xb[:]),
                    waits=[(V, v_e2)], inc=OS, amt=16)
                x2e_free[g % 2] = [(V, v_e2)]
                x3_free_LD = OS.n

        if dbg == 2:
            for nm, t, shp in [("gatesT", gatesT, [8, 512]), ("gst", gst, [128, 16]),
                               ("lg", lg_sb, [128, 8]), ("pex", pex, [128, 8]),
                               ("vs2", vs2, [128, 4]), ("pmask", pmask, [128, 8])]:
                o = nc.dram_tensor("dbg3_" + nm, shp, t.dtype, kind="ExternalOutput").ap()
                E("sync", lambda s, o=o, t=t: s.dma_start(out=o, in_=t[:]),
                  waits=[(c, c.n) for c in [LD, OS]] + [(T, T.n), (V, V.n), (S, S.n)],
                  inc=LD, amt=16)
        if dbg == 1:
            dmap = [("base", base_sb, [128, 32 * 512]), ("hsum", hsum, [128, 32 * 512]),
                    ("asb", a_sb, [16, 8 * 512]),
                    ("gatesT", gatesT, [8, 512]), ("grep0", grep[0], [128, 512]),
                    ("grep1", grep[1], [128, 512]), ("oacc", oacc, [128, 8 * 512]),
                    ("vs2", vs2, [128, 4]), ("hsTb", hsTb, [128, 8 * 512])]
            for nm, t, shp in dmap:
                o = nc.dram_tensor("dbg2_" + nm, shp, t.dtype, kind="ExternalOutput").ap()
                E("sync", lambda s, o=o, t=t: s.dma_start(out=o, in_=t[:]),
                  waits=[(c, c.n) for c in [LD, OS]] + [(T, T.n), (V, V.n), (S, S.n)],
                  inc=LD, amt=16)
        for o, t in dbg_outs:
            E("sync", lambda s, o=o, t=t: s.dma_start(out=o, in_=t),
              waits=[(LD, LD.n)], inc=LD, amt=16)

        allsems = [LD, GD, OS, XQ, VST, SST] + LB2 + WS2 + KV + XD3 + XG3 + WO2 + XT2 + WI2 + XE2
        E("sync", lambda s: s.nop(), waits=[(c, c.n) for c in allsems])

        block = En(nc.Block())
        P.realize(block)

    return nc


def _layout(inputs):
    w_qkv = np.asarray(inputs["L0_attn_w_qkv"], dtype=np.float32)
    perm = []
    for t in range(3):
        for h in range(16):
            for d in range(64):
                perm.append(192 * h + 3 * d + t)
    wqkv_perm = np.ascontiguousarray(w_qkv[:, perm])
    wqkv_h = np.ascontiguousarray(wqkv_perm.reshape(8, 128, 3072).transpose(1, 0, 2))

    gamma_c = np.ascontiguousarray(
        np.asarray(inputs["L0_attn_ln_scale"], np.float32).reshape(8, 128).T)
    beta_c = np.ascontiguousarray(
        np.asarray(inputs["L0_attn_ln_bias"], np.float32).reshape(8, 128).T)
    wproj_h = np.ascontiguousarray(
        np.asarray(inputs["L0_attn_w_proj"], np.float32).reshape(8, 128, 1024)
        .transpose(1, 0, 2))
    bproj_c = np.ascontiguousarray(
        np.asarray(inputs["L0_attn_b_proj"], np.float32).reshape(8, 128).T)

    wi = np.asarray(inputs["L0_ffn_wi"], np.float32)
    wiT_h = np.ascontiguousarray(wi.reshape(32, 128, 8, 128).transpose(0, 3, 2, 1))
    bi_c = np.ascontiguousarray(
        np.asarray(inputs["L0_ffn_bi"], np.float32).reshape(32, 128).T)
    wo = np.asarray(inputs["L0_ffn_wo"], np.float32)
    woT_h = np.ascontiguousarray(wo.reshape(8, 128, 32, 128).transpose(0, 3, 2, 1))

    laT_h = np.ascontiguousarray(
        np.asarray(inputs["L0_ffn_lora_A"], np.float32).transpose(2, 0, 1)
        .reshape(8, 128, 8, 16).transpose(1, 0, 2, 3))
    lbT_h = np.ascontiguousarray(
        np.asarray(inputs["L0_ffn_lora_B"], np.float32).transpose(0, 2, 1))
    gwT_h = np.ascontiguousarray(
        np.asarray(inputs["L0_ffn_gate_w"], np.float32).T.reshape(8, 128, 8)
        .transpose(1, 0, 2))
    gb_row = np.ascontiguousarray(
        np.asarray(inputs["L0_ffn_gate_b"], np.float32).reshape(1, 8))
    bo_row = np.ascontiguousarray(
        np.asarray(inputs["L0_ffn_bo"], np.float32).reshape(1, D))

    return dict(wqkv_h=wqkv_h, gamma_c=gamma_c, beta_c=beta_c, wproj_h=wproj_h,
                bproj_c=bproj_c, wiT_h=wiT_h, bi_c=bi_c, woT_h=woT_h, laT_h=laT_h,
                lbT_h=lbT_h, gwT_h=gwT_h, gb_row=gb_row, bo_row=bo_row)


def kernel(**inputs):
    x = np.asarray(inputs["L0_attn_z"], dtype=np.float32)
    shared = _layout(inputs)
    nc = build_nc()
    in_maps = []
    for c in range(8):
        b = c // 4
        qoff = (c % 4) * NQ
        m = dict(shared)
        m["x_b"] = np.ascontiguousarray(x[b])
        m["xq"] = np.ascontiguousarray(x[b, qoff:qoff + NQ])
        in_maps.append(m)
    res = run_bass_kernel_spmd(nc, in_maps, core_ids=list(range(8)))
    outp = np.empty((2, NB, D), dtype=np.float32)
    for c in range(8):
        b = c // 4
        qoff = (c % 4) * NQ
        outp[b, qoff:qoff + NQ] = res.results[c]["out"]
    return outp
